# version 05 (final)
# speedup vs baseline: 1.1133x; 1.0157x over previous
from contextlib import ExitStack

import numpy as np

import concourse.bass as bass
import concourse.mybir as mybir
from concourse.alu_op_type import AluOpType
from concourse.bass_utils import run_bass_kernel_spmd
from bass_rust import IndirectOffsetOnAxis

F32 = mybir.dt.float32
BF16 = mybir.dt.bfloat16
U32 = mybir.dt.uint32
AF = mybir.ActivationFunctionType
AX = mybir.AxisListType
OP = AluOpType

D = 1024
NT = 8
TS = 512
CHUNK = 16
NCH = TS // CHUNK
CAP = 640
NE1 = 32
SLOTS = NE1 * CAP
XS_ROWS = SLOTS + 8192

EPS_LN = 1e-5
EPS_RMS = 1e-6
EPS_GN = 1e-5


class Prog:
    CENG = ("T", "V", "S", "G")
    QENG = ("Q", "S", "G")

    def __init__(self, nc):
        self.nc = nc
        self.ops = []
        self.d_rr = {q: 0 for q in self.QENG}
        self.NPOOL = 8

    def op(self, eng, fn, reads=(), writes=(), inc=True):
        self.ops.append(dict(eng=eng, kind="c", fn=fn, reads=tuple(reads),
                             writes=tuple(writes), inc=inc))

    def dma(self, q, fn, reads=(), writes=()):
        self.ops.append(dict(eng=q, kind="d", fn=fn, reads=tuple(reads),
                             writes=tuple(writes)))

    def cc(self, fn, reads=(), writes=()):
        self.ops.append(dict(eng="G", kind="cc", fn=fn, reads=tuple(reads),
                             writes=tuple(writes)))

    def fence(self, reads, writes):
        self.op("V", lambda e: e.nop(), reads=tuple(reads), writes=tuple(writes))

    def finalize(self):
        c_cnt = {e: 0 for e in self.CENG}
        d_idx = {}
        cc_cnt = 0
        prev = None
        groups = []
        for i, o in enumerate(self.ops):
            if o["kind"] == "d":
                q = o["eng"]
                if prev is not None and prev[0] == q and prev[1] == i - 1:
                    groups[-1].append(i)
                    prev = (q, i)
                else:
                    groups.append([i])
                    prev = (q, i)
            else:
                prev = None
        gof = {}
        for gi, g in enumerate(groups):
            q = self.ops[g[0]]["eng"]
            slot = self.d_rr[q]
            self.d_rr[q] = (slot + 1) % self.NPOOL
            key = (q, slot)
            start = d_idx.get(key, 0)
            for i in g:
                d_idx[key] = d_idx.get(key, 0) + 16
                self.ops[i]["token"] = ("d", key, d_idx[key])
                gof[i] = (key, start, None)
            for i in g:
                gof[i] = (key, start, d_idx[key])
        for o in self.ops:
            if o["kind"] == "c":
                if o["inc"]:
                    c_cnt[o["eng"]] += 1
                    o["token"] = ("c", o["eng"], c_cnt[o["eng"]])
                else:
                    o["token"] = None
            elif o["kind"] == "cc":
                cc_cnt += 1
                o["token"] = ("cc", "CC", cc_cnt)
        pend = {e: None for e in self.CENG}
        for o in reversed(self.ops):
            if o["kind"] == "c":
                if o["inc"]:
                    pend[o["eng"]] = o["token"]
                elif o["token"] is None:
                    o["token"] = pend[o["eng"]]
        state = {}
        c_sofar = {e: 0 for e in self.CENG}
        for i, o in enumerate(self.ops):
            deps = []
            for b in o["reads"]:
                st = state.get(b)
                if st and st["w"]:
                    deps.append(st["w"])
            for b in o["writes"]:
                st = state.get(b)
                if st:
                    if st["w"]:
                        deps.append(st["w"])
                    deps.extend(st["r"])
            need = {}
            own_key, own_start = None, 0
            if o["kind"] == "d":
                own_key, own_start, _ = gof[i]
                if own_start > 0:
                    need[("d", own_key)] = own_start
            for dep in deps:
                if dep is None:
                    continue
                kind, key, cnt = dep
                if o["kind"] == "c" and kind == "c" and key == o["eng"] == "T":
                    continue
                if kind == "d" and key == own_key and cnt > own_start:
                    continue
                if kind == "c" and o["kind"] != "d" and key == o.get("eng"):
                    cnt = min(cnt, c_sofar[key])
                    if cnt == 0:
                        continue
                elif kind == "c" and o["kind"] == "d" and key == o.get("eng") and key in c_sofar:
                    cnt = min(cnt, c_sofar[key])
                    if cnt == 0:
                        continue
                k = (kind, key)
                if k not in need or need[k] < cnt:
                    need[k] = cnt
            o["waits"] = need
            if o["kind"] == "c" and o["inc"]:
                c_sofar[o["eng"]] += 1
            tok = o["token"]
            if o["kind"] == "d":
                key, start, end = gof[i]
                tok = ("d", key, end)
            for b in o["reads"]:
                st = state.setdefault(b, {"w": None, "r": []})
                if tok:
                    st["r"].append(tok)
                    if len(st["r"]) > 32:
                        st["r"] = st["r"][-32:]
            for b in o["writes"]:
                state[b] = {"w": tok, "r": []}

    def emit(self):
        nc = self.nc
        self.finalize()
        with ExitStack() as ctx:
            csem = {e: ctx.enter_context(nc.semaphore(f"c{e}")) for e in self.CENG}
            dsem = {(q, s): ctx.enter_context(nc.semaphore(f"d{q}{s}"))
                    for q in self.QENG for s in range(self.NPOOL)}
            ccsem = ctx.enter_context(nc.semaphore("ccs"))
            blk = ctx.enter_context(nc.Block())

            def semof(kind, key):
                return csem[key] if kind == "c" else (dsem[key] if kind == "d" else ccsem)

            eng_ops = {e: [o for o in self.ops if o["eng"] == e]
                       for e in ("T", "V", "S", "G", "Q")}
            final_d, final_c, n_cc = {}, {e: 0 for e in self.CENG}, 0
            for o in self.ops:
                if o["kind"] == "d":
                    final_d[o["token"][1]] = o["token"][2]
                elif o["kind"] == "c" and o["inc"]:
                    final_c[o["eng"]] = o["token"][2]
                elif o["kind"] == "cc":
                    n_cc = o["token"][2]

            def run_engine(e, builder):
                sat = {}
                for o in eng_ops[e]:
                    for (kind, key), cnt in sorted(o["waits"].items(), key=str):
                        if sat.get((kind, key), 0) >= cnt:
                            continue
                        builder.wait_ge(semof(kind, key), cnt)
                        sat[(kind, key)] = cnt
                    inst = o["fn"](builder)
                    if o["kind"] == "c":
                        if o["inc"]:
                            inst.then_inc(csem[e], 1)
                    elif o["kind"] == "d":
                        inst.then_inc(dsem[o["token"][1]], 16)
                    else:
                        inst.then_inc(ccsem)

            @blk.tensor
            def _(t):
                run_engine("T", t)

            @blk.vector
            def _(v):
                run_engine("V", v)

            @blk.scalar
            def _(s):
                run_engine("S", s)

            @blk.gpsimd
            def _(g):
                run_engine("G", g)

            @blk.sync
            def _(q):
                run_engine("Q", q)
                for key in sorted(final_d, key=str):
                    q.wait_ge(dsem[key], final_d[key])
                for e in self.CENG:
                    if final_c[e]:
                        q.wait_ge(csem[e], final_c[e])
                if n_cc:
                    q.wait_ge(ccsem, n_cc)


def cchunks(total, cap=512):
    off = 0
    while off < total:
        n = min(cap, total - off)
        yield off, n
        off += n


STAGES = ("l0a", "l0f", "l1a", "l1f")

FA_XSL = 0
FA_LB = 0
FA_ACC = 4224
FA_OB = 4224
FA_SCF = 8320
FA_YAC = 10368
FA_AB = 10880
FA_END = 14976



class K:
    def __init__(self, tl=4096, stages=STAGES, dbg=()):
        self.tl = tl
        self.ns = tl // TS
        self.stages = stages
        self.dbg = set(dbg)
        self.nc = bass.Bass("TRN2", target_bir_lowering=False)
        self.ap = {}
        self.sb = {}
        self.psum = []
        self._pb = 0
        self.p = None

    def din(self, name, shape, dt=F32):
        self.ap[name] = self.nc.dram_tensor(name, list(shape), dt, kind="ExternalInput").ap()

    def dout(self, name, shape, dt=F32):
        self.ap[name] = self.nc.dram_tensor(name, list(shape), dt, kind="ExternalOutput").ap()

    def dint(self, name, shape, dt=F32):
        self.ap[name] = self.nc.dram_tensor(name, list(shape), dt, kind="Internal").ap()

    def declare(self):
        tl = self.tl
        self.din("xdm", (D, tl + 1))
        for n in ("ln1w", "mu_r", "mu_k", "mu_v", "mu_w", "mu_g", "w0", "u",
                  "gnw", "gnb", "ln2w"):
            self.din("c_" + n, (128, 8))
        for n in ("Wr", "Wk", "Wv", "Wg", "Wo"):
            self.din("w_" + n, (128, 8 * D))
        self.din("w_wA", (128, 8 * 64))
        self.din("w_wB", (64, D))
        self.din("w_f1", (128, 8 * 4096))
        self.din("w_f2", (128, 32 * D))
        self.din("w_sup", (128, 2 * 8 * 512))
        self.din("w_sdn", (128, 2 * 4 * D))
        self.din("w_rup", (16, 128, 8 * 512))
        self.din("w_rdn", (16, 128, 4 * D))
        self.din("w_rtr", (128, 8 * 16))
        self.din("w_Win", (128, 8 * 4096))
        self.din("c_wconv", (128, 16 * 4))
        self.din("c_bconv", (128, 16))
        self.din("w_Wx", (128, 16 * 32))
        self.din("w_Wdt", (128, 16 * 2048))
        self.din("c_bdt", (128, 16))
        self.din("w_Wout", (128, 16 * D))
        self.din("c_A", (128, 16 * 16))
        self.din("c_D", (128, 16))
        self.din("w_rt0", (128, 8 * 32))
        self.din("w_rt1", (128, 8 * 32))
        for g in (0, 1):
            self.din(f"w_wg{g}", (NE1, 128, 8 * 512))
            self.din(f"w_wu{g}", (NE1, 128, 8 * 512))
            self.din(f"w_wd{g}", (NE1, 128, 4 * D))
        self.din("c_ident", (128, 128), BF16)
        self.din("c_identf", (128, 128))
        self.din("c_headrep", (128, 128), BF16)
        self.din("c_gnrep", (128, 128), BF16)
        self.din("c_onesrowf", (1, 128))
        self.din("c_onescolf", (128, 1))
        self.din("c_onescolb", (128, 1), BF16)
        self.din("c_tril", (128, 128), BF16)
        self.din("c_erow32", (128, 32))
        self.din("c_gate", (128, 1))

        self.dint("r1", (D, tl))
        self.dint("r2", (D, tl))
        self.dint("r3", (D, tl))
        self.dint("ln4d", (D, tl), BF16)
        self.dint("sp_o", (D, tl))
        self.dint("sp_rx", (D, tl), BF16)
        self.dint("sp_g", (D, tl), BF16)
        self.dint("h0d", (D, tl + 1), BF16)
        self.dint("rd", (D, tl), BF16)
        self.dint("kd", (D, tl), BF16)
        self.dint("vd", (D, tl), BF16)
        self.dint("ldd", (D, tl))
        self.dint("omd", (D, tl), BF16)
        self.dint("h2d", (D, tl), BF16)
        self.dint("hidd", (4096, tl), BF16)
        self.dint("xtd", (D, tl), BF16)
        self.dint("gted", (16, tl))
        self.dint("xsmd", (2048, tl), BF16)
        self.dint("resd", (2048, tl), BF16)
        self.dint("dtd", (2048, tl), BF16)
        self.dint("bcd", (32, tl))
        self.dint("yd", (2048, tl), BF16)
        self.dint("yfd", (2048, tl), BF16)
        self.dint("cc_s_in", (128, 1024))
        self.dint("cc_s_out", (256, 1024))
        self.dint("cc_h_in", (128, 256))
        self.dint("cc_h_out", (256, 256))
        self.dint("cc_t_in", (128, 48))
        self.dint("cc_t_out", (256, 48))
        for g in (0, 1):
            self.dint(f"xs{g}", (XS_ROWS, D), BF16)
            self.dint(f"ys{g}", (SLOTS, D), BF16)
        self.dout("out", (tl, D))
        for name in self.dbg:
            self.dout("dbg_" + name, (D, tl))

    def alloc(self, ctx):
        nc = self.nc

        def SB(name, shape, dt=F32):
            self.sb[name] = ctx.enter_context(nc.sbuf_tensor("sb_" + name, list(shape), dt))

        for i in range(8):
            self.psum.append(ctx.enter_context(
                nc.psum_tensor(f"pb{i}", [128, 512], F32)))
        SB("FAR", (128, FA_END))
        SB("BAR", (128, 24576), BF16)
        SB("WRES", (128, 24576), BF16)
        SB("Scab", (128, 1024), BF16)
        SB("wB", (64, D), BF16)
        for n in ("c_ident", "c_headrep", "c_gnrep", "c_tril"):
            SB(n, (128, 128), BF16)
        SB("c_identf", (128, 128))
        SB("c_erow32", (128, 32))
        SB("c_onesrowf", (1, 128))
        SB("c_onescolf", (128, 1))
        SB("c_onescolb", (128, 1), BF16)
        SB("c_gate", (128, 1))
        SB("onesF", (128, 512))
        for n in ("ln1w", "mu_r", "mu_k", "mu_v", "mu_w", "mu_g", "w0", "u",
                  "gnw", "gnb", "ln2w"):
            SB("v_" + n, (128, 8))
        SB("v_wconv", (128, 64))
        SB("v_bconv", (128, 16))
        SB("v_bdt", (128, 16))
        SB("v_D", (128, 16))
        SB("v_A", (128, 256))
        SB("dec", (128, 8 * 32))
        SB("kT", (16, 1024), BF16)
        SB("vT", (16, 1024), BF16)
        SB("Sa", (128, 1024))
        SB("Sb", (128, 1024))
        SB("Lcar", (128, 8))
        SB("Ltl", (128, 128))
        SB("row0", (1, 513))
        SB("row1", (1, 513))
        SB("hst", (128, 256))
        SB("xpt", (128, 16 * 3), BF16)
        SB("prb", (128, 32))
        SB("pr2", (128, 32))
        SB("oh", (128, 32))
        SB("rk", (128, 32))
        SB("m8", (128, 8))
        SB("ng", (128, 1))
        SB("crow", (1, 32))
        SB("slot", (128, 256), U32)
        SB("wgt", (128, 256))
        SB("tmaj", (128, 1024), BF16)
        SB("ytok", (128, 1024))
        SB("gath", (128, 1024), BF16)
        SB("brep", (128, 512), BF16)
        SB("lgt", (32, 512))
        SB("lgt2", (32, 512))
        SB("grow", (1, 512))
        SB("bcs", (32, 512))
        SB("xtl", (128, 48))
        SB("hcar", (128, 256))
        SB("crow2", (1, 32))
        SB("ohb", (128, 32), BF16)
        SB("idxc", (128, 4))
        SB("rkc", (128, 4))
        SB("bsc", (128, 4))
        SB("ovm", (128, 1))
        SB("ovmu", (128, 1), U32)
        SB("zz", (128, 1))
        SB("slotc", (128, 256), U32)
        SB("slotF", (128, 4))
        SB("fdum", (1, 4))
        SB("bt0", (128, 512))

    def fv(self, off, n):
        return self.sb["FAR"][:, off:off + n]

    def bv(self, off, n):
        return self.sb["BAR"][:, off:off + n]

    def xslv(self, j, a=0, b=513):
        return self.fv(FA_XSL + 513 * j + a, b - a)

    def obv(self, j, a=0, b=512):
        return self.fv(FA_OB + 512 * j + a, b - a)

    def accv(self, j, a=0, b=512):
        return self.fv(FA_ACC + 512 * j + a, b - a)

    def lbv(self, j, a=0, b=528):
        return self.fv(FA_LB + 528 * j + a, b - a)

    def scf(self, i, a=0, b=512):
        return self.fv(FA_SCF + 512 * i + a, b - a)

    def bank(self):
        b = self.psum[self._pb % 8]
        key = f"pb{self._pb % 8}"
        self._pb += 1
        return b, key

    def mm(self, p, out_view, out_key, lhs_view, lhs_key, rhs_view, rhs_key,
           start=True, stop=True, inc=True):
        p.op("T", lambda e, o=out_view, l=lhs_view, r=rhs_view, st=start, sp=stop:
             e.matmul(o, l, r, start=st, stop=sp),
             reads=(lhs_key, rhs_key), writes=(out_key,), inc=inc)

    def trans(self, p, out_view, out_key, in_view, in_key, f32=False):
        idn = self.sb["c_identf"] if f32 else self.sb["c_ident"]
        ik = "c_identf" if f32 else "c_ident"
        np_ = in_view.shape[0]
        p.op("T", lambda e, o=out_view, i=in_view, d=idn, np_=np_:
             e.transpose(o, i, d[0:np_, 0:np_]),
             reads=(in_key, ik), writes=(out_key,))

    def ldvec(self, p):
        for n in ("c_ident", "c_headrep", "c_gnrep", "c_tril", "c_identf",
                  "c_erow32", "c_onesrowf", "c_onescolf", "c_onescolb", "c_gate"):
            p.dma("Q", lambda e, d=self.sb[n][:], s=self.ap[n]: e.dma_start(out=d, in_=s),
                  writes=(n,))
        for n in ("ln1w", "mu_r", "mu_k", "mu_v", "mu_w", "mu_g", "w0", "u",
                  "gnw", "gnb", "ln2w", "wconv", "bconv", "bdt", "D", "A"):
            p.dma("Q", lambda e, d=self.sb["v_" + n][:], s=self.ap["c_" + n]: e.dma_start(out=d, in_=s),
                  writes=("v_" + n,))
        p.op("V", lambda e: e.memset(self.sb["onesF"][:], 1.0), writes=("onesF",))

    def wload(self, p, dst, dkey, src_ap):
        p.dma("G", lambda e, d=dst, s=src_ap: e.dma_start(out=d, in_=s),
              writes=(dkey,))

    def dbg1(self, p, name, view, key, j, t0, n=512, bf=False):
        if name in self.dbg:
            dst = self.ap["dbg_" + name]
            q = "G" if bf else "Q"
            p.dma(q, lambda e, v=view: e.dma_start(
                out=dst[128 * j:128 * (j + 1), t0:t0 + n], in_=v),
                reads=(key,))

    def colsum(self, p, views, keys, F, bank, bkey, square=None):
        ones = self.sb["c_onescolf"]
        for j in range(NT):
            src, skey = views[j], keys[j]
            if square is not None:
                sv, sk = square
                p.op("S", lambda e, d=sv[:, 0:F] if sv.shape[1] >= F else sv, s=src:
                     e.activation(d, s, AF.Square),
                     reads=(skey,), writes=(sk,))
                src, skey = sv[:, 0:F], sk
            for off, n in cchunks(F):
                self.mm(p, bank[0:1, off:off + n], bkey,
                        ones[:, 0:1], "c_onescolf",
                        src[:, off:off + n], skey,
                        start=(j == 0), stop=(j == NT - 1), inc=True)

    def rstd_from(self, p, bankA, bkA, bankB, bkB, F, eps, rstd, rk, mu, mk):
        inv = 1.0 / D
        if mu is not None:
            p.op("V", lambda e: e.tensor_scalar(
                mu[0:1, 0:F], bankA[0:1, 0:F], inv, None, OP.mult),
                reads=(bkA,), writes=(mk,))
            p.op("V", lambda e: e.tensor_tensor(
                rstd[0:1, 0:F], mu[0:1, 0:F], mu[0:1, 0:F], OP.mult),
                reads=(mk,), writes=(rk,))
            p.op("V", lambda e: e.scalar_tensor_tensor(
                rstd[0:1, 0:F], bankB[0:1, 0:F], inv, rstd[0:1, 0:F],
                OP.mult, OP.subtract),
                reads=(bkB, rk), writes=(rk,))
        else:
            p.op("V", lambda e: e.tensor_scalar(
                rstd[0:1, 0:F], bankB[0:1, 0:F], inv, None, OP.mult),
                reads=(bkB,), writes=(rk,))
        p.op("V", lambda e: e.tensor_scalar(
            rstd[0:1, 0:F], rstd[0:1, 0:F], float(eps), None, OP.add),
            reads=(rk,), writes=(rk,))
        p.op("V", lambda e: e.reciprocal(rstd[0:1, 0:F], rstd[0:1, 0:F]),
             reads=(rk,), writes=(rk,))
        p.op("S", lambda e: e.activation(rstd[0:1, 0:F], rstd[0:1, 0:F], AF.Sqrt),
             reads=(rk,), writes=(rk,))

    def repl(self, p, row, rk, F, bank, bkey):
        ones = self.sb["c_onesrowf"]
        for off, n in cchunks(F):
            self.mm(p, bank[:, off:off + n], bkey, ones[0:1, 0:128], "c_onesrowf",
                    row[0:1, off:off + n], rk, start=True, stop=True)

    ARENA_KEYS = tuple(
        ["xsl", "ob", "acc", "Lb", "scF0", "scF1", "scF2", "scF3", "yac", "xx",
         "h0", "xm", "rb", "kb", "vb", "ab", "khb", "sbf0", "sbf1", "sbf2",
         "sbf3", "hid", "xssm", "res", "beta", "hsc", "WR", "WR2"]
        + [f"xsl{j}" for j in range(8)] + [f"ob{j}" for j in range(8)]
        + [f"acc{j}" for j in range(8)] + [f"Lb{j}" for j in range(8)]
        + [f"h0{j}" for j in range(8)] + [f"rb{j}" for j in range(8)]
        + [f"kb{j}" for j in range(8)] + [f"vb{j}" for j in range(8)]
        + [f"ab{j}" for j in range(8)] + [f"khb{j}" for j in range(8)]
        + [f"xm{j}" for j in range(8)] + [f"hid{j}" for j in range(32)]
        + [f"xssm{j}" for j in range(16)] + [f"res{j}" for j in range(16)]
        + [f"dt{j}" for j in range(16)] + [f"ysl{j}" for j in range(16)]
        + ["pr8", "bt0", "bt1"])

    def fence(self):
        self.p.op("V", lambda e: e.memset(self.sb["fdum"][0:1, 0:1], 0.0),
                  writes=self.ARENA_KEYS)


    def l0a_A1(self, p):
        self._a1_block(p, 0, 1)
        for s in range(self.ns):
            self._a1_block(p, 1 + TS * s, 512)

    def _a1_block(self, p, c0, W):
        sb, ap_ = self.sb, self.ap
        xk = [f"xsl{j}" for j in range(NT)]

        def xv(j):
            return self.fv(FA_XSL + 513 * j, W)

        slow = (W == 1)
        for j in range(NT):
            p.dma("Q", lambda e, j=j: e.dma_start(
                out=xv(j), in_=ap_["xdm"][128 * j:128 * (j + 1), c0:c0 + W],
                allow_slow_non_contiguous=slow),
                reads=("xdm@d",), writes=(xk[j],))
        xvs = [xv(j) for j in range(NT)]
        sq = (self.fv(FA_ACC, W), "acc")
        bA, kA = self.bank()
        bB, kB = self.bank()
        self.colsum(p, xvs, xk, W, bA, kA)
        self.colsum(p, xvs, xk, W, bB, kB, square=sq)
        self.rstd_from(p, bA, kA, bB, kB, W, EPS_LN,
                       sb["row0"], "row0", sb["row1"], "row1")
        bM, kM = self.bank()
        bR, kR = self.bank()
        self.repl(p, sb["row1"], "row1", W, bM, kM)
        self.repl(p, sb["row0"], "row0", W, bR, kR)
        p.op("S", lambda e, bM=bM, W=W: e.copy(self.scf(0, 0, W), bM[:, 0:W]),
             reads=(kM,), writes=("scF0",))
        p.op("S", lambda e, bR=bR, W=W: e.copy(self.scf(1, 0, W), bR[:, 0:W]),
             reads=(kR,), writes=("scF1",))
        for j in range(NT):
            p.op("V", lambda e, j=j, W=W: e.tensor_tensor(
                xv(j), xv(j), self.scf(0, 0, W), OP.subtract),
                reads=(xk[j], "scF0"), writes=(xk[j],))
            p.op("V", lambda e, j=j, W=W: e.tensor_tensor(
                xv(j), xv(j), self.scf(1, 0, W), OP.mult),
                reads=(xk[j], "scF1"), writes=(xk[j],))
        bC, kC = self.bank()
        self.colsum(p, xvs, xk, W, bC, kC, square=sq)
        self.rstd_from(p, None, None, bC, kC, W, EPS_RMS,
                       sb["row0"], "row0", None, None)
        bR2, kR2 = self.bank()
        self.repl(p, sb["row0"], "row0", W, bR2, kR2)
        p.op("S", lambda e, bR2=bR2, W=W: e.copy(self.scf(0, 0, W), bR2[:, 0:W]),
             reads=(kR2,), writes=("scF0",))
        for j in range(NT):
            h0j = self.bv(513 * j, W)
            p.op("V", lambda e, j=j, h=h0j, W=W: e.scalar_tensor_tensor(
                h, xv(j), 1.0, self.scf(0, 0, W), OP.mult, OP.mult),
                reads=(xk[j], "scF0"), writes=(f"h0{j}",))
            p.op("V", lambda e, j=j, h=h0j: e.tensor_scalar(
                h, h, sb["v_ln1w"][:, j:j + 1], None, OP.mult),
                reads=(f"h0{j}", "v_ln1w"), writes=(f"h0{j}",))
            p.dma("G", lambda e, j=j, h=h0j: e.dma_start(
                out=ap_["h0d"][128 * j:128 * (j + 1), c0:c0 + W], in_=h,
                allow_slow_non_contiguous=slow),
                reads=(f"h0{j}",), writes=("h0d@d",))
            if W == 512:
                self.dbg1(p, "h0", h0j, f"h0{j}", j, c0 - 1, bf=True)

    def l0a_A2(self, p):
        sb, ap_ = self.sb, self.ap
        WR = self.bv(0, 8 * 1024)
        for pi, (mn, wname, dst, act) in enumerate([
                ("r", "w_Wr", "rd", None), ("k", "w_Wk", "kd", None),
                ("v", "w_Wv", "vd", None), ("g", "w_Wg", "sp_g", AF.Silu)]):
            self.wload(p, WR, "WR", ap_[wname])
            for s in range(self.ns):
                t0 = TS * s
                self._mix_load(p, s, mn)
                for m in range(NT):
                    bw, kw = self.bank()
                    for k in range(NT):
                        self.mm(p, bw[:, 0:512], kw,
                                WR[:, 1024 * k + 128 * m:1024 * k + 128 * (m + 1)],
                                "WR", self.bv(12808 + 512 * k, 512), f"xm{k}",
                                start=(k == 0), stop=(k == NT - 1), inc=(k == NT - 1))
                    sc = self.bv(16904, 512)
                    if act is None:
                        p.op("S", lambda e, d=sc, bw=bw: e.copy(d, bw[:, 0:512]),
                             reads=(kw,), writes=("sbf0",))
                    else:
                        p.op("S", lambda e, d=sc, bw=bw, act=act: e.activation(d, bw[:, 0:512], act),
                             reads=(kw,), writes=("sbf0",))
                    p.dma("G", lambda e, m=m, t0=t0, d=sc, dst=dst: e.dma_start(
                        out=ap_[dst][128 * m:128 * (m + 1), t0:t0 + 512], in_=d),
                        reads=("sbf0",), writes=(dst + "@d",))
        self.wload(p, WR[:, 0:512], "WR", ap_["w_wA"])
        self.wload(p, sb["wB"][:], "wB", ap_["w_wB"])
        for s in range(self.ns):
            t0 = TS * s
            self._mix_load(p, s, "w")
            bw, kw = self.bank()
            for k in range(NT):
                self.mm(p, bw[0:64, 0:512], kw, WR[:, 64 * k:64 * (k + 1)], "WR",
                        self.bv(12808 + 512 * k, 512), f"xm{k}",
                        start=(k == 0), stop=(k == NT - 1), inc=(k == NT - 1))
            th = self.sb["BAR"][0:64, 17416:17416 + 512]
            p.op("S", lambda e, d=th, bw=bw: e.activation(d, bw[0:64, 0:512], AF.Tanh),
                 reads=(kw,), writes=("sbf0",))
            for m in range(NT):
                bw2, kw2 = self.bank()
                self.mm(p, bw2[:, 0:512], kw2, sb["wB"][:, 128 * m:128 * (m + 1)],
                        "wB", self.sb["BAR"][0:64, 17416:17416 + 512], "sbf0")
                p.op("S", lambda e, m=m, bw2=bw2: e.activation(
                    self.scf(0), bw2[:, 0:512], AF.Exp,
                    bias=sb["v_w0"][:, m:m + 1], scale=1.0),
                    reads=(kw2, "v_w0"), writes=("scF0",))
                p.op("V", lambda e: e.tensor_scalar(
                    self.scf(0), self.scf(0), -1.0, None, OP.mult),
                    reads=("scF0",), writes=("scF0",))
                p.dma("Q", lambda e, m=m, t0=t0: e.dma_start(
                    out=ap_["ldd"][128 * m:128 * (m + 1), t0:t0 + 512],
                    in_=self.scf(0)), reads=("scF0",), writes=("ldd@d",))
                self.dbg1(p, "ld", self.scf(0), "scF0", m, t0)

    def _mix_load(self, p, s, mn):
        sb, ap_ = self.sb, self.ap
        t0 = TS * s
        mu = sb["v_mu_" + mn]
        for j in range(NT):
            h = self.bv(8192 + 513 * j, 513)
            p.dma("Q", lambda e, j=j, h=h, t0=t0: e.dma_start(
                out=h, in_=ap_["h0d"][128 * j:128 * (j + 1), t0:t0 + 513]),
                reads=("h0d@d",), writes=(f"h0{j}",))
            xx = self.bv(12296, 512)
            p.op("V", lambda e, j=j, h=h, xx=xx: e.tensor_tensor(
                xx, h[:, 0:512], h[:, 1:513], OP.subtract),
                reads=(f"h0{j}",), writes=("xx",))
            p.op("V", lambda e, j=j, h=h, xx=xx, mu=mu: e.scalar_tensor_tensor(
                self.bv(12808 + 512 * j, 512), xx, mu[:, j:j + 1], h[:, 1:513],
                OP.mult, OP.add),
                reads=("xx", "v_mu_" + mn, f"h0{j}"), writes=(f"xm{j}",))

    BSB = 20864

    def bsb(self, i, n=512):
        return self.bv(self.BSB + 512 * i, n)

    def l0a_A3(self, p):
        sb, ap_ = self.sb, self.ap
        p.op("V", lambda e: e.memset(sb["Lcar"][:], 0.0), writes=("Lcar",))
        p.op("V", lambda e: e.memset(sb["Sa"][:], 0.0), writes=("Sa",))
        p.op("V", lambda e: e.memset(sb["Sb"][:], 0.0), writes=("Sb",))
        for s in range(self.ns):
            t0 = TS * s
            for j in range(NT):
                p.dma("Q", lambda e, j=j, t0=t0: e.dma_start(
                    out=self.bv(528 * j, 512),
                    in_=ap_["rd"][128 * j:128 * (j + 1), t0:t0 + 512]),
                    reads=("rd@d",), writes=(f"rb{j}",))
                p.dma("Q", lambda e, j=j, t0=t0: e.dma_start(
                    out=self.bv(4224 + 528 * j + 16, 512),
                    in_=ap_["kd"][128 * j:128 * (j + 1), t0:t0 + 512]),
                    reads=("kd@d",), writes=(f"kb{j}",))
                p.dma("Q", lambda e, j=j, t0=t0: e.dma_start(
                    out=self.bv(8448 + 528 * j + 16, 512),
                    in_=ap_["vd"][128 * j:128 * (j + 1), t0:t0 + 512]),
                    reads=("vd@d",), writes=(f"vb{j}",))
                if s == 0:
                    p.op("V", lambda e, j=j: e.memset(self.bv(4224 + 528 * j, 16), 0.0),
                         writes=(f"kb{j}",))
                    p.op("V", lambda e, j=j: e.memset(self.bv(8448 + 528 * j, 16), 0.0),
                         writes=(f"vb{j}",))
                    p.op("V", lambda e, j=j: e.memset(self.lbv(j, 0, 16), 0.0),
                         writes=(f"Lb{j}",))
                else:
                    p.dma("G", lambda e, j=j, t0=t0: e.dma_start(
                        out=self.bv(4224 + 528 * j, 16),
                        in_=ap_["kd"][128 * j:128 * (j + 1), t0 - 16:t0]),
                        reads=("kd@d",), writes=(f"kb{j}",))
                    p.dma("G", lambda e, j=j, t0=t0: e.dma_start(
                        out=self.bv(8448 + 528 * j, 16),
                        in_=ap_["vd"][128 * j:128 * (j + 1), t0 - 16:t0]),
                        reads=("vd@d",), writes=(f"vb{j}",))
                p.dma("Q", lambda e, j=j, t0=t0: e.dma_start(
                    out=self.scf(0), in_=ap_["ldd"][128 * j:128 * (j + 1), t0:t0 + 512]),
                    reads=("ldd@d",), writes=("scF0",))
                Lsl = self.lbv(j, 16, 528)
                p.op("V", lambda e, j=j, L=Lsl: e.tensor_tensor_scan(
                    L, sb["onesF"][:], self.scf(0), sb["Lcar"][:, j:j + 1],
                    OP.mult, OP.add),
                    reads=("onesF", "scF0", "Lcar"), writes=(f"Lb{j}",))
                p.op("V", lambda e, j=j, L=Lsl: e.tensor_copy(
                    sb["Lcar"][:, j:j + 1], L[:, 511:512]),
                    reads=(f"Lb{j}",), writes=("Lcar",))
                if s > 0:
                    p.op("V", lambda e, j=j: e.tensor_copy(
                        self.lbv(j, 0, 16), sb["Ltl"][:, 16 * j:16 * (j + 1)]),
                        reads=("Ltl",), writes=(f"Lb{j}",))
            self.dbg_L(p, t0)
            for j in range(NT):
                p.op("V", lambda e, j=j: e.memset(self.obv(j), 0.0),
                     writes=(f"ob{j}",))
            KHK = tuple(f"khb{j}" for j in range(NT))
            r8 = lambda off, n=512: self.sb["BAR"][:, 0:4224].rearrange(
                "p (t c) -> p t c", c=528)[:, :, off:off + n]
            k8 = lambda off, n=512: self.sb["BAR"][:, 4224:8448].rearrange(
                "p (t c) -> p t c", c=528)[:, :, off:off + n]
            L8 = lambda half, off, n=512: self.sb["FAR"][:, FA_LB + 2112 * half:FA_LB + 2112 * (half + 1)].rearrange(
                "p (t c) -> p t c", c=528)[:, :, off:off + n]
            E8 = self.bv(16768, 4096)
            P8 = self.bv(12672, 4096)
            LBK = tuple(f"Lb{j}" for j in range(NT))
            RBK = tuple(f"rb{j}" for j in range(NT))
            KBK = tuple(f"kb{j}" for j in range(NT))
            for dlt in range(CHUNK):
                if dlt == 0:
                    p.op("V", lambda e: e.tensor_tensor(
                        P8.rearrange("p (t c) -> p t c", c=512), r8(0), k8(16),
                        OP.mult), reads=RBK + KBK, writes=("pr8",))
                    for j in range(NT):
                        p.op("V", lambda e, j=j: e.tensor_scalar(
                            self.bv(12672 + 512 * j, 512), self.bv(12672 + 512 * j, 512),
                            sb["v_u"][:, j:j + 1], None, OP.mult),
                            reads=("pr8", "v_u"), writes=("pr8",))
                else:
                    for half in range(2):
                        p.op("G", lambda e, half=half, dlt=dlt: e.tensor_tensor(
                            self.fv(FA_SCF, 2048).rearrange("p (t c) -> p t c", c=512),
                            L8(half, 15, 512), L8(half, 16 - dlt, 512), OP.subtract),
                            reads=LBK, writes=("scF0", "scF1", "scF2", "scF3"))
                        p.op("S", lambda e, half=half: e.activation(
                            E8[:, 2048 * half:2048 * (half + 1)],
                            self.fv(FA_SCF, 2048), AF.Exp),
                            reads=("scF0", "scF1", "scF2", "scF3"), writes=KHK)
                    p.op("V", lambda e, dlt=dlt: e.memset(
                        E8.rearrange("p (tc q) -> p tc q", q=16)[:, :, 0:dlt],
                        0.0), writes=KHK)
                    p.op("V", lambda e, dlt=dlt: e.tensor_tensor(
                        P8.rearrange("p (t c) -> p t c", c=512), r8(0), k8(16 - dlt),
                        OP.mult), reads=RBK + KBK, writes=("pr8",))
                    p.op("V", lambda e: e.tensor_tensor(P8, P8, E8, OP.mult),
                         reads=KHK + ("pr8",), writes=("pr8",))
                for j in range(NT):
                    vv = self.bv(8448 + 528 * j + 16 - dlt, 512)
                    bk, bkey = self.bank()
                    self.mm(p, bk[:, 0:512], bkey, sb["c_headrep"][:], "c_headrep",
                            self.bv(12672 + 512 * j, 512), "pr8")
                    tj = "bt0" if j % 2 else "yac"
                    tv = sb["bt0"][:, 0:512] if j % 2 else self.fv(FA_YAC, 512)
                    p.op("V", lambda e, vv=vv, bk=bk, tv=tv: e.tensor_tensor(
                        tv, bk[:, 0:512], vv, OP.mult),
                        reads=(bkey, f"vb{j}"), writes=(tj,))
                    p.op("V", lambda e, j=j, tv=tv: e.tensor_tensor(
                        self.obv(j), self.obv(j), tv, OP.add),
                        reads=(tj, f"ob{j}"), writes=(f"ob{j}",))
            for j in range(NT):
                Lsl = self.lbv(j, 16, 528)
                Lm1 = self.lbv(j, 15, 527)
                for q in range(CHUNK):
                    p.op("V", lambda e, j=j, q=q: e.tensor_copy(
                        self.scf(0)[:, q:512:16], self.lbv(j, 31, 528)[:, 0:497:16]),
                        reads=(f"Lb{j}",), writes=("scF0",), inc=(q == CHUNK - 1))
                    p.op("V", lambda e, j=j, q=q: e.tensor_copy(
                        self.scf(1)[:, q:512:16], self.lbv(j, 15, 512)[:, 0:497:16]),
                        reads=(f"Lb{j}",), writes=("scF1",), inc=(q == CHUNK - 1))
                p.op("G", lambda e, j=j, L=Lsl: e.tensor_tensor(
                    self.scf(2), self.scf(0), L, OP.subtract),
                    reads=("scF0", f"Lb{j}"), writes=("scF2",))
                p.op("S", lambda e: e.activation(self.bsb(0), self.scf(2), AF.Exp),
                     reads=("scF2",), writes=("sbf0",))
                p.op("V", lambda e, j=j: e.tensor_tensor(
                    self.bv(16768 + 512 * j, 512),
                    self.bv(4224 + 528 * j + 16, 512), self.bsb(0), OP.mult),
                    reads=("sbf0", f"kb{j}"), writes=(f"khb{j}",))
                p.op("G", lambda e, j=j, L=Lm1: e.tensor_tensor(
                    self.scf(2), L, self.scf(1), OP.subtract),
                    reads=("scF1", f"Lb{j}"), writes=("scF2",))
                p.op("S", lambda e: e.activation(self.bsb(1), self.scf(2), AF.Exp),
                     reads=("scF2",), writes=("sbf1",))
                p.op("V", lambda e, j=j: e.tensor_tensor(
                    self.fv(FA_AB + 512 * j, 512),
                    self.bv(528 * j, 512), self.bsb(1), OP.mult),
                    reads=("sbf1", f"rb{j}"), writes=(f"ab{j}",))
                p.op("S", lambda e, j=j, L=Lm1: e.activation(
                    self.bsb(2), L, AF.Exp), reads=(f"Lb{j}",), writes=("sbf2",))
                p.op("V", lambda e, j=j: e.tensor_tensor(
                    self.bsb(3), self.bv(528 * j, 512), self.bsb(2), OP.mult),
                    reads=("sbf2", f"rb{j}"), writes=("sbf3",))
                p.dma("G", lambda e, j=j, t0=t0: e.dma_start(
                    out=ap_["sp_rx"][128 * j:128 * (j + 1), t0:t0 + 512],
                    in_=self.bsb(3)), reads=("sbf3",), writes=("sp_rx@d",))
                p.op("V", lambda e, j=j: e.tensor_tensor(
                    sb["dec"][:, 32 * j:32 * (j + 1)],
                    self.lbv(j, 31, 528)[:, 0:497:16],
                    self.lbv(j, 15, 512)[:, 0:497:16], OP.subtract),
                    reads=(f"Lb{j}",), writes=("dec",))
            p.op("S", lambda e: e.activation(sb["dec"][:], sb["dec"][:], AF.Exp),
                 reads=("dec",), writes=("dec",))
            for j in range(NT):
                p.op("V", lambda e, j=j: e.tensor_copy(
                    sb["Ltl"][:, 16 * j:16 * (j + 1)], self.lbv(j, 512, 528)),
                    reads=(f"Lb{j}",), writes=("Ltl",))
            for c in range(NCH):
                gi = s * NCH + c
                cur, new = ("Sa", "Sb") if gi % 2 == 0 else ("Sb", "Sa")
                bT1, kT1b = self.bank()
                bT2, kT2b = self.bank()
                pt1, pt2 = bT1.bitcast(BF16), bT2.bitcast(BF16)
                for j in range(NT):
                    self.trans(p, pt1[0:16, 128 * j:128 * (j + 1)], kT1b,
                               self.bv(16768 + 512 * j + 16 * c, 16), f"khb{j}")
                    self.trans(p, pt2[0:16, 128 * j:128 * (j + 1)], kT2b,
                               self.bv(8448 + 528 * j + 16 + 16 * c, 16), f"vb{j}")
                p.op("S", lambda e, pt1=pt1: e.copy(sb["kT"][:], pt1[0:16, 0:1024]),
                     reads=(kT1b,), writes=("kT",))
                p.op("S", lambda e, pt2=pt2: e.copy(sb["vT"][:], pt2[0:16, 0:1024]),
                     reads=(kT2b,), writes=("vT",))
                bS1, kS1 = self.bank()
                bS2, kS2 = self.bank()
                bO, kO = self.bank()
                for pj in range(8):
                    bS = bS1 if pj < 4 else bS2
                    kS = kS1 if pj < 4 else kS2
                    self.mm(p, bS[:, 128 * (pj % 4):128 * (pj % 4) + 128], kS,
                            sb["kT"][0:16, 128 * pj:128 * (pj + 1)], "kT",
                            sb["vT"][0:16, 128 * pj:128 * (pj + 1)], "vT",
                            start=True, stop=True, inc=(pj % 4 == 3))
                for pj in range(8):
                    self.mm(p, bO[:, 16 * pj:16 * (pj + 1)], kO,
                            sb[cur][:, 128 * pj:128 * (pj + 1)], cur,
                            self.fv(FA_AB + 512 * pj + 16 * c, 16), f"ab{pj}",
                            start=True, stop=True, inc=(pj == 7))
                for j in range(NT):
                    p.op("V", lambda e, j=j, c=c, bO=bO: e.tensor_tensor(
                        self.obv(j, 16 * c, 16 * c + 16),
                        self.obv(j, 16 * c, 16 * c + 16),
                        bO[:, 16 * j:16 * (j + 1)], OP.add),
                        reads=(kO, f"ob{j}"), writes=(f"ob{j}",))
                for h in range(16):
                    pj, hp = h // 2, h % 2
                    bS = bS1 if pj < 4 else bS2
                    kS = kS1 if pj < 4 else kS2
                    p.op("V", lambda e, pj=pj, hp=hp, cur=cur, new=new, c=c, bS=bS:
                         e.scalar_tensor_tensor(
                             sb[new][64 * hp:64 * hp + 64,
                                     128 * pj + 64 * hp:128 * pj + 64 * hp + 64],
                             sb[cur][64 * hp:64 * hp + 64,
                                     128 * pj + 64 * hp:128 * pj + 64 * hp + 64],
                             sb["dec"][64 * hp:64 * hp + 64, 32 * pj + c:32 * pj + c + 1],
                             bS[64 * hp:64 * hp + 64,
                                128 * (pj % 4) + 64 * hp:128 * (pj % 4) + 64 * hp + 64],
                             OP.mult, OP.add),
                         reads=(cur, "dec", kS), writes=(new,))
            for j in range(NT):
                p.dma("Q", lambda e, j=j, t0=t0: e.dma_start(
                    out=ap_["sp_o"][128 * j:128 * (j + 1), t0:t0 + 512],
                    in_=self.obv(j)), reads=(f"ob{j}",), writes=("sp_o@d",))
                self.dbg1(p, "o1", self.obv(j), f"ob{j}", j, t0)
        self.S_fin = "Sa" if (self.ns * NCH) % 2 == 0 else "Sb"

    def dbg_L(self, p, t0):
        if "L" in self.dbg:
            for j in range(NT):
                p.dma("Q", lambda e, j=j: e.dma_start(
                    out=self.ap["dbg_L"][128 * j:128 * (j + 1), t0:t0 + 512],
                    in_=self.lbv(j, 16, 528)), reads=(f"Lb{j}",))

    def l0a_AGS(self, p):
        sb, ap_ = self.sb, self.ap
        if getattr(self, "nocc", False):
            p.op("V", lambda e: e.memset(sb["Scab"][:], 0.0), writes=("Scab",))
            return
        p.dma("Q", lambda e: e.dma_start(out=ap_["cc_s_in"], in_=sb[self.S_fin][:]),
              reads=(self.S_fin,), writes=("cc_s_in@d",))
        p.cc(lambda e: e.collective_compute(
            "AllGather", OP.bypass, replica_groups=[[0, 1], [2, 3], [4, 5], [6, 7]],
            ins=[ap_["cc_s_in"].opt()], outs=[ap_["cc_s_out"].opt()]),
            reads=("cc_s_in@d",), writes=("cc_s_out@d",))
        p.dma("Q", lambda e: e.dma_start(out=sb["Sb"][:], in_=ap_["cc_s_out"][0:128, :]),
              reads=("cc_s_out@d",), writes=("Sb",))
        p.op("V", lambda e: e.tensor_scalar(
            sb["Sb"][:], sb["Sb"][:], sb["c_gate"][:, 0:1], None, OP.mult),
            reads=("Sb", "c_gate"), writes=("Scar",))
        p.op("V", lambda e: e.tensor_copy(sb["Scab"][:], sb["Sb"][:]),
             reads=("Scar",), writes=("Scab",))

    def l0a_A4(self, p):
        sb, ap_ = self.sb, self.ap
        for s in range(self.ns):
            t0 = TS * s
            for j in range(NT):
                p.dma("Q", lambda e, j=j, t0=t0: e.dma_start(
                    out=self.obv(j), in_=ap_["sp_o"][128 * j:128 * (j + 1), t0:t0 + 512]),
                    reads=("sp_o@d",), writes=(f"ob{j}",))
                p.dma("Q", lambda e, j=j, t0=t0: e.dma_start(
                    out=self.bv(512 * j, 512),
                    in_=ap_["sp_rx"][128 * j:128 * (j + 1), t0:t0 + 512]),
                    reads=("sp_rx@d",), writes=(f"rb{j}",))
                p.dma("Q", lambda e, j=j, t0=t0: e.dma_start(
                    out=self.bv(4096 + 512 * j, 512),
                    in_=ap_["sp_g"][128 * j:128 * (j + 1), t0:t0 + 512]),
                    reads=("sp_g@d",), writes=(f"kb{j}",))
            for j in range(NT):
                bc, kc = self.bank()
                self.mm(p, bc[:, 0:512], kc, sb["Scab"][:, 128 * j:128 * (j + 1)],
                        "Scab", self.bv(512 * j, 512), f"rb{j}")
                p.op("V", lambda e, j=j, bc=bc: e.tensor_tensor(
                    self.obv(j), self.obv(j), bc[:, 0:512], OP.add),
                    reads=(kc, f"ob{j}"), writes=(f"ob{j}",))
                p.op("S", lambda e, j=j: e.copy(self.bsb(0), self.obv(j)),
                     reads=(f"ob{j}",), writes=("sbf0",))
                bE, kE = self.bank()
                self.mm(p, bE[:, 0:512], kE, sb["c_gnrep"][:], "c_gnrep",
                        self.bsb(0), "sbf0")
                p.op("S", lambda e: e.activation(self.bsb(1), self.bsb(0), AF.Square),
                     reads=("sbf0",), writes=("sbf1",))
                bE2, kE2 = self.bank()
                self.mm(p, bE2[:, 0:512], kE2, sb["c_gnrep"][:], "c_gnrep",
                        self.bsb(1), "sbf1")
                p.op("S", lambda e, bE=bE: e.copy(self.scf(0), bE[:, 0:512]),
                     reads=(kE,), writes=("scF0",))
                p.op("V", lambda e: e.tensor_tensor(
                    self.scf(1), self.scf(0), self.scf(0), OP.mult),
                    reads=("scF0",), writes=("scF1",))
                p.op("V", lambda e, bE2=bE2: e.scalar_tensor_tensor(
                    self.scf(1), bE2[:, 0:512], 1.0, self.scf(1), OP.mult, OP.subtract),
                    reads=(kE2, "scF1"), writes=("scF1",))
                p.op("V", lambda e: e.tensor_scalar(
                    self.scf(1), self.scf(1), EPS_GN, None, OP.add),
                    reads=("scF1",), writes=("scF1",))
                p.op("V", lambda e: e.reciprocal(self.scf(1), self.scf(1)),
                     reads=("scF1",), writes=("scF1",))
                p.op("S", lambda e: e.activation(self.scf(1), self.scf(1), AF.Sqrt),
                     reads=("scF1",), writes=("scF1",))
                p.op("V", lambda e, j=j: e.tensor_tensor(
                    self.scf(2), self.obv(j), self.scf(0), OP.subtract),
                    reads=(f"ob{j}", "scF0"), writes=("scF2",))
                p.op("V", lambda e: e.tensor_tensor(
                    self.scf(2), self.scf(2), self.scf(1), OP.mult),
                    reads=("scF1", "scF2"), writes=("scF2",))
                p.op("V", lambda e, j=j: e.tensor_scalar(
                    self.scf(2), self.scf(2), sb["v_gnw"][:, j:j + 1],
                    sb["v_gnb"][:, j:j + 1], OP.mult, OP.add),
                    reads=("scF2", "v_gnw", "v_gnb"), writes=("scF2",))
                p.op("V", lambda e, j=j: e.tensor_tensor(
                    self.bsb(2), self.scf(2), self.bv(4096 + 512 * j, 512), OP.mult),
                    reads=("scF2", f"kb{j}"), writes=("sbf2",))
                p.dma("G", lambda e, j=j, t0=t0: e.dma_start(
                    out=ap_["omd"][128 * j:128 * (j + 1), t0:t0 + 512],
                    in_=self.bsb(2)), reads=("sbf2",), writes=("omd@d",))

    def l0a_A5(self, p):
        sb, ap_ = self.sb, self.ap
        WR = sb["WRES"][:, 0:8192]
        self.wload(p, WR, "WR", ap_["w_Wo"])
        for s in range(self.ns):
            t0 = TS * s
            for j in range(NT):
                p.dma("Q", lambda e, j=j, t0=t0: e.dma_start(
                    out=self.bv(512 * j, 512),
                    in_=ap_["omd"][128 * j:128 * (j + 1), t0:t0 + 512]),
                    reads=("omd@d",), writes=(f"xm{j}",))
                p.dma("Q", lambda e, j=j, t0=t0: e.dma_start(
                    out=self.xslv(j, 0, 512),
                    in_=ap_["xdm"][128 * j:128 * (j + 1), t0 + 1:t0 + 513]),
                    reads=("xdm@d",), writes=(f"xsl{j}",))
            if getattr(self, "a5cut", 9) < 2:
                continue
            skipln = getattr(self, "a5skipln", 0)
            xk = [f"xsl{j}" for j in range(NT)]
            xvs = [self.xslv(j, 0, 512) for j in range(NT)]
            lnlvl = getattr(self, "a5ln", 9)
            if not skipln:
                bA, kA = self.bank()
                bB, kB = self.bank()
                self.colsum(p, xvs, xk, 512, bA, kA)
                if lnlvl >= 2:
                    self.colsum(p, xvs, xk, 512, bB, kB, square=(self.scf(0), "scF0"))
                if lnlvl >= 3:
                    self.rstd_from(p, bA, kA, bB, kB, 512, EPS_LN,
                                   sb["row0"], "row0", sb["row1"], "row1")
                bM, kM = self.bank()
                bR, kR = self.bank()
                if lnlvl >= 4:
                    self.repl(p, sb["row1"], "row1", 512, bM, kM)
                    self.repl(p, sb["row0"], "row0", 512, bR, kR)
                if lnlvl == 5:
                    pass
                elif lnlvl < 5:
                    continue
                else:
                    p.op("S", lambda e, bM=bM: e.copy(self.scf(0), bM[:, 0:512]),
                         reads=(kM,), writes=("scF0",))
                    p.op("S", lambda e, bR=bR: e.copy(self.scf(1), bR[:, 0:512]),
                         reads=(kR,), writes=("scF1",))
                    for j in range(NT):
                        p.op("V", lambda e, j=j: e.tensor_tensor(
                            self.obv(j), self.xslv(j, 0, 512), self.scf(0), OP.subtract),
                            reads=(xk[j], "scF0"), writes=(f"ob{j}",))
                        p.op("V", lambda e, j=j: e.tensor_tensor(
                            self.obv(j), self.obv(j), self.scf(1), OP.mult),
                            reads=(f"ob{j}", "scF1"), writes=(f"ob{j}",))
            if getattr(self, "a5cut", 9) < 3:
                continue
            for m in range(NT):
                bw, kw = self.bank()
                for k in range(NT):
                    self.mm(p, bw[:, 0:512], kw,
                            WR[:, 1024 * k + 128 * m:1024 * k + 128 * (m + 1)], "WR",
                            self.bv(512 * k, 512), f"xm{k}",
                            start=(k == 0), stop=(k == NT - 1), inc=(k == NT - 1))
                if getattr(self, "a5mm", 0) == 1:
                    continue
                p.op("V", lambda e, m=m, bw=bw: e.tensor_tensor(
                    self.obv(m), self.obv(m), bw[:, 0:512], OP.add),
                    reads=(kw, f"ob{m}"), writes=(f"ob{m}",))
                if getattr(self, "a5mm", 0) == 2:
                    continue
                p.op("V", lambda e, m=m: e.tensor_tensor(
                    self.scf(2), self.xslv(m, 0, 512), self.obv(m), OP.add),
                    reads=(f"ob{m}", f"xsl{m}"), writes=("scF2",))
                if getattr(self, "a5mm", 0) == 3:
                    continue
                p.dma("Q", lambda e, m=m, t0=t0: e.dma_start(
                    out=ap_["r1"][128 * m:128 * (m + 1), t0:t0 + 512],
                    in_=self.scf(2)), reads=("scF2",), writes=("r1@d",))
            if getattr(self, "a5cut", 9) < 4:
                continue
            obv_ = [self.obv(j) for j in range(NT)]
            obk = [f"ob{j}" for j in range(NT)]
            bC, kC = self.bank()
            self.colsum(p, obv_, obk, 512, bC, kC, square=(self.scf(0), "scF0"))
            self.rstd_from(p, None, None, bC, kC, 512, EPS_RMS,
                           sb["row0"], "row0", None, None)
            bR2, kR2 = self.bank()
            self.repl(p, sb["row0"], "row0", 512, bR2, kR2)
            p.op("S", lambda e, bR2=bR2: e.copy(self.scf(0), bR2[:, 0:512]),
                 reads=(kR2,), writes=("scF0",))
            for j in range(NT):
                p.op("V", lambda e, j=j: e.scalar_tensor_tensor(
                    self.bsb(0), self.obv(j), 1.0, self.scf(0), OP.mult, OP.mult),
                    reads=(f"ob{j}", "scF0"), writes=("sbf0",))
                p.op("V", lambda e, j=j: e.tensor_scalar(
                    self.bsb(0), self.bsb(0), sb["v_ln2w"][:, j:j + 1], None, OP.mult),
                    reads=("sbf0", "v_ln2w"), writes=("sbf0",))
                p.dma("G", lambda e, j=j, t0=t0: e.dma_start(
                    out=ap_["h2d"][128 * j:128 * (j + 1), t0:t0 + 512],
                    in_=self.bsb(0)), reads=("sbf0",), writes=("h2d@d",))

    def l0a_A6(self, p):
        sb, ap_ = self.sb, self.ap
        WR = sb["WRES"][:, 0:16384]
        f1r = ap_["w_f1"].rearrange("p (k m) -> p k m", m=4096)
        for half in range(2):
            m0 = 16 * half
            p.dma("G", lambda e, half=half: e.dma_start(
                out=WR.rearrange("p (k m) -> p k m", m=2048),
                in_=f1r[:, :, 2048 * half:2048 * (half + 1)]),
                writes=("WR",))
            for s in range(self.ns):
                t0 = TS * s
                for j in range(NT):
                    p.dma("Q", lambda e, j=j, t0=t0: e.dma_start(
                        out=self.bv(512 * j, 512),
                        in_=ap_["h2d"][128 * j:128 * (j + 1), t0:t0 + 512]),
                        reads=("h2d@d",), writes=(f"xm{j}",))
                for m in range(m0, m0 + 16):
                    bw, kw = self.bank()
                    for k in range(NT):
                        self.mm(p, bw[:, 0:512], kw,
                                WR[:, 2048 * k + 128 * (m - m0):2048 * k + 128 * (m - m0) + 128],
                                "WR", self.bv(512 * k, 512), f"xm{k}",
                                start=(k == 0), stop=(k == NT - 1), inc=(k == NT - 1))
                    p.op("S", lambda e, bw=bw: e.activation(self.bsb(0), bw[:, 0:512], AF.Gelu),
                         reads=(kw,), writes=("sbf0",))
                    p.dma("G", lambda e, m=m, t0=t0: e.dma_start(
                        out=ap_["hidd"][128 * m:128 * (m + 1), t0:t0 + 512],
                        in_=self.bsb(0)), reads=("sbf0",), writes=("hidd@d",))

    def l0a_A7(self, p):
        sb, ap_ = self.sb, self.ap
        WR = sb["WRES"][:, 0:16384]
        for half in range(2):
            k0 = 16 * half
            p.dma("G", lambda e, half=half: e.dma_start(
                out=WR, in_=ap_["w_f2"][:, 16384 * half:16384 * (half + 1)]),
                writes=("WR",))
            for s in range(self.ns):
                t0 = TS * s
                for k in range(16):
                    p.dma("Q", lambda e, k=k, k0=k0, t0=t0: e.dma_start(
                        out=self.bv(4096 + 512 * k, 512),
                        in_=ap_["hidd"][128 * (k0 + k):128 * (k0 + k + 1), t0:t0 + 512]),
                        reads=("hidd@d",), writes=(f"hid{k}",))
                for j in range(NT):
                    p.dma("Q", lambda e, j=j, t0=t0: e.dma_start(
                        out=self.accv(j), in_=ap_["r1"][128 * j:128 * (j + 1), t0:t0 + 512]),
                        reads=("r1@d",), writes=(f"acc{j}",))
                for m in range(NT):
                    bw, kw = self.bank()
                    for k in range(16):
                        self.mm(p, bw[:, 0:512], kw,
                                WR[:, 1024 * k + 128 * m:1024 * k + 128 * (m + 1)], "WR",
                                self.bv(4096 + 512 * k, 512), f"hid{k}",
                                start=(k == 0), stop=(k == 15), inc=(k == 15))
                    p.op("V", lambda e, m=m, bw=bw: e.tensor_tensor(
                        self.accv(m), self.accv(m), bw[:, 0:512], OP.add),
                        reads=(kw, f"acc{m}"), writes=(f"acc{m}",))
                    p.dma("Q", lambda e, m=m, t0=t0: e.dma_start(
                        out=ap_["r1"][128 * m:128 * (m + 1), t0:t0 + 512],
                        in_=self.accv(m)), reads=(f"acc{m}",), writes=("r1@d",))
                    if half == 1:
                        self.dbg1(p, "x1", self.accv(m), f"acc{m}", m, t0)

    def l0f(self, p):
        sb, ap_ = self.sb, self.ap
        WR = sb["WRES"]
        self.wload(p, WR[:, 0:8192], "WR", ap_["w_sup"])
        self.wload(p, WR[:, 8192:16384], "WR", ap_["w_sdn"])
        self.wload(p, WR[:, 16384:16512], "WR", ap_["w_rtr"])
        for s in range(self.ns):
            t0 = TS * s
            xk = [f"xsl{j}" for j in range(NT)]
            for j in range(NT):
                p.dma("Q", lambda e, j=j, t0=t0: e.dma_start(
                    out=self.xslv(j, 0, 512),
                    in_=ap_["r1"][128 * j:128 * (j + 1), t0:t0 + 512]),
                    reads=("r1@d",), writes=(xk[j],))
            self._ln_to(p, xk, [self.xslv(j, 0, 512) for j in range(NT)],
                        [self.bv(512 * j, 512) for j in range(NT)],
                        [f"xm{j}" for j in range(NT)])
            for j in range(NT):
                p.dma("G", lambda e, j=j, t0=t0: e.dma_start(
                    out=ap_["xtd"][128 * j:128 * (j + 1), t0:t0 + 512],
                    in_=self.bv(512 * j, 512)),
                    reads=(f"xm{j}",), writes=("xtd@d",))
            for s2 in range(2):
                for m in range(4):
                    bw, kw = self.bank()
                    for k in range(NT):
                        self.mm(p, bw[:, 0:512], kw,
                                WR[:, 4096 * s2 + 512 * k + 128 * m:4096 * s2 + 512 * k + 128 * m + 128],
                                "WR", self.bv(512 * k, 512), f"xm{k}",
                                start=(k == 0), stop=(k == NT - 1), inc=(k == NT - 1))
                    p.op("S", lambda e, bw=bw, s2=s2, m=m: e.activation(
                        self.bv(8192 + 2048 * s2 + 512 * m, 512), bw[:, 0:512], AF.Gelu),
                        reads=(kw,), writes=("hid",))
            for mo in range(NT):
                bw, kw = self.bank()
                first = True
                for s2 in range(2):
                    for ke in range(4):
                        self.mm(p, bw[:, 0:512], kw,
                                WR[:, 8192 + 4096 * s2 + 1024 * ke + 128 * mo:8192 + 4096 * s2 + 1024 * ke + 128 * mo + 128],
                                "WR", self.bv(8192 + 2048 * s2 + 512 * ke, 512), "hid",
                                start=first, stop=(s2 == 1 and ke == 3),
                                inc=(s2 == 1 and ke == 3))
                        first = False
                p.op("V", lambda e, mo=mo, bw=bw: e.tensor_tensor(
                    self.accv(mo), self.xslv(mo, 0, 512), bw[:, 0:512], OP.add),
                    reads=(kw, xk[mo]), writes=(f"acc{mo}",))
                p.dma("Q", lambda e, mo=mo, t0=t0: e.dma_start(
                    out=ap_["r2"][128 * mo:128 * (mo + 1), t0:t0 + 512],
                    in_=self.accv(mo)), reads=(f"acc{mo}",), writes=("r2@d",))
            bw, kw = self.bank()
            for k in range(NT):
                self.mm(p, bw[0:16, 0:512], kw, WR[:, 16384 + 16 * k:16384 + 16 * (k + 1)],
                        "WR", self.bv(512 * k, 512), f"xm{k}",
                        start=(k == 0), stop=(k == NT - 1), inc=(k == NT - 1))
            p.op("S", lambda e, bw=bw: e.copy(sb["lgt"][0:16, 0:512], bw[0:16, 0:512]),
                 reads=(kw,), writes=("lgt",))
            for tc in range(4):
                bt, kt = self.bank()
                self.trans(p, bt[:, 0:16], kt, sb["lgt"][0:16, 128 * tc:128 * (tc + 1)],
                           "lgt", f32=True)
                p.op("S", lambda e, bt=bt: e.copy(sb["prb"][:, 0:16], bt[:, 0:16]),
                     reads=(kt,), writes=("prb",))
                self._softmax_topk(p, 16, 4)
                p.op("V", lambda e: e.tensor_scalar(
                    sb["oh"][:, 0:16], sb["prb"][:, 0:16], sb["m8"][:, 3:4], None,
                    OP.is_ge), reads=("prb", "m8"), writes=("oh",))
                p.op("V", lambda e: e.tensor_tensor(
                    sb["pr2"][:, 0:16], sb["prb"][:, 0:16], sb["oh"][:, 0:16], OP.mult),
                    reads=("prb", "oh"), writes=("pr2",))
                bt2, kt2 = self.bank()
                self.trans(p, bt2[0:16, 0:128], kt2, sb["pr2"][:, 0:16], "pr2", f32=True)
                p.op("S", lambda e, bt2=bt2, tc=tc: e.copy(
                    sb["lgt2"][0:16, 128 * tc:128 * (tc + 1)], bt2[0:16, 0:128]),
                    reads=(kt2,), writes=("lgt2",))
            p.dma("Q", lambda e, t0=t0: e.dma_start(
                out=ap_["gted"][:, t0:t0 + 512], in_=sb["lgt2"][0:16, 0:512]),
                reads=("lgt2",), writes=("gted@d",))
        self.fence()
        for blk0 in range(0, 16, 2):
            for e2 in range(2):
                e = blk0 + e2
                self.wload(p, WR[:, 8192 * e2:8192 * e2 + 4096], "WR", ap_["w_rup"][e])
                self.wload(p, WR[:, 8192 * e2 + 4096:8192 * e2 + 8192], "WR", ap_["w_rdn"][e])
            for s in range(self.ns):
                t0 = TS * s
                for j in range(NT):
                    p.dma("Q", lambda e, j=j, t0=t0: e.dma_start(
                        out=self.bv(512 * j, 512),
                        in_=ap_["xtd"][128 * j:128 * (j + 1), t0:t0 + 512]),
                        reads=("xtd@d",), writes=(f"xm{j}",))
                    p.dma("Q", lambda e, j=j, t0=t0: e.dma_start(
                        out=self.accv(j), in_=ap_["r2"][128 * j:128 * (j + 1), t0:t0 + 512]),
                        reads=("r2@d",), writes=(f"acc{j}",))
                p.dma("Q", lambda e, t0=t0: e.dma_start(
                    out=sb["lgt2"][0:16, 0:512], in_=ap_["gted"][:, t0:t0 + 512]),
                    reads=("gted@d",), writes=("lgt2",))
                for e2 in range(2):
                    e = blk0 + e2
                    p.dma("Q", lambda e_, e=e, t0=t0: e_.dma_start(
                        out=sb["grow"][0:1, 0:512], in_=ap_["gted"][e:e + 1, t0:t0 + 512]),
                        reads=("gted@d",), writes=("grow",))
                    bg, kg = self.bank()
                    self.mm(p, bg[:, 0:512], kg, sb["c_onesrowf"][0:1, 0:128],
                            "c_onesrowf", sb["grow"][0:1, 0:512], "grow",
                            start=True, stop=True)
                    p.op("S", lambda e_, bg=bg: e_.copy(self.scf(3), bg[:, 0:512]),
                         reads=(kg,), writes=("scF3",))
                    for m in range(4):
                        bw, kw = self.bank()
                        for k in range(NT):
                            self.mm(p, bw[:, 0:512], kw,
                                    WR[:, 8192 * e2 + 512 * k + 128 * m:8192 * e2 + 512 * k + 128 * m + 128],
                                    "WR", self.bv(512 * k, 512), f"xm{k}",
                                    start=(k == 0), stop=(k == NT - 1), inc=(k == NT - 1))
                        p.op("S", lambda e_, bw=bw, m=m: e_.activation(
                            self.bsb(0), bw[:, 0:512], AF.Gelu),
                            reads=(kw,), writes=("sbf0",))
                        p.op("V", lambda e_, m=m: e_.tensor_tensor(
                            self.bv(8192 + 512 * m, 512), self.bsb(0), self.scf(3), OP.mult),
                            reads=("sbf0", "scF3"), writes=("hid",))
                    for mo in range(NT):
                        bw, kw = self.bank()
                        for ke in range(4):
                            self.mm(p, bw[:, 0:512], kw,
                                    WR[:, 8192 * e2 + 4096 + 1024 * ke + 128 * mo:8192 * e2 + 4096 + 1024 * ke + 128 * mo + 128],
                                    "WR", self.bv(8192 + 512 * ke, 512), "hid",
                                    start=(ke == 0), stop=(ke == 3), inc=(ke == 3))
                        p.op("V", lambda e_, mo=mo, bw=bw: e_.tensor_tensor(
                            self.accv(mo), self.accv(mo), bw[:, 0:512], OP.add),
                            reads=(kw, f"acc{mo}"), writes=(f"acc{mo}",))
                for mo in range(NT):
                    p.dma("Q", lambda e_, mo=mo, t0=t0: e_.dma_start(
                        out=ap_["r2"][128 * mo:128 * (mo + 1), t0:t0 + 512],
                        in_=self.accv(mo)), reads=(f"acc{mo}",), writes=("r2@d",))
                    if blk0 == 14:
                        self.dbg1(p, "x2", self.accv(mo), f"acc{mo}", mo, t0)
            self.fence()

    def _ln_to(self, p, xk, xvs, outs, outks, W=512):
        sb = self.sb
        bA, kA = self.bank()
        bB, kB = self.bank()
        self.colsum(p, xvs, xk, W, bA, kA)
        self.colsum(p, xvs, xk, W, bB, kB, square=(self.fv(FA_SCF, W), "scF0"))
        self.rstd_from(p, bA, kA, bB, kB, W, EPS_LN,
                       sb["row0"], "row0", sb["row1"], "row1")
        bM, kM = self.bank()
        bR, kR = self.bank()
        self.repl(p, sb["row1"], "row1", W, bM, kM)
        self.repl(p, sb["row0"], "row0", W, bR, kR)
        p.op("S", lambda e, bM=bM: e.copy(self.scf(0, 0, W), bM[:, 0:W]),
             reads=(kM,), writes=("scF0",))
        p.op("S", lambda e, bR=bR: e.copy(self.scf(1, 0, W), bR[:, 0:W]),
             reads=(kR,), writes=("scF1",))
        for j in range(NT):
            p.op("V", lambda e, j=j: e.tensor_tensor(
                self.scf(2, 0, W), xvs[j], self.scf(0, 0, W), OP.subtract),
                reads=(xk[j], "scF0"), writes=("scF2",))
            p.op("V", lambda e, j=j: e.tensor_tensor(
                outs[j], self.scf(2, 0, W), self.scf(1, 0, W), OP.mult),
                reads=("scF2", "scF1"), writes=(outks[j],))

    def _softmax_topk(self, p, ne, kk):
        sb = self.sb
        p.op("V", lambda e: e.reduce_max(sb["ng"][:], sb["prb"][:, 0:ne], axis=AX.X),
             reads=("prb",), writes=("ng",))
        p.op("V", lambda e: e.tensor_scalar(sb["ng"][:], sb["ng"][:], -1.0, None, OP.mult),
             reads=("ng",), writes=("ng",))
        p.op("S", lambda e: e.activation(sb["pr2"][:, 0:ne], sb["prb"][:, 0:ne],
                                         AF.Exp, bias=sb["ng"][:], scale=1.0),
             reads=("prb", "ng"), writes=("pr2",))
        p.op("V", lambda e: e.tensor_reduce(sb["ng"][:], sb["pr2"][:, 0:ne], AX.X, OP.add),
             reads=("pr2",), writes=("ng",))
        p.op("V", lambda e: e.reciprocal(sb["ng"][:], sb["ng"][:]),
             reads=("ng",), writes=("ng",))
        p.op("V", lambda e: e.tensor_scalar(sb["prb"][:, 0:ne], sb["pr2"][:, 0:ne],
                                            sb["ng"][:], None, OP.mult),
             reads=("pr2", "ng"), writes=("prb",))
        p.op("V", lambda e: e.max(sb["m8"][:], sb["prb"][:, 0:ne]),
             reads=("prb",), writes=("m8",))

    def l1a(self, p):
        sb, ap_ = self.sb, self.ap
        WR = sb["WRES"]
        tl = self.tl
        p.op("S", lambda e: e.activation(sb["v_A"][:], sb["v_A"][:], AF.Exp),
             reads=("v_A",), writes=("v_A",))
        p.op("V", lambda e: e.tensor_scalar(sb["v_A"][:], sb["v_A"][:], -1.0, None, OP.mult),
             reads=("v_A",), writes=("v_A",))
        xk = [f"xsl{j}" for j in range(NT)]
        for j in range(NT):
            p.dma("Q", lambda e, j=j: e.dma_start(
                out=self.xslv(j, 0, 3), in_=ap_["r2"][128 * j:128 * (j + 1), tl - 3:tl],
                allow_slow_non_contiguous=True),
                reads=("r2@d",), writes=(xk[j],))
        self._ln_to(p, xk, [self.xslv(j, 0, 3) for j in range(NT)],
                    [self.bv(512 * j, 3) for j in range(NT)],
                    [f"xm{j}" for j in range(NT)], W=3)
        self.wload(p, WR[:, 0:16384], "WR",
                   ap_["w_Win"].rearrange("p (k m) -> p k m", m=4096)[:, :, 0:2048])
        for m in range(16):
            bw, kw = self.bank()
            for k in range(NT):
                self.mm(p, bw[:, 0:3], kw,
                        WR[:, 2048 * k + 128 * m:2048 * k + 128 * m + 128], "WR",
                        self.bv(512 * k, 3), f"xm{k}",
                        start=(k == 0), stop=(k == NT - 1), inc=(k == NT - 1))
            p.op("S", lambda e, bw=bw, m=m: e.copy(sb["xtl"][:, 3 * m:3 * m + 3],
                                                   bw[:, 0:3]),
                 reads=(kw,), writes=("xtl",))
        p.dma("Q", lambda e: e.dma_start(out=ap_["cc_t_in"], in_=sb["xtl"][:]),
              reads=("xtl",), writes=("cc_t_in@d",))
        p.cc(lambda e: e.collective_compute(
            "AllGather", OP.bypass, replica_groups=[[0, 1], [2, 3], [4, 5], [6, 7]],
            ins=[ap_["cc_t_in"].opt()], outs=[ap_["cc_t_out"].opt()]),
            reads=("cc_t_in@d",), writes=("cc_t_out@d",))
        p.dma("Q", lambda e: e.dma_start(out=sb["xtl"][:], in_=ap_["cc_t_out"][0:128, :]),
              reads=("cc_t_out@d",), writes=("xtl",))
        p.op("V", lambda e: e.tensor_scalar(
            sb["xtl"][:], sb["xtl"][:], sb["c_gate"][:, 0:1], None, OP.mult),
            reads=("xtl", "c_gate"), writes=("xtl",))
        p.op("G", lambda e: e.tensor_copy(sb["xpt"][:], sb["xtl"][:]),
             reads=("xtl",), writes=("xpt",))
        self.fence()
        for s in range(self.ns):
            t0 = TS * s
            for j in range(NT):
                p.dma("Q", lambda e, j=j, t0=t0: e.dma_start(
                    out=self.xslv(j, 0, 512),
                    in_=ap_["r2"][128 * j:128 * (j + 1), t0:t0 + 512]),
                    reads=("r2@d",), writes=(xk[j],))
            self._ln_to(p, xk, [self.xslv(j, 0, 512) for j in range(NT)],
                        [self.bv(512 * j, 512) for j in range(NT)],
                        [f"xm{j}" for j in range(NT)])
            for j in range(NT):
                p.dma("G", lambda e, j=j, t0=t0: e.dma_start(
                    out=ap_["xtd"][128 * j:128 * (j + 1), t0:t0 + 512],
                    in_=self.bv(512 * j, 512)),
                    reads=(f"xm{j}",), writes=("xtd@d",))
            for m in range(16):
                bw, kw = self.bank()
                for k in range(NT):
                    self.mm(p, bw[:, 0:512], kw,
                            WR[:, 2048 * k + 128 * m:2048 * k + 128 * m + 128], "WR",
                            self.bv(512 * k, 512), f"xm{k}",
                            start=(k == 0), stop=(k == NT - 1), inc=(k == NT - 1))
                p.op("S", lambda e, bw=bw, m=m: e.copy(
                    self.bv(8192 + 515 * m + 3, 512), bw[:, 0:512]),
                    reads=(kw,), writes=("res",))
                p.op("V", lambda e, m=m: e.tensor_copy(
                    self.bv(8192 + 515 * m, 3), sb["xpt"][:, 3 * m:3 * m + 3]),
                    reads=("xpt",), writes=("res",))
                p.op("V", lambda e, m=m: e.tensor_scalar(
                    self.scf(0), self.bv(8192 + 515 * m, 512),
                    sb["v_wconv"][:, 4 * m:4 * m + 1], None, OP.mult),
                    reads=("res", "v_wconv"), writes=("scF0",))
                for q in (1, 2, 3):
                    p.op("V", lambda e, m=m, q=q: e.scalar_tensor_tensor(
                        self.scf(0), self.bv(8192 + 515 * m + q, 512),
                        sb["v_wconv"][:, 4 * m + q:4 * m + q + 1],
                        self.scf(0), OP.mult, OP.add),
                        reads=("res", "v_wconv", "scF0"), writes=("scF0",))
                p.op("S", lambda e, m=m: e.activation(
                    self.bsb(0), self.scf(0), AF.Silu,
                    bias=sb["v_bconv"][:, m:m + 1], scale=1.0),
                    reads=("scF0", "v_bconv"), writes=("sbf0",))
                p.dma("G", lambda e, m=m, t0=t0: e.dma_start(
                    out=ap_["xsmd"][128 * m:128 * (m + 1), t0:t0 + 512],
                    in_=self.bsb(0)), reads=("sbf0",), writes=("xsmd@d",))
                p.op("V", lambda e, m=m: e.tensor_copy(
                    sb["xpt"][:, 3 * m:3 * m + 3], self.bv(8192 + 515 * m + 512, 3)),
                    reads=("res",), writes=("xpt",))
        self.fence()
        self.wload(p, WR[:, 0:16384], "WR",
                   ap_["w_Win"].rearrange("p (k m) -> p k m", m=4096)[:, :, 2048:4096])
        for s in range(self.ns):
            t0 = TS * s
            for j in range(NT):
                p.dma("Q", lambda e, j=j, t0=t0: e.dma_start(
                    out=self.bv(512 * j, 512),
                    in_=ap_["xtd"][128 * j:128 * (j + 1), t0:t0 + 512]),
                    reads=("xtd@d",), writes=(f"xm{j}",))
            for m in range(16):
                bw, kw = self.bank()
                for k in range(NT):
                    self.mm(p, bw[:, 0:512], kw,
                            WR[:, 2048 * k + 128 * m:2048 * k + 128 * m + 128], "WR",
                            self.bv(512 * k, 512), f"xm{k}",
                            start=(k == 0), stop=(k == NT - 1), inc=(k == NT - 1))
                p.op("S", lambda e, bw=bw: e.activation(self.bsb(0), bw[:, 0:512], AF.Silu),
                     reads=(kw,), writes=("sbf0",))
                p.dma("G", lambda e, m=m, t0=t0: e.dma_start(
                    out=ap_["resd"][128 * m:128 * (m + 1), t0:t0 + 512],
                    in_=self.bsb(0)), reads=("sbf0",), writes=("resd@d",))
        self.fence()
        self.wload(p, WR[:, 16384:16896], "WR", ap_["w_Wx"])
        for half in range(2):
            m0 = 8 * half
            p.dma("G", lambda e, half=half: e.dma_start(
                out=WR[:, 0:16384].rearrange("p (k m) -> p k m", m=1024),
                in_=ap_["w_Wdt"].rearrange("p (k m) -> p k m", m=2048)[:, :, 1024 * half:1024 * (half + 1)]),
                writes=("WR",))
            for s in range(self.ns):
                t0 = TS * s
                for k in range(16):
                    p.dma("Q", lambda e, k=k, t0=t0: e.dma_start(
                        out=self.bv(512 * k, 512),
                        in_=ap_["xsmd"][128 * k:128 * (k + 1), t0:t0 + 512]),
                        reads=("xsmd@d",), writes=(f"xssm{k}",))
                for m in range(m0, m0 + 8):
                    bw, kw = self.bank()
                    for k in range(16):
                        self.mm(p, bw[:, 0:512], kw,
                                WR[:, 1024 * k + 128 * (m - m0):1024 * k + 128 * (m - m0) + 128],
                                "WR", self.bv(512 * k, 512), f"xssm{k}",
                                start=(k == 0), stop=(k == 15), inc=(k == 15))
                    p.op("V", lambda e, bw=bw, m=m: e.tensor_scalar(
                        self.scf(0), bw[:, 0:512], sb["v_bdt"][:, m:m + 1], None, OP.add),
                        reads=(kw, "v_bdt"), writes=("scF0",))
                    p.op("S", lambda e: e.activation(self.scf(1), self.scf(0), AF.Abs),
                         reads=("scF0",), writes=("scF1",))
                    p.op("S", lambda e: e.activation(self.scf(1), self.scf(1), AF.Exp,
                                                     scale=-1.0),
                         reads=("scF1",), writes=("scF1",))
                    p.op("S", lambda e: e.activation(self.scf(1), self.scf(1), AF.Ln,
                                                     bias=1.0),
                         reads=("scF1",), writes=("scF1",))
                    p.op("V", lambda e: e.tensor_scalar(
                        self.scf(2), self.scf(0), 0.0, None, OP.max),
                        reads=("scF0",), writes=("scF2",))
                    p.op("V", lambda e, m=m: e.tensor_tensor(
                        self.bsb(0), self.scf(1), self.scf(2), OP.add),
                        reads=("scF1", "scF2"), writes=("sbf0",))
                    p.dma("G", lambda e, m=m, t0=t0: e.dma_start(
                        out=ap_["dtd"][128 * m:128 * (m + 1), t0:t0 + 512],
                        in_=self.bsb(0)), reads=("sbf0",), writes=("dtd@d",))
                if half == 0:
                    bw, kw = self.bank()
                    for k in range(16):
                        self.mm(p, bw[0:32, 0:512], kw,
                                WR[:, 16384 + 32 * k:16384 + 32 * (k + 1)], "WR",
                                self.bv(512 * k, 512), f"xssm{k}",
                                start=(k == 0), stop=(k == 15), inc=(k == 15))
                    p.op("S", lambda e, bw=bw: e.copy(sb["bcs"][:, 0:512], bw[0:32, 0:512]),
                         reads=(kw,), writes=("bcs",))
                    p.dma("Q", lambda e, t0=t0: e.dma_start(
                        out=ap_["bcd"][:, t0:t0 + 512], in_=sb["bcs"][0:32, 0:512]),
                        reads=("bcs",), writes=("bcd@d",))
        self.fence()
        p.op("V", lambda e: e.memset(sb["hst"][:], 0.0), writes=("hst",))
        for s in range(self.ns):
            t0 = TS * s
            for k in range(16):
                p.dma("Q", lambda e, k=k, t0=t0: e.dma_start(
                    out=self.bv(512 * k, 512),
                    in_=ap_["xsmd"][128 * k:128 * (k + 1), t0:t0 + 512]),
                    reads=("xsmd@d",), writes=(f"xssm{k}",))
                p.dma("Q", lambda e, k=k, t0=t0: e.dma_start(
                    out=self.bv(8192 + 512 * k, 512),
                    in_=ap_["dtd"][128 * k:128 * (k + 1), t0:t0 + 512]),
                    reads=("dtd@d",), writes=(f"dt{k}",))
            for n in range(16):
                p.dma("Q", lambda e, n=n, t0=t0: e.dma_start(
                    out=sb["grow"][0:1, 0:512],
                    in_=ap_["bcd"][16 + n:17 + n, t0:t0 + 512]),
                    reads=("bcd@d",), writes=("grow",))
                bg, kg = self.bank()
                self.mm(p, bg[:, 0:512], kg, sb["c_onesrowf"][0:1, 0:128],
                        "c_onesrowf", sb["grow"][0:1, 0:512], "grow",
                        start=True, stop=True)
                p.op("S", lambda e, bg=bg, n=n: e.copy(
                    self.bv(16384 + 512 * n, 512), bg[:, 0:512]),
                    reads=(kg,), writes=("beta",))
            for j in range(16):
                p.op("V", lambda e, j=j: e.tensor_tensor(
                    sb["tmaj"][:, 0:512], self.bv(8192 + 512 * j, 512),
                    self.bv(512 * j, 512),
                    OP.mult), reads=(f"dt{j}", f"xssm{j}"), writes=("tmaj",))
                yfirst = True
                for n in range(16):
                    q = n % 4
                    q2 = n % 4
                    p.op("S", lambda e, j=j, n=n, q=q: e.activation(
                        self.fv(FA_SCF + 512 * q, 512),
                        self.bv(8192 + 512 * j, 512), AF.Exp,
                        scale=sb["v_A"][:, 16 * j + n:16 * j + n + 1]),
                        reads=(f"dt{j}", "v_A"), writes=(f"scF{q}",))
                    p.op("G", lambda e, n=n, q2=q2: e.tensor_tensor(
                        self.fv(FA_XSL + 512 * q2, 512), sb["tmaj"][:, 0:512],
                        self.bv(16384 + 512 * n, 512), OP.mult),
                        reads=("tmaj", "beta"), writes=(f"xsl{q2}",))
                    p.op("V", lambda e, j=j, n=n, q=q, q2=q2: e.tensor_tensor_scan(
                        self.fv(FA_OB + 512 * q2, 512),
                        self.fv(FA_SCF + 512 * q, 512),
                        self.fv(FA_XSL + 512 * q2, 512),
                        sb["hst"][:, 16 * j + n:16 * j + n + 1],
                        OP.mult, OP.add),
                        reads=(f"scF{q}", f"xsl{q2}", "hst"), writes=(f"ob{q2}",))
                    p.op("V", lambda e, j=j, n=n, q2=q2: e.tensor_copy(
                        sb["hst"][:, 16 * j + n:16 * j + n + 1],
                        self.fv(FA_OB + 512 * q2 + 511, 1)),
                        reads=(f"ob{q2}",), writes=("hst",))
                    if yfirst:
                        p.op("V", lambda e, q2=q2: e.tensor_copy(
                            self.fv(FA_YAC, 512), self.fv(FA_OB + 512 * q2, 512)),
                            reads=(f"ob{q2}",), writes=("yac",))
                        yfirst = False
                    else:
                        p.op("V", lambda e, q2=q2: e.tensor_tensor(
                            self.fv(FA_YAC, 512), self.fv(FA_YAC, 512),
                            self.fv(FA_OB + 512 * q2, 512), OP.add),
                            reads=(f"ob{q2}", "yac"), writes=("yac",))
                p.op("V", lambda e, j=j: e.scalar_tensor_tensor(
                    sb["tmaj"][:, 512:1024], self.bv(512 * j, 512), sb["v_D"][:, j:j + 1],
                    self.fv(FA_YAC, 512), OP.mult, OP.add),
                    reads=(f"xssm{j}", "v_D", "yac"), writes=("tmaj",))
                p.dma("G", lambda e, j=j, t0=t0: e.dma_start(
                    out=ap_["yd"][128 * j:128 * (j + 1), t0:t0 + 512],
                    in_=sb["tmaj"][:, 512:1024]), reads=("tmaj",), writes=("yd@d",))
        p.dma("Q", lambda e: e.dma_start(out=ap_["cc_h_in"], in_=sb["hst"][:]),
              reads=("hst",), writes=("cc_h_in@d",))
        p.cc(lambda e: e.collective_compute(
            "AllGather", OP.bypass, replica_groups=[[0, 1], [2, 3], [4, 5], [6, 7]],
            ins=[ap_["cc_h_in"].opt()], outs=[ap_["cc_h_out"].opt()]),
            reads=("cc_h_in@d",), writes=("cc_h_out@d",))
        p.dma("Q", lambda e: e.dma_start(out=sb["hcar"][:], in_=ap_["cc_h_out"][0:128, :]),
              reads=("cc_h_out@d",), writes=("hcar",))
        p.op("V", lambda e: e.tensor_scalar(
            sb["hcar"][:], sb["hcar"][:], sb["c_gate"][:, 0:1], None, OP.mult),
            reads=("hcar", "c_gate"), writes=("hcar",))
        self.fence()
        p.op("V", lambda e: e.memset(sb["hst"][:, 0:16], 0.0), writes=("hst",))
        for s in range(self.ns):
            t0 = TS * s
            for k in range(16):
                p.dma("Q", lambda e, k=k, t0=t0: e.dma_start(
                    out=self.bv(512 * k, 512),
                    in_=ap_["yd"][128 * k:128 * (k + 1), t0:t0 + 512]),
                    reads=("yd@d",), writes=(f"ysl{k}",))
                p.dma("Q", lambda e, k=k, t0=t0: e.dma_start(
                    out=self.bv(8192 + 512 * k, 512),
                    in_=ap_["dtd"][128 * k:128 * (k + 1), t0:t0 + 512]),
                    reads=("dtd@d",), writes=(f"dt{k}",))
                p.dma("Q", lambda e, k=k, t0=t0: e.dma_start(
                    out=self.bv(16384 + 512 * k, 512),
                    in_=ap_["resd"][128 * k:128 * (k + 1), t0:t0 + 512]),
                    reads=("resd@d",), writes=(f"res{k}",))
            for j in range(16):
                p.op("V", lambda e, j=j: e.tensor_tensor_scan(
                    self.scf(0), sb["onesF"][:], self.bv(8192 + 512 * j, 512),
                    sb["hst"][:, j:j + 1], OP.mult, OP.add),
                    reads=("onesF", f"dt{j}", "hst"), writes=("scF0",))
                p.op("V", lambda e, j=j: e.tensor_copy(
                    sb["hst"][:, j:j + 1], self.scf(0)[:, 511:512]),
                    reads=("scF0",), writes=("hst",))
                p.op("V", lambda e: e.memset(self.fv(FA_YAC, 512), 0.0),
                     writes=("yac",))
                for n in range(16):
                    p.op("S", lambda e, j=j, n=n: e.activation(
                        self.scf(1), self.scf(0), AF.Exp,
                        scale=sb["v_A"][:, 16 * j + n:16 * j + n + 1]),
                        reads=("scF0", "v_A"), writes=("scF1",))
                    p.op("V", lambda e, j=j, n=n: e.scalar_tensor_tensor(
                        self.fv(FA_YAC, 512), self.scf(1),
                        sb["hcar"][:, 16 * j + n:16 * j + n + 1],
                        self.fv(FA_YAC, 512), OP.mult, OP.add),
                        reads=("scF1", "hcar", "yac"), writes=("yac",))
                p.op("V", lambda e, j=j: e.tensor_tensor(
                    self.scf(2), self.bv(512 * j, 512), self.fv(FA_YAC, 512), OP.add),
                    reads=(f"ysl{j}", "yac"), writes=("scF2",))
                p.op("V", lambda e, j=j: e.tensor_tensor(
                    self.scf(3), self.scf(2), self.bv(16384 + 512 * j, 512), OP.mult),
                    reads=("scF2", f"res{j}"), writes=("scF3",))
                p.dma("G", lambda e, j=j, t0=t0: e.dma_start(
                    out=ap_["yfd"][128 * j:128 * (j + 1), t0:t0 + 512],
                    in_=self.scf(3)), reads=("scF3",), writes=("yfd@d",))
        self.fence()
        self.wload(p, WR[:, 0:16384], "WR", ap_["w_Wout"])
        for s in range(self.ns):
            t0 = TS * s
            for k in range(16):
                p.dma("Q", lambda e, k=k, t0=t0: e.dma_start(
                    out=self.bv(512 * k, 512),
                    in_=ap_["yfd"][128 * k:128 * (k + 1), t0:t0 + 512]),
                    reads=("yfd@d",), writes=(f"ysl{k}",))
            for j in range(NT):
                p.dma("Q", lambda e, j=j, t0=t0: e.dma_start(
                    out=self.xslv(j, 0, 512),
                    in_=ap_["r2"][128 * j:128 * (j + 1), t0:t0 + 512]),
                    reads=("r2@d",), writes=(xk[j],))
            for mo in range(NT):
                bw, kw = self.bank()
                for k in range(16):
                    self.mm(p, bw[:, 0:512], kw,
                            WR[:, 1024 * k + 128 * mo:1024 * k + 128 * mo + 128], "WR",
                            self.bv(512 * k, 512), f"ysl{k}",
                            start=(k == 0), stop=(k == 15), inc=(k == 15))
                p.op("V", lambda e, mo=mo, bw=bw: e.tensor_tensor(
                    self.scf(2), self.xslv(mo, 0, 512), bw[:, 0:512], OP.add),
                    reads=(kw, xk[mo]), writes=("scF2",))
                p.dma("Q", lambda e, mo=mo, t0=t0: e.dma_start(
                    out=ap_["r3"][128 * mo:128 * (mo + 1), t0:t0 + 512],
                    in_=self.scf(2)), reads=("scF2",), writes=("r3@d",))
                self.dbg1(p, "x3", self.scf(2), "scF2", mo, t0)

    def l1f(self, p):
        sb, ap_ = self.sb, self.ap
        WR = sb["WRES"]
        tl = self.tl
        ntt = tl // 128
        xk = [f"xsl{j}" for j in range(NT)]
        for s in range(self.ns):
            t0 = TS * s
            for j in range(NT):
                p.dma("Q", lambda e, j=j, t0=t0: e.dma_start(
                    out=self.xslv(j, 0, 512),
                    in_=ap_["r3"][128 * j:128 * (j + 1), t0:t0 + 512]),
                    reads=("r3@d",), writes=(xk[j],))
            self._ln_to(p, xk, [self.xslv(j, 0, 512) for j in range(NT)],
                        [self.bv(512 * j, 512) for j in range(NT)],
                        [f"xm{j}" for j in range(NT)])
            for j in range(NT):
                p.dma("G", lambda e, j=j, t0=t0: e.dma_start(
                    out=ap_["ln4d"][128 * j:128 * (j + 1), t0:t0 + 512],
                    in_=self.bv(512 * j, 512)),
                    reads=(f"xm{j}",), writes=("ln4d@d",))
        self.fence()
        self.wload(p, WR[:, 0:256], "WR", ap_["w_rt0"])
        self.wload(p, WR[:, 256:512], "WR", ap_["w_rt1"])
        p.op("V", lambda e: e.memset(sb["crow"][:], 0.0), writes=("crow",))
        p.op("V", lambda e: e.memset(sb["crow2"][:], 0.0), writes=("crow2",))
        p.op("V", lambda e: e.memset(sb["zz"][:, 0:1], 0.0), writes=("zz",))
        p.op("V", lambda e: e.memset(sb["wgt"][:], 0.0), writes=("wgt",))
        for s in range(self.ns):
            t0 = TS * s
            for j in range(NT):
                p.dma("Q", lambda e, j=j, t0=t0: e.dma_start(
                    out=self.bv(512 * j, 512),
                    in_=ap_["ln4d"][128 * j:128 * (j + 1), t0:t0 + 512]),
                    reads=("ln4d@d",), writes=(f"xm{j}",))
            for g in range(2):
                crk = "crow" if g == 0 else "crow2"
                crb = sb["crow"] if g == 0 else sb["crow2"]
                bw, kw = self.bank()
                for k in range(NT):
                    self.mm(p, bw[0:32, 0:512], kw,
                            WR[:, 256 * g + 32 * k:256 * g + 32 * (k + 1)], "WR",
                            self.bv(512 * k, 512), f"xm{k}",
                            start=(k == 0), stop=(k == NT - 1), inc=(k == NT - 1))
                p.op("S", lambda e, bw=bw: e.copy(sb["lgt"][:], bw[0:32, 0:512]),
                     reads=(kw,), writes=("lgt",))
                for tc in range(4):
                    tt = 4 * s + tc
                    bt, kt = self.bank()
                    self.trans(p, bt[:, 0:32], kt, sb["lgt"][0:32, 128 * tc:128 * (tc + 1)],
                               "lgt", f32=True)
                    p.op("S", lambda e, bt=bt: e.copy(sb["prb"][:, 0:32], bt[:, 0:32]),
                         reads=(kt,), writes=("prb",))
                    self._softmax_topk(p, 32, 2)
                    for jj in range(2):
                        wv = sb["m8"][:, jj:jj + 1]
                        p.op("V", lambda e, wv=wv: e.tensor_scalar(
                            sb["ohb"][:, 0:32], sb["prb"][:, 0:32], wv, None, OP.is_equal),
                            reads=("prb", "m8"), writes=("ohb",))
                        p.op("V", lambda e: e.tensor_copy(sb["oh"][:, 0:32], sb["ohb"][:, 0:32]),
                             reads=("ohb",), writes=("oh",))
                        p.op("V", lambda e: e.tensor_tensor(
                            sb["pr2"][:, 0:32], sb["oh"][:, 0:32], sb["c_erow32"][:, 0:32],
                            OP.mult), reads=("oh", "c_erow32"), writes=("pr2",))
                        p.op("V", lambda e: e.tensor_reduce(
                            sb["idxc"][:, 0:1], sb["pr2"][:, 0:32], AX.X, OP.add),
                            reads=("pr2",), writes=("idxc",))
                        bp, kp = self.bank()
                        self.mm(p, bp[:, 0:32], kp, sb["c_tril"][:], "c_tril",
                                sb["ohb"][:, 0:32], "ohb", start=True, stop=True)
                        p.op("S", lambda e, bp=bp: e.copy(sb["pr2"][:, 0:32], bp[:, 0:32]),
                             reads=(kp,), writes=("pr2",))
                        p.op("V", lambda e: e.tensor_tensor(
                            sb["pr2"][:, 0:32], sb["pr2"][:, 0:32], sb["oh"][:, 0:32],
                            OP.mult), reads=("pr2", "oh"), writes=("pr2",))
                        p.op("V", lambda e: e.tensor_reduce(
                            sb["rkc"][:, 0:1], sb["pr2"][:, 0:32], AX.X, OP.add),
                            reads=("pr2",), writes=("rkc",))
                        bc2, kc2 = self.bank()
                        self.mm(p, bc2[:, 0:32], kc2, sb["c_onesrowf"][0:1, 0:128],
                                "c_onesrowf", crb[0:1, 0:32], crk, start=True, stop=True)
                        p.op("S", lambda e, bc2=bc2: e.copy(sb["pr2"][:, 0:32], bc2[:, 0:32]),
                             reads=(kc2,), writes=("pr2",))
                        p.op("V", lambda e: e.tensor_tensor(
                            sb["pr2"][:, 0:32], sb["pr2"][:, 0:32], sb["oh"][:, 0:32],
                            OP.mult), reads=("pr2", "oh"), writes=("pr2",))
                        p.op("V", lambda e: e.tensor_reduce(
                            sb["bsc"][:, 0:1], sb["pr2"][:, 0:32], AX.X, OP.add),
                            reads=("pr2",), writes=("bsc",))
                        p.op("V", lambda e: e.tensor_tensor(
                            sb["rkc"][:, 0:1], sb["rkc"][:, 0:1], sb["bsc"][:, 0:1],
                            OP.add), reads=("rkc", "bsc"), writes=("rkc",))
                        bc3, kc3 = self.bank()
                        self.mm(p, bc3[0:1, 0:32], kc3, sb["c_onescolb"][:, 0:1],
                                "c_onescolb", sb["ohb"][:, 0:32], "ohb",
                                start=True, stop=True)
                        p.op("V", lambda e, crb=crb, bc3=bc3: e.tensor_tensor(
                            crb[0:1, 0:32], crb[0:1, 0:32], bc3[0:1, 0:32], OP.add),
                            reads=(crk, kc3), writes=(crk,))
                        col = 128 * g + 64 * jj + tt
                        p.op("V", lambda e: e.tensor_scalar(
                            sb["slotF"][:, 0:1], sb["idxc"][:, 0:1], float(CAP),
                            sb["rkc"][:, 0:1], OP.mult, OP.add),
                            reads=("idxc", "rkc"), writes=("slotF",))
                        p.op("V", lambda e: e.tensor_scalar(
                            sb["ovmu"][:, 0:1], sb["rkc"][:, 0:1], float(CAP), None,
                            OP.is_ge), reads=("rkc",), writes=("ovm",))
                        p.op("V", lambda e: e.tensor_scalar(
                            sb["slotF"][:, 1:2], sb["rkc"][:, 0:1], float(SLOTS), None,
                            OP.add), reads=("rkc",), writes=("slotF",))
                        p.op("V", lambda e: e.select(
                            sb["slotF"][:, 2:3], sb["ovmu"][:, 0:1],
                            sb["slotF"][:, 1:2], sb["slotF"][:, 0:1], add_drain=True),
                            reads=("ovm", "slotF"), writes=("slotF",))
                        p.op("V", lambda e: e.select(
                            sb["slotF"][:, 3:4], sb["ovmu"][:, 0:1], sb["zz"][:, 0:1],
                            sb["slotF"][:, 0:1], add_drain=True),
                            reads=("ovm", "slotF", "zz"), writes=("slotF",))
                        p.op("V", lambda e, col=col: e.tensor_copy(
                            sb["slot"][:, col:col + 1], sb["slotF"][:, 2:3]),
                            reads=("slotF",), writes=("slot",))
                        p.op("V", lambda e, col=col: e.tensor_copy(
                            sb["slotc"][:, col:col + 1], sb["slotF"][:, 3:4]),
                            reads=("slotF",), writes=("slotc",))
                        p.op("V", lambda e, col=col, wv=wv: e.select(
                            sb["wgt"][:, col:col + 1], sb["ovmu"][:, 0:1], sb["zz"][:, 0:1],
                            wv, add_drain=True), reads=("ovm", "m8", "zz"), writes=("wgt",))
        p.op("V", lambda e: e.tensor_scalar(
            sb["wgt"][:], sb["wgt"][:], 0.5, None, OP.mult),
            reads=("wgt",), writes=("wgt",))
        self.fence()
        p.op("V", lambda e: e.memset(sb["gath"][:], 0.0), writes=("gath",))
        for g in range(2):
            for ch in range(SLOTS // 128):
                p.dma("Q", lambda e, g=g, ch=ch: e.dma_start(
                    out=ap_[f"xs{g}"][128 * ch:128 * (ch + 1), :], in_=sb["gath"][:]),
                    reads=("gath",), writes=(f"xs{g}@d",))
        for tt in range(ntt):
            bt, ktb = self.bank()
            ptb = bt.bitcast(BF16)
            for j in range(NT):
                p.dma("Q", lambda e, j=j, tt=tt: e.dma_start(
                    out=self.bv(512 * j, 128),
                    in_=ap_["ln4d"][128 * j:128 * (j + 1), 128 * tt:128 * (tt + 1)]),
                    reads=("ln4d@d",), writes=(f"xm{j}",))
                self.trans(p, ptb[0:128, 128 * j:128 * (j + 1)], ktb,
                           self.bv(512 * j, 128), f"xm{j}")
            p.op("S", lambda e, ptb=ptb: e.copy(sb["tmaj"][:], ptb[0:128, 0:1024]),
                 reads=(ktb,), writes=("tmaj",))
            for g in range(2):
                for jj in range(2):
                    col = 128 * g + 64 * jj + tt
                    p.dma("G", lambda e, g=g, col=col: e.indirect_dma_start(
                        out=ap_[f"xs{g}"],
                        out_offset=IndirectOffsetOnAxis(ap=sb["slot"][:, col:col + 1], axis=0),
                        in_=sb["tmaj"][:], in_offset=None),
                        reads=("tmaj", "slot"), writes=(f"xs{g}@d",))
        self.fence()
        for g in range(2):
            for e in range(NE1):
                wb = 12288 * (e % 2)
                wkey = "WR" if e % 2 == 0 else "WR2"
                self.wload(p, WR[:, wb:wb + 4096], wkey, ap_[f"w_wg{g}"][e])
                self.wload(p, WR[:, wb + 4096:wb + 8192], wkey, ap_[f"w_wu{g}"][e])
                self.wload(p, WR[:, wb + 8192:wb + 12288], wkey, ap_[f"w_wd{g}"][e])
                for dc in range(NT):
                    p.dma("Q", lambda e_, g=g, e=e, dc=dc: e_.dma_start(
                        out=self.bv(640 * dc, 640),
                        in_=ap_[f"xs{g}"][CAP * e:CAP * (e + 1), 128 * dc:128 * (dc + 1)],
                        transpose=True),
                        reads=(f"xs{g}@d",), writes=(f"xm{dc}",))
                for m in range(4):
                    for fo, fn in ((0, 512), (512, 128)):
                        bg_, kg_ = self.bank()
                        bu_, ku_ = self.bank()
                        for k in range(NT):
                            self.mm(p, bg_[:, 0:fn], kg_,
                                    WR[:, wb + 512 * k + 128 * m:wb + 512 * k + 128 * m + 128],
                                    wkey, self.bv(640 * k + fo, fn), f"xm{k}",
                                    start=(k == 0), stop=(k == NT - 1), inc=(k == NT - 1))
                        for k in range(NT):
                            self.mm(p, bu_[:, 0:fn], ku_,
                                    WR[:, wb + 4096 + 512 * k + 128 * m:wb + 4096 + 512 * k + 128 * m + 128],
                                    wkey, self.bv(640 * k + fo, fn), f"xm{k}",
                                    start=(k == 0), stop=(k == NT - 1), inc=(k == NT - 1))
                        p.op("S", lambda e_, bg_=bg_, fn=fn: e_.activation(
                            self.bsb(0, fn), bg_[:, 0:fn], AF.Silu),
                            reads=(kg_,), writes=("sbf0",))
                        p.op("V", lambda e_, m=m, fo=fo, fn=fn, bu_=bu_: e_.tensor_tensor(
                            self.bv(8192 + 640 * m + fo, fn), self.bsb(0, fn),
                            bu_[:, 0:fn], OP.mult),
                            reads=("sbf0", ku_), writes=("hid",))
                for sc in range(5):
                    for dc2 in range(2):
                        bw, kw = self.bank()
                        for ke in range(4):
                            self.mm(p, bw[:, 0:512], kw,
                                    self.bv(8192 + 640 * ke + 128 * sc, 128), "hid",
                                    WR[:, wb + 8192 + 1024 * ke + 512 * dc2:wb + 8192 + 1024 * ke + 512 * (dc2 + 1)],
                                    wkey, start=(ke == 0), stop=(ke == 3), inc=(ke == 3))
                        p.op("S", lambda e_, bw=bw, dc2=dc2: e_.copy(
                            sb["tmaj"][:, 512 * dc2:512 * (dc2 + 1)], bw[:, 0:512]),
                            reads=(kw,), writes=("tmaj",))
                    p.dma("Q", lambda e_, g=g, e=e, sc=sc: e_.dma_start(
                        out=ap_[f"ys{g}"][CAP * e + 128 * sc:CAP * e + 128 * (sc + 1), :],
                        in_=sb["tmaj"][:]),
                        reads=("tmaj",), writes=(f"ys{g}@d",))
        self.fence()
        for tt in range(ntt):
            bt1, kt1 = self.bank()
            bt2, kt2 = self.bank()
            for j in range(NT):
                p.dma("Q", lambda e_, j=j, tt=tt: e_.dma_start(
                    out=self.xslv(j, 0, 128),
                    in_=ap_["r3"][128 * j:128 * (j + 1), 128 * tt:128 * (tt + 1)]),
                    reads=("r3@d",), writes=(xk[j],))
                bsel, ksel = (bt1, kt1) if j < 4 else (bt2, kt2)
                self.trans(p, bsel[:, 128 * (j % 4):128 * (j % 4) + 128], ksel,
                           self.xslv(j, 0, 128), xk[j], f32=True)
            p.op("S", lambda e_, bt1=bt1: e_.copy(sb["ytok"][:, 0:512], bt1[:, 0:512]),
                 reads=(kt1,), writes=("ytok",))
            p.op("S", lambda e_, bt2=bt2: e_.copy(sb["ytok"][:, 512:1024], bt2[:, 0:512]),
                 reads=(kt2,), writes=("ytok",))
            for g in range(2):
                for jj in range(2):
                    col = 128 * g + 64 * jj + tt
                    p.dma("G", lambda e_, g=g, col=col: e_.indirect_dma_start(
                        out=sb["gath"][:], out_offset=None,
                        in_=ap_[f"ys{g}"],
                        in_offset=IndirectOffsetOnAxis(ap=sb["slotc"][:, col:col + 1], axis=0)),
                        reads=(f"ys{g}@d", "slotc"), writes=("gath",))
                    p.op("V", lambda e_, col=col: e_.scalar_tensor_tensor(
                        sb["ytok"][:], sb["gath"][:], sb["wgt"][:, col:col + 1],
                        sb["ytok"][:], OP.mult, OP.add),
                        reads=("gath", "wgt", "ytok"), writes=("ytok",))
            p.dma("Q", lambda e_, tt=tt: e_.dma_start(
                out=ap_["out"][128 * tt:128 * (tt + 1), :], in_=sb["ytok"][:]),
                reads=("ytok",), writes=("out@d",))

    def build(self):
        self.declare()
        p = Prog(self.nc)
        self.p = p
        with ExitStack() as ctx:
            self.alloc(ctx)
            self.ldvec(p)
            pl = getattr(self, "passes", None)
            if "l0a" in self.stages:
                for nm, fn in [("A1", self.l0a_A1), ("A2", self.l0a_A2),
                               ("A3", self.l0a_A3), ("AGS", self.l0a_AGS),
                               ("A4", self.l0a_A4), ("A5", self.l0a_A5),
                               ("A6", self.l0a_A6), ("A7", self.l0a_A7)]:
                    if pl is None or nm in pl:
                        fn(p)
                        if nm != "AGS":
                            self.fence()
            if "l0f" in self.stages:
                self.l0f(p)
                self.fence()
            if "l1a" in self.stages:
                self.l1a(p)
                self.fence()
            if "l1f" in self.stages:
                self.l1f(p)
            p.emit()
        return self.nc


def _kt(w):
    din, dout = w.shape
    nk = din // 128
    return np.ascontiguousarray(
        w.reshape(nk, 128, dout).transpose(1, 0, 2).reshape(128, nk * dout))


def _vec(v, nk=8):
    return np.ascontiguousarray(np.asarray(v).reshape(nk, 128).T)


def _consts(core):
    import ml_dtypes
    BFD = ml_dtypes.bfloat16
    ident = np.eye(128)
    headrep = (np.arange(128)[:, None] // 64 == np.arange(128)[None] // 64)
    tril = (np.arange(128)[:, None] < np.arange(128)[None]).astype(np.float32)
    import ml_dtypes
    BF = ml_dtypes.bfloat16
    return {
        "c_ident": ident.astype(BF),
        "c_identf": ident.astype(np.float32),
        "c_headrep": headrep.astype(BF),
        "c_gnrep": (headrep / 64.0).astype(BF),
        "c_tril": tril.T.astype(BF),
        "c_onesrowf": np.ones((1, 128), np.float32),
        "c_onescolf": np.ones((128, 1), np.float32),
        "c_erow32": np.tile(np.arange(32).astype(np.float32)[None], (128, 1)),
        "c_gate": np.full((128, 1), 1.0 if core % 2 == 1 else 0.0, np.float32),
        "c_onescolb": np.ones((128, 1), BFD),
    }


def stage_inputs(inputs, tl=4096):
    x = np.asarray(inputs["L0_attn_x"], np.float32)
    B = x.shape[0]
    maps = []
    wcommon = {
        "w_Wr": _kt(inputs["L0_attn_Wr"]), "w_Wk": _kt(inputs["L0_attn_Wk"]),
        "w_Wv": _kt(inputs["L0_attn_Wv"]), "w_Wg": _kt(inputs["L0_attn_Wg"]),
        "w_Wo": _kt(inputs["L0_attn_Wo"]), "w_wA": _kt(inputs["L0_attn_wA"]),
        "w_wB": np.ascontiguousarray(inputs["L0_attn_wB"]),
        "w_f1": _kt(inputs["L0_attn_ffn_w1"]), "w_f2": _kt(inputs["L0_attn_ffn_w2"]),
        "w_sup": np.concatenate([_kt(inputs["L0_ffn_shared_up"][s].T) for s in range(2)], 1),
        "w_sdn": np.concatenate([_kt(inputs["L0_ffn_shared_down"][s].T) for s in range(2)], 1),
        "w_rup": np.stack([_kt(inputs["L0_ffn_routed_up"][r].T) for r in range(16)]),
        "w_rdn": np.stack([_kt(inputs["L0_ffn_routed_down"][r].T) for r in range(16)]),
        "w_rtr": _kt(np.ascontiguousarray(inputs["L0_ffn_router_w"].T)),
        "w_Win": _kt(np.ascontiguousarray(inputs["L1_attn_W_in"].T)),
        "c_wconv": _wconv(inputs["L1_attn_W_conv"]),
        "c_bconv": _vec(inputs["L1_attn_b_conv"], 16),
        "w_Wx": _kt(np.ascontiguousarray(inputs["L1_attn_W_x"].T)),
        "w_Wdt": _kt(np.ascontiguousarray(inputs["L1_attn_W_dt"].T)),
        "c_bdt": _vec(inputs["L1_attn_b_dt"], 16),
        "w_Wout": _kt(np.ascontiguousarray(inputs["L1_attn_W_out"].T)),
        "c_A": _amat(inputs["L1_attn_A_log"]),
        "c_D": _vec(inputs["L1_attn_D"], 16),
        "w_rt0": _kt(inputs["L1_ffn_router_w0"]),
        "w_rt1": _kt(inputs["L1_ffn_router_w1"]),
        "c_ln1w": _vec(inputs["L0_attn_ln1_w"]),
        "c_mu_r": _vec(inputs["L0_attn_mu_r"]), "c_mu_k": _vec(inputs["L0_attn_mu_k"]),
        "c_mu_v": _vec(inputs["L0_attn_mu_v"]), "c_mu_w": _vec(inputs["L0_attn_mu_w"]),
        "c_mu_g": _vec(inputs["L0_attn_mu_g"]), "c_w0": _vec(inputs["L0_attn_w0"]),
        "c_u": _vec(inputs["L0_attn_u"].reshape(-1)),
        "c_gnw": _vec(inputs["L0_attn_gn_w"].reshape(-1)),
        "c_gnb": _vec(inputs["L0_attn_gn_b"].reshape(-1)),
        "c_ln2w": _vec(inputs["L0_attn_ln2_w"]),
    }
    for g in (0, 1):
        wcommon[f"w_wg{g}"] = np.stack([_kt(inputs[f"L1_ffn_wg{g}"][e]) for e in range(NE1)])
        wcommon[f"w_wu{g}"] = np.stack([_kt(inputs[f"L1_ffn_wu{g}"][e]) for e in range(NE1)])
        wcommon[f"w_wd{g}"] = np.stack([_kt(inputs[f"L1_ffn_wd{g}"][e]) for e in range(NE1)])
    for c in range(8):
        b, h = c // 2, c % 2
        s0 = h * tl
        xc = np.zeros((D, tl + 1), np.float32)
        xc[:, 1:] = x[b, s0:s0 + tl].T
        if h == 1:
            xc[:, 0] = x[b, s0 - 1]
        m = {"xdm": np.ascontiguousarray(xc)}
        m.update(wcommon)
        m.update(_consts(c))
        maps.append(m)
    return maps


def _wconv(wc):
    w = np.asarray(wc)[:, 0, :]
    return np.ascontiguousarray(
        w.reshape(16, 128, 4).transpose(1, 0, 2).reshape(128, 64))


def _amat(alog):
    a = np.asarray(alog, np.float32)
    return np.ascontiguousarray(
        a.reshape(16, 128, 16).transpose(1, 0, 2).reshape(128, 256))


_BUILT = {}


def kernel(**inputs):
    key = ("full", 4096)
    if key not in _BUILT:
        kk = K(tl=4096, stages=STAGES)
        _BUILT[key] = kk.build()
    nc = _BUILT[key]
    maps = stage_inputs(inputs, 4096)
    res = run_bass_kernel_spmd(nc, maps, core_ids=list(range(8)))
    out = np.zeros((4, 8192, 1024), np.float32)
    for c in range(8):
        b, h = c // 2, c % 2
        out[b, h * 4096:(h + 1) * 4096] = res.results[c]["out"]
    return out


# version 06
# speedup vs baseline: 1.1169x; 1.0033x over previous
from contextlib import ExitStack

import numpy as np

import concourse.bass as bass
import concourse.mybir as mybir
from concourse.alu_op_type import AluOpType
from concourse.bass_utils import run_bass_kernel_spmd
from bass_rust import IndirectOffsetOnAxis

F32 = mybir.dt.float32
BF16 = mybir.dt.bfloat16
U32 = mybir.dt.uint32
AF = mybir.ActivationFunctionType
AX = mybir.AxisListType
OP = AluOpType

D = 1024
NT = 8
TS = 512
CHUNK = 16
NCH = TS // CHUNK
CAP = 640
NE1 = 32
SLOTS = NE1 * CAP
XS_ROWS = SLOTS + 8192

EPS_LN = 1e-5
EPS_RMS = 1e-6
EPS_GN = 1e-5


class Prog:
    CENG = ("T", "V", "S", "G")
    QENG = ("Q", "S", "G")

    def __init__(self, nc):
        self.nc = nc
        self.ops = []
        self.d_rr = {q: 0 for q in self.QENG}
        self.NPOOL = 8

    def op(self, eng, fn, reads=(), writes=(), inc=True):
        self.ops.append(dict(eng=eng, kind="c", fn=fn, reads=tuple(reads),
                             writes=tuple(writes), inc=inc))

    def dma(self, q, fn, reads=(), writes=()):
        self.ops.append(dict(eng=q, kind="d", fn=fn, reads=tuple(reads),
                             writes=tuple(writes)))

    def cc(self, fn, reads=(), writes=()):
        self.ops.append(dict(eng="G", kind="cc", fn=fn, reads=tuple(reads),
                             writes=tuple(writes)))

    def fence(self, reads, writes):
        self.op("V", lambda e: e.nop(), reads=tuple(reads), writes=tuple(writes))

    def finalize(self):
        c_cnt = {e: 0 for e in self.CENG}
        d_idx = {}
        cc_cnt = 0
        prev = None
        groups = []
        for i, o in enumerate(self.ops):
            if o["kind"] == "d":
                q = o["eng"]
                if prev is not None and prev[0] == q and prev[1] == i - 1:
                    groups[-1].append(i)
                    prev = (q, i)
                else:
                    groups.append([i])
                    prev = (q, i)
            else:
                prev = None
        gof = {}
        for gi, g in enumerate(groups):
            q = self.ops[g[0]]["eng"]
            slot = self.d_rr[q]
            self.d_rr[q] = (slot + 1) % self.NPOOL
            key = (q, slot)
            start = d_idx.get(key, 0)
            for i in g:
                d_idx[key] = d_idx.get(key, 0) + 16
                self.ops[i]["token"] = ("d", key, d_idx[key])
                gof[i] = (key, start, None)
            for i in g:
                gof[i] = (key, start, d_idx[key])
        for o in self.ops:
            if o["kind"] == "c":
                if o["inc"]:
                    c_cnt[o["eng"]] += 1
                    o["token"] = ("c", o["eng"], c_cnt[o["eng"]])
                else:
                    o["token"] = None
            elif o["kind"] == "cc":
                cc_cnt += 1
                o["token"] = ("cc", "CC", cc_cnt)
        pend = {e: None for e in self.CENG}
        for o in reversed(self.ops):
            if o["kind"] == "c":
                if o["inc"]:
                    pend[o["eng"]] = o["token"]
                elif o["token"] is None:
                    o["token"] = pend[o["eng"]]
        state = {}
        c_sofar = {e: 0 for e in self.CENG}
        for i, o in enumerate(self.ops):
            deps = []
            for b in o["reads"]:
                st = state.get(b)
                if st and st["w"]:
                    deps.append(st["w"])
            for b in o["writes"]:
                st = state.get(b)
                if st:
                    if st["w"]:
                        deps.append(st["w"])
                    deps.extend(st["r"])
            need = {}
            own_key, own_start = None, 0
            if o["kind"] == "d":
                own_key, own_start, _ = gof[i]
                if own_start > 0:
                    need[("d", own_key)] = own_start
            for dep in deps:
                if dep is None:
                    continue
                kind, key, cnt = dep
                if o["kind"] == "c" and kind == "c" and key == o["eng"] == "T":
                    continue
                if kind == "d" and key == own_key and cnt > own_start:
                    continue
                if kind == "c" and o["kind"] != "d" and key == o.get("eng"):
                    cnt = min(cnt, c_sofar[key])
                    if cnt == 0:
                        continue
                elif kind == "c" and o["kind"] == "d" and key == o.get("eng") and key in c_sofar:
                    cnt = min(cnt, c_sofar[key])
                    if cnt == 0:
                        continue
                k = (kind, key)
                if k not in need or need[k] < cnt:
                    need[k] = cnt
            o["waits"] = need
            if o["kind"] == "c" and o["inc"]:
                c_sofar[o["eng"]] += 1
            tok = o["token"]
            if o["kind"] == "d":
                key, start, end = gof[i]
                tok = ("d", key, end)
            for b in o["reads"]:
                st = state.setdefault(b, {"w": None, "r": []})
                if tok:
                    st["r"].append(tok)
                    if len(st["r"]) > 32:
                        st["r"] = st["r"][-32:]
            for b in o["writes"]:
                state[b] = {"w": tok, "r": []}

    def emit(self):
        nc = self.nc
        self.finalize()
        with ExitStack() as ctx:
            csem = {e: ctx.enter_context(nc.semaphore(f"c{e}")) for e in self.CENG}
            dsem = {(q, s): ctx.enter_context(nc.semaphore(f"d{q}{s}"))
                    for q in self.QENG for s in range(self.NPOOL)}
            ccsem = ctx.enter_context(nc.semaphore("ccs"))
            blk = ctx.enter_context(nc.Block())

            def semof(kind, key):
                return csem[key] if kind == "c" else (dsem[key] if kind == "d" else ccsem)

            eng_ops = {e: [o for o in self.ops if o["eng"] == e]
                       for e in ("T", "V", "S", "G", "Q")}
            final_d, final_c, n_cc = {}, {e: 0 for e in self.CENG}, 0
            for o in self.ops:
                if o["kind"] == "d":
                    final_d[o["token"][1]] = o["token"][2]
                elif o["kind"] == "c" and o["inc"]:
                    final_c[o["eng"]] = o["token"][2]
                elif o["kind"] == "cc":
                    n_cc = o["token"][2]

            def run_engine(e, builder):
                sat = {}
                for o in eng_ops[e]:
                    for (kind, key), cnt in sorted(o["waits"].items(), key=str):
                        if sat.get((kind, key), 0) >= cnt:
                            continue
                        builder.wait_ge(semof(kind, key), cnt)
                        sat[(kind, key)] = cnt
                    inst = o["fn"](builder)
                    if o["kind"] == "c":
                        if o["inc"]:
                            inst.then_inc(csem[e], 1)
                    elif o["kind"] == "d":
                        inst.then_inc(dsem[o["token"][1]], 16)
                    else:
                        inst.then_inc(ccsem)

            @blk.tensor
            def _(t):
                run_engine("T", t)

            @blk.vector
            def _(v):
                run_engine("V", v)

            @blk.scalar
            def _(s):
                run_engine("S", s)

            @blk.gpsimd
            def _(g):
                run_engine("G", g)

            @blk.sync
            def _(q):
                run_engine("Q", q)
                for key in sorted(final_d, key=str):
                    q.wait_ge(dsem[key], final_d[key])
                for e in self.CENG:
                    if final_c[e]:
                        q.wait_ge(csem[e], final_c[e])
                if n_cc:
                    q.wait_ge(ccsem, n_cc)


def cchunks(total, cap=512):
    off = 0
    while off < total:
        n = min(cap, total - off)
        yield off, n
        off += n


STAGES = ("l0a", "l0f", "l1a", "l1f")

FA_XSL = 0
FA_LB = 0
FA_ACC = 4224
FA_OB = 4224
FA_SCF = 8320
FA_YAC = 10368
FA_AB = 10880
FA_END = 14976



class K:
    def __init__(self, tl=4096, stages=STAGES, dbg=()):
        self.tl = tl
        self.ns = tl // TS
        self.stages = stages
        self.dbg = set(dbg)
        self.nc = bass.Bass("TRN2", target_bir_lowering=False)
        self.ap = {}
        self.sb = {}
        self.psum = []
        self._pb = 0
        self.p = None

    def din(self, name, shape, dt=F32):
        self.ap[name] = self.nc.dram_tensor(name, list(shape), dt, kind="ExternalInput").ap()

    def dout(self, name, shape, dt=F32):
        self.ap[name] = self.nc.dram_tensor(name, list(shape), dt, kind="ExternalOutput").ap()

    def dint(self, name, shape, dt=F32):
        self.ap[name] = self.nc.dram_tensor(name, list(shape), dt, kind="Internal").ap()

    def declare(self):
        tl = self.tl
        self.din("xdm", (D, tl + 1))
        for n in ("ln1w", "mu_r", "mu_k", "mu_v", "mu_w", "mu_g", "w0", "u",
                  "gnw", "gnb", "ln2w"):
            self.din("c_" + n, (128, 8))
        for n in ("Wr", "Wk", "Wv", "Wg", "Wo"):
            self.din("w_" + n, (128, 8 * D))
        self.din("w_wA", (128, 8 * 64))
        self.din("w_wB", (64, D))
        self.din("w_f1", (128, 8 * 4096))
        self.din("w_f2", (128, 32 * D))
        self.din("w_sup", (128, 2 * 8 * 512))
        self.din("w_sdn", (128, 2 * 4 * D))
        self.din("w_rup", (16, 128, 8 * 512))
        self.din("w_rdn", (16, 128, 4 * D))
        self.din("w_rtr", (128, 8 * 16))
        self.din("w_Win", (128, 8 * 4096))
        self.din("c_wconv", (128, 16 * 4))
        self.din("c_bconv", (128, 16))
        self.din("w_Wx", (128, 16 * 32))
        self.din("w_Wdt", (128, 16 * 2048))
        self.din("c_bdt", (128, 16))
        self.din("w_Wout", (128, 16 * D))
        self.din("c_A", (128, 16 * 16))
        self.din("c_D", (128, 16))
        self.din("w_rt0", (128, 8 * 32))
        self.din("w_rt1", (128, 8 * 32))
        for g in (0, 1):
            self.din(f"w_wg{g}", (NE1, 128, 8 * 512))
            self.din(f"w_wu{g}", (NE1, 128, 8 * 512))
            self.din(f"w_wd{g}", (NE1, 128, 4 * D))
        self.din("c_ident", (128, 128), BF16)
        self.din("c_identf", (128, 128))
        self.din("c_headrep", (128, 128), BF16)
        self.din("c_gnrep", (128, 128), BF16)
        self.din("c_onesrowf", (1, 128))
        self.din("c_onescolf", (128, 1))
        self.din("c_onescolb", (128, 1), BF16)
        self.din("c_tril", (128, 128), BF16)
        self.din("c_erow32", (128, 32))
        self.din("c_gate", (128, 1))

        self.dint("r1", (D, tl))
        self.dint("r2", (D, tl))
        self.dint("r3", (D, tl))
        self.dint("ln4d", (D, tl), BF16)
        self.dint("sp_o", (D, tl))
        self.dint("sp_rx", (D, tl), BF16)
        self.dint("sp_g", (D, tl), BF16)
        self.dint("h0d", (D, tl + 1), BF16)
        self.dint("rd", (D, tl), BF16)
        self.dint("kd", (D, tl), BF16)
        self.dint("vd", (D, tl), BF16)
        self.dint("ldd", (D, tl))
        self.dint("omd", (D, tl), BF16)
        self.dint("h2d", (D, tl), BF16)
        self.dint("hidd", (4096, tl), BF16)
        self.dint("xtd", (D, tl), BF16)
        self.dint("gted", (16, tl))
        self.dint("xsmd", (2048, tl), BF16)
        self.dint("resd", (2048, tl), BF16)
        self.dint("dtd", (2048, tl), BF16)
        self.dint("bcd", (32, tl))
        self.dint("yd", (2048, tl), BF16)
        self.dint("yfd", (2048, tl), BF16)
        self.dint("cc_s_in", (128, 1024))
        self.dint("cc_s_out", (256, 1024))
        self.dint("cc_h_in", (128, 256))
        self.dint("cc_h_out", (256, 256))
        self.dint("cc_t_in", (128, 48))
        self.dint("cc_t_out", (256, 48))
        for g in (0, 1):
            self.dint(f"xs{g}", (XS_ROWS, D), BF16)
            self.dint(f"ys{g}", (SLOTS, D), BF16)
        self.dout("out", (tl, D))
        for name in self.dbg:
            self.dout("dbg_" + name, (D, tl))

    def alloc(self, ctx):
        nc = self.nc

        def SB(name, shape, dt=F32):
            self.sb[name] = ctx.enter_context(nc.sbuf_tensor("sb_" + name, list(shape), dt))

        for i in range(8):
            self.psum.append(ctx.enter_context(
                nc.psum_tensor(f"pb{i}", [128, 512], F32)))
        SB("FAR", (128, FA_END))
        SB("BAR", (128, 24576), BF16)
        SB("WRES", (128, 24576), BF16)
        SB("Scab", (128, 1024), BF16)
        SB("wB", (64, D), BF16)
        for n in ("c_ident", "c_headrep", "c_gnrep", "c_tril"):
            SB(n, (128, 128), BF16)
        SB("c_identf", (128, 128))
        SB("c_erow32", (128, 32))
        SB("c_onesrowf", (1, 128))
        SB("c_onescolf", (128, 1))
        SB("c_onescolb", (128, 1), BF16)
        SB("c_gate", (128, 1))
        SB("onesF", (128, 512))
        for n in ("ln1w", "mu_r", "mu_k", "mu_v", "mu_w", "mu_g", "w0", "u",
                  "gnw", "gnb", "ln2w"):
            SB("v_" + n, (128, 8))
        SB("v_wconv", (128, 64))
        SB("v_bconv", (128, 16))
        SB("v_bdt", (128, 16))
        SB("v_D", (128, 16))
        SB("v_A", (128, 256))
        SB("dec", (128, 8 * 32))
        SB("kT", (16, 1024), BF16)
        SB("vT", (16, 1024), BF16)
        SB("Sa", (128, 1024))
        SB("Sb", (128, 1024))
        SB("Lcar", (128, 8))
        SB("Ltl", (128, 128))
        SB("row0", (1, 513))
        SB("row1", (1, 513))
        SB("hst", (128, 256))
        SB("xpt", (128, 16 * 3), BF16)
        SB("prb", (128, 32))
        SB("pr2", (128, 32))
        SB("oh", (128, 32))
        SB("rk", (128, 32))
        SB("m8", (128, 8))
        SB("ng", (128, 1))
        SB("crow", (1, 32))
        SB("slot", (128, 256), U32)
        SB("wgt", (128, 256))
        SB("tmaj", (128, 1024), BF16)
        SB("ytok", (128, 1024))
        SB("gath", (128, 1024), BF16)
        SB("brep", (128, 512), BF16)
        SB("lgt", (32, 512))
        SB("lgt2", (32, 512))
        SB("grow", (1, 512))
        SB("bcs", (32, 512))
        SB("xtl", (128, 48))
        SB("hcar", (128, 256))
        SB("crow2", (1, 32))
        SB("ohb", (128, 32), BF16)
        SB("idxc", (128, 4))
        SB("rkc", (128, 4))
        SB("bsc", (128, 4))
        SB("ovm", (128, 1))
        SB("ovmu", (128, 1), U32)
        SB("zz", (128, 1))
        SB("slotc", (128, 256), U32)
        SB("slotF", (128, 4))
        SB("fdum", (1, 4))
        SB("bt0", (128, 512))

    def fv(self, off, n):
        return self.sb["FAR"][:, off:off + n]

    def bv(self, off, n):
        return self.sb["BAR"][:, off:off + n]

    def xslv(self, j, a=0, b=513):
        return self.fv(FA_XSL + 513 * j + a, b - a)

    def obv(self, j, a=0, b=512):
        return self.fv(FA_OB + 512 * j + a, b - a)

    def accv(self, j, a=0, b=512):
        return self.fv(FA_ACC + 512 * j + a, b - a)

    def lbv(self, j, a=0, b=528):
        return self.fv(FA_LB + 528 * j + a, b - a)

    def scf(self, i, a=0, b=512):
        return self.fv(FA_SCF + 512 * i + a, b - a)

    def bank(self):
        b = self.psum[self._pb % 8]
        key = f"pb{self._pb % 8}"
        self._pb += 1
        return b, key

    def mm(self, p, out_view, out_key, lhs_view, lhs_key, rhs_view, rhs_key,
           start=True, stop=True, inc=True):
        p.op("T", lambda e, o=out_view, l=lhs_view, r=rhs_view, st=start, sp=stop:
             e.matmul(o, l, r, start=st, stop=sp),
             reads=(lhs_key, rhs_key), writes=(out_key,), inc=inc)

    def trans(self, p, out_view, out_key, in_view, in_key, f32=False):
        idn = self.sb["c_identf"] if f32 else self.sb["c_ident"]
        ik = "c_identf" if f32 else "c_ident"
        np_ = in_view.shape[0]
        p.op("T", lambda e, o=out_view, i=in_view, d=idn, np_=np_:
             e.transpose(o, i, d[0:np_, 0:np_]),
             reads=(in_key, ik), writes=(out_key,))

    def ldvec(self, p):
        for n in ("c_ident", "c_headrep", "c_gnrep", "c_tril", "c_identf",
                  "c_erow32", "c_onesrowf", "c_onescolf", "c_onescolb", "c_gate"):
            p.dma("Q", lambda e, d=self.sb[n][:], s=self.ap[n]: e.dma_start(out=d, in_=s),
                  writes=(n,))
        for n in ("ln1w", "mu_r", "mu_k", "mu_v", "mu_w", "mu_g", "w0", "u",
                  "gnw", "gnb", "ln2w", "wconv", "bconv", "bdt", "D", "A"):
            p.dma("Q", lambda e, d=self.sb["v_" + n][:], s=self.ap["c_" + n]: e.dma_start(out=d, in_=s),
                  writes=("v_" + n,))
        p.op("V", lambda e: e.memset(self.sb["onesF"][:], 1.0), writes=("onesF",))

    def wload(self, p, dst, dkey, src_ap):
        p.dma("G", lambda e, d=dst, s=src_ap: e.dma_start(out=d, in_=s),
              writes=(dkey,))

    def dbg1(self, p, name, view, key, j, t0, n=512, bf=False):
        if name in self.dbg:
            dst = self.ap["dbg_" + name]
            q = "G" if bf else "Q"
            p.dma(q, lambda e, v=view: e.dma_start(
                out=dst[128 * j:128 * (j + 1), t0:t0 + n], in_=v),
                reads=(key,))

    def colsum(self, p, views, keys, F, bank, bkey, square=None):
        ones = self.sb["c_onescolf"]
        for j in range(NT):
            src, skey = views[j], keys[j]
            if square is not None:
                sv, sk = square
                p.op("S", lambda e, d=sv[:, 0:F] if sv.shape[1] >= F else sv, s=src:
                     e.activation(d, s, AF.Square),
                     reads=(skey,), writes=(sk,))
                src, skey = sv[:, 0:F], sk
            for off, n in cchunks(F):
                self.mm(p, bank[0:1, off:off + n], bkey,
                        ones[:, 0:1], "c_onescolf",
                        src[:, off:off + n], skey,
                        start=(j == 0), stop=(j == NT - 1), inc=True)

    def rstd_from(self, p, bankA, bkA, bankB, bkB, F, eps, rstd, rk, mu, mk):
        inv = 1.0 / D
        if mu is not None:
            p.op("V", lambda e: e.tensor_scalar(
                mu[0:1, 0:F], bankA[0:1, 0:F], inv, None, OP.mult),
                reads=(bkA,), writes=(mk,))
            p.op("V", lambda e: e.tensor_tensor(
                rstd[0:1, 0:F], mu[0:1, 0:F], mu[0:1, 0:F], OP.mult),
                reads=(mk,), writes=(rk,))
            p.op("V", lambda e: e.scalar_tensor_tensor(
                rstd[0:1, 0:F], bankB[0:1, 0:F], inv, rstd[0:1, 0:F],
                OP.mult, OP.subtract),
                reads=(bkB, rk), writes=(rk,))
        else:
            p.op("V", lambda e: e.tensor_scalar(
                rstd[0:1, 0:F], bankB[0:1, 0:F], inv, None, OP.mult),
                reads=(bkB,), writes=(rk,))
        p.op("V", lambda e: e.tensor_scalar(
            rstd[0:1, 0:F], rstd[0:1, 0:F], float(eps), None, OP.add),
            reads=(rk,), writes=(rk,))
        p.op("V", lambda e: e.reciprocal(rstd[0:1, 0:F], rstd[0:1, 0:F]),
             reads=(rk,), writes=(rk,))
        p.op("S", lambda e: e.activation(rstd[0:1, 0:F], rstd[0:1, 0:F], AF.Sqrt),
             reads=(rk,), writes=(rk,))

    def repl(self, p, row, rk, F, bank, bkey):
        ones = self.sb["c_onesrowf"]
        for off, n in cchunks(F):
            self.mm(p, bank[:, off:off + n], bkey, ones[0:1, 0:128], "c_onesrowf",
                    row[0:1, off:off + n], rk, start=True, stop=True)

    ARENA_KEYS = tuple(
        ["xsl", "ob", "acc", "Lb", "scF0", "scF1", "scF2", "scF3", "yac", "xx",
         "h0", "xm", "rb", "kb", "vb", "ab", "khb", "sbf0", "sbf1", "sbf2",
         "sbf3", "hid", "xssm", "res", "beta", "hsc", "WR", "WR2"]
        + [f"xsl{j}" for j in range(8)] + [f"ob{j}" for j in range(8)]
        + [f"acc{j}" for j in range(8)] + [f"Lb{j}" for j in range(8)]
        + [f"h0{j}" for j in range(8)] + [f"rb{j}" for j in range(8)]
        + [f"kb{j}" for j in range(8)] + [f"vb{j}" for j in range(8)]
        + [f"ab{j}" for j in range(8)] + [f"khb{j}" for j in range(8)]
        + [f"xm{j}" for j in range(8)] + [f"hid{j}" for j in range(32)]
        + [f"xssm{j}" for j in range(16)] + [f"res{j}" for j in range(16)]
        + [f"dt{j}" for j in range(16)] + [f"ysl{j}" for j in range(16)]
        + ["pr8", "bt0", "bt1"])

    def fence(self):
        self.p.op("V", lambda e: e.memset(self.sb["fdum"][0:1, 0:1], 0.0),
                  writes=self.ARENA_KEYS)


    def l0a_A1(self, p):
        self._a1_block(p, 0, 1)
        for s in range(self.ns):
            self._a1_block(p, 1 + TS * s, 512)

    def _a1_block(self, p, c0, W):
        sb, ap_ = self.sb, self.ap
        xk = [f"xsl{j}" for j in range(NT)]

        def xv(j):
            return self.fv(FA_XSL + 513 * j, W)

        slow = (W == 1)
        for j in range(NT):
            p.dma("Q", lambda e, j=j: e.dma_start(
                out=xv(j), in_=ap_["xdm"][128 * j:128 * (j + 1), c0:c0 + W],
                allow_slow_non_contiguous=slow),
                reads=("xdm@d",), writes=(xk[j],))
        xvs = [xv(j) for j in range(NT)]
        sq = (self.fv(FA_ACC, W), "acc")
        bA, kA = self.bank()
        bB, kB = self.bank()
        self.colsum(p, xvs, xk, W, bA, kA)
        self.colsum(p, xvs, xk, W, bB, kB, square=sq)
        self.rstd_from(p, bA, kA, bB, kB, W, EPS_LN,
                       sb["row0"], "row0", sb["row1"], "row1")
        bM, kM = self.bank()
        bR, kR = self.bank()
        self.repl(p, sb["row1"], "row1", W, bM, kM)
        self.repl(p, sb["row0"], "row0", W, bR, kR)
        p.op("S", lambda e, bM=bM, W=W: e.copy(self.scf(0, 0, W), bM[:, 0:W]),
             reads=(kM,), writes=("scF0",))
        p.op("S", lambda e, bR=bR, W=W: e.copy(self.scf(1, 0, W), bR[:, 0:W]),
             reads=(kR,), writes=("scF1",))
        for j in range(NT):
            p.op("V", lambda e, j=j, W=W: e.tensor_tensor(
                xv(j), xv(j), self.scf(0, 0, W), OP.subtract),
                reads=(xk[j], "scF0"), writes=(xk[j],))
            p.op("V", lambda e, j=j, W=W: e.tensor_tensor(
                xv(j), xv(j), self.scf(1, 0, W), OP.mult),
                reads=(xk[j], "scF1"), writes=(xk[j],))
        bC, kC = self.bank()
        self.colsum(p, xvs, xk, W, bC, kC, square=sq)
        self.rstd_from(p, None, None, bC, kC, W, EPS_RMS,
                       sb["row0"], "row0", None, None)
        bR2, kR2 = self.bank()
        self.repl(p, sb["row0"], "row0", W, bR2, kR2)
        p.op("S", lambda e, bR2=bR2, W=W: e.copy(self.scf(0, 0, W), bR2[:, 0:W]),
             reads=(kR2,), writes=("scF0",))
        for j in range(NT):
            h0j = self.bv(513 * j, W)
            p.op("V", lambda e, j=j, h=h0j, W=W: e.scalar_tensor_tensor(
                h, xv(j), 1.0, self.scf(0, 0, W), OP.mult, OP.mult),
                reads=(xk[j], "scF0"), writes=(f"h0{j}",))
            p.op("V", lambda e, j=j, h=h0j: e.tensor_scalar(
                h, h, sb["v_ln1w"][:, j:j + 1], None, OP.mult),
                reads=(f"h0{j}", "v_ln1w"), writes=(f"h0{j}",))
            p.dma("G", lambda e, j=j, h=h0j: e.dma_start(
                out=ap_["h0d"][128 * j:128 * (j + 1), c0:c0 + W], in_=h,
                allow_slow_non_contiguous=slow),
                reads=(f"h0{j}",), writes=("h0d@d",))
            if W == 512:
                self.dbg1(p, "h0", h0j, f"h0{j}", j, c0 - 1, bf=True)

    def l0a_A2(self, p):
        sb, ap_ = self.sb, self.ap
        WR = self.bv(0, 8 * 1024)
        for pi, (mn, wname, dst, act) in enumerate([
                ("r", "w_Wr", "rd", None), ("k", "w_Wk", "kd", None),
                ("v", "w_Wv", "vd", None), ("g", "w_Wg", "sp_g", AF.Silu)]):
            self.wload(p, WR, "WR", ap_[wname])
            for s in range(self.ns):
                t0 = TS * s
                self._mix_load(p, s, mn)
                for m in range(NT):
                    bw, kw = self.bank()
                    for k in range(NT):
                        self.mm(p, bw[:, 0:512], kw,
                                WR[:, 1024 * k + 128 * m:1024 * k + 128 * (m + 1)],
                                "WR", self.bv(12808 + 512 * k, 512), f"xm{k}",
                                start=(k == 0), stop=(k == NT - 1), inc=(k == NT - 1))
                    sc = self.bv(16904, 512)
                    if act is None:
                        p.op("S", lambda e, d=sc, bw=bw: e.copy(d, bw[:, 0:512]),
                             reads=(kw,), writes=("sbf0",))
                    else:
                        p.op("S", lambda e, d=sc, bw=bw, act=act: e.activation(d, bw[:, 0:512], act),
                             reads=(kw,), writes=("sbf0",))
                    p.dma("G", lambda e, m=m, t0=t0, d=sc, dst=dst: e.dma_start(
                        out=ap_[dst][128 * m:128 * (m + 1), t0:t0 + 512], in_=d),
                        reads=("sbf0",), writes=(dst + "@d",))
        self.wload(p, WR[:, 0:512], "WR", ap_["w_wA"])
        self.wload(p, sb["wB"][:], "wB", ap_["w_wB"])
        for s in range(self.ns):
            t0 = TS * s
            self._mix_load(p, s, "w")
            bw, kw = self.bank()
            for k in range(NT):
                self.mm(p, bw[0:64, 0:512], kw, WR[:, 64 * k:64 * (k + 1)], "WR",
                        self.bv(12808 + 512 * k, 512), f"xm{k}",
                        start=(k == 0), stop=(k == NT - 1), inc=(k == NT - 1))
            th = self.sb["BAR"][0:64, 17416:17416 + 512]
            p.op("S", lambda e, d=th, bw=bw: e.activation(d, bw[0:64, 0:512], AF.Tanh),
                 reads=(kw,), writes=("sbf0",))
            for m in range(NT):
                bw2, kw2 = self.bank()
                self.mm(p, bw2[:, 0:512], kw2, sb["wB"][:, 128 * m:128 * (m + 1)],
                        "wB", self.sb["BAR"][0:64, 17416:17416 + 512], "sbf0")
                p.op("S", lambda e, m=m, bw2=bw2: e.activation(
                    self.scf(0), bw2[:, 0:512], AF.Exp,
                    bias=sb["v_w0"][:, m:m + 1], scale=1.0),
                    reads=(kw2, "v_w0"), writes=("scF0",))
                p.op("V", lambda e: e.tensor_scalar(
                    self.scf(0), self.scf(0), -1.0, None, OP.mult),
                    reads=("scF0",), writes=("scF0",))
                p.dma("Q", lambda e, m=m, t0=t0: e.dma_start(
                    out=ap_["ldd"][128 * m:128 * (m + 1), t0:t0 + 512],
                    in_=self.scf(0)), reads=("scF0",), writes=("ldd@d",))
                self.dbg1(p, "ld", self.scf(0), "scF0", m, t0)

    def _mix_load(self, p, s, mn):
        sb, ap_ = self.sb, self.ap
        t0 = TS * s
        mu = sb["v_mu_" + mn]
        for j in range(NT):
            h = self.bv(8192 + 513 * j, 513)
            p.dma("Q", lambda e, j=j, h=h, t0=t0: e.dma_start(
                out=h, in_=ap_["h0d"][128 * j:128 * (j + 1), t0:t0 + 513]),
                reads=("h0d@d",), writes=(f"h0{j}",))
            xx = self.bv(12296, 512)
            p.op("V", lambda e, j=j, h=h, xx=xx: e.tensor_tensor(
                xx, h[:, 0:512], h[:, 1:513], OP.subtract),
                reads=(f"h0{j}",), writes=("xx",))
            p.op("V", lambda e, j=j, h=h, xx=xx, mu=mu: e.scalar_tensor_tensor(
                self.bv(12808 + 512 * j, 512), xx, mu[:, j:j + 1], h[:, 1:513],
                OP.mult, OP.add),
                reads=("xx", "v_mu_" + mn, f"h0{j}"), writes=(f"xm{j}",))

    BSB = 20864

    def bsb(self, i, n=512):
        return self.bv(self.BSB + 512 * i, n)

    def l0a_A3(self, p):
        sb, ap_ = self.sb, self.ap
        p.op("V", lambda e: e.memset(sb["Lcar"][:], 0.0), writes=("Lcar",))
        p.op("V", lambda e: e.memset(sb["Sa"][:], 0.0), writes=("Sa",))
        p.op("V", lambda e: e.memset(sb["Sb"][:], 0.0), writes=("Sb",))
        for s in range(self.ns):
            t0 = TS * s
            for j in range(NT):
                p.dma("Q", lambda e, j=j, t0=t0: e.dma_start(
                    out=self.bv(528 * j, 512),
                    in_=ap_["rd"][128 * j:128 * (j + 1), t0:t0 + 512]),
                    reads=("rd@d",), writes=(f"rb{j}",))
                p.dma("Q", lambda e, j=j, t0=t0: e.dma_start(
                    out=self.bv(4224 + 528 * j + 16, 512),
                    in_=ap_["kd"][128 * j:128 * (j + 1), t0:t0 + 512]),
                    reads=("kd@d",), writes=(f"kb{j}",))
                p.dma("Q", lambda e, j=j, t0=t0: e.dma_start(
                    out=self.bv(8448 + 528 * j + 16, 512),
                    in_=ap_["vd"][128 * j:128 * (j + 1), t0:t0 + 512]),
                    reads=("vd@d",), writes=(f"vb{j}",))
                if s == 0:
                    p.op("V", lambda e, j=j: e.memset(self.bv(4224 + 528 * j, 16), 0.0),
                         writes=(f"kb{j}",))
                    p.op("V", lambda e, j=j: e.memset(self.bv(8448 + 528 * j, 16), 0.0),
                         writes=(f"vb{j}",))
                    p.op("V", lambda e, j=j: e.memset(self.lbv(j, 0, 16), 0.0),
                         writes=(f"Lb{j}",))
                else:
                    p.dma("G", lambda e, j=j, t0=t0: e.dma_start(
                        out=self.bv(4224 + 528 * j, 16),
                        in_=ap_["kd"][128 * j:128 * (j + 1), t0 - 16:t0]),
                        reads=("kd@d",), writes=(f"kb{j}",))
                    p.dma("G", lambda e, j=j, t0=t0: e.dma_start(
                        out=self.bv(8448 + 528 * j, 16),
                        in_=ap_["vd"][128 * j:128 * (j + 1), t0 - 16:t0]),
                        reads=("vd@d",), writes=(f"vb{j}",))
                p.dma("Q", lambda e, j=j, t0=t0: e.dma_start(
                    out=self.scf(0), in_=ap_["ldd"][128 * j:128 * (j + 1), t0:t0 + 512]),
                    reads=("ldd@d",), writes=("scF0",))
                Lsl = self.lbv(j, 16, 528)
                p.op("V", lambda e, j=j, L=Lsl: e.tensor_tensor_scan(
                    L, sb["onesF"][:], self.scf(0), sb["Lcar"][:, j:j + 1],
                    OP.mult, OP.add),
                    reads=("onesF", "scF0", "Lcar"), writes=(f"Lb{j}",))
                p.op("V", lambda e, j=j, L=Lsl: e.tensor_copy(
                    sb["Lcar"][:, j:j + 1], L[:, 511:512]),
                    reads=(f"Lb{j}",), writes=("Lcar",))
                if s > 0:
                    p.op("V", lambda e, j=j: e.tensor_copy(
                        self.lbv(j, 0, 16), sb["Ltl"][:, 16 * j:16 * (j + 1)]),
                        reads=("Ltl",), writes=(f"Lb{j}",))
            self.dbg_L(p, t0)
            for j in range(NT):
                p.op("V", lambda e, j=j: e.memset(self.obv(j), 0.0),
                     writes=(f"ob{j}",))
            KHK = tuple(f"khb{j}" for j in range(NT))
            r8 = lambda off, n=512: self.sb["BAR"][:, 0:4224].rearrange(
                "p (t c) -> p t c", c=528)[:, :, off:off + n]
            k8 = lambda off, n=512: self.sb["BAR"][:, 4224:8448].rearrange(
                "p (t c) -> p t c", c=528)[:, :, off:off + n]
            L8 = lambda half, off, n=512: self.sb["FAR"][:, FA_LB + 2112 * half:FA_LB + 2112 * (half + 1)].rearrange(
                "p (t c) -> p t c", c=528)[:, :, off:off + n]
            E8 = self.bv(16768, 4096)
            P8 = self.bv(12672, 4096)
            LBK = tuple(f"Lb{j}" for j in range(NT))
            RBK = tuple(f"rb{j}" for j in range(NT))
            KBK = tuple(f"kb{j}" for j in range(NT))
            for dlt in range(CHUNK):
                if dlt == 0:
                    p.op("V", lambda e: e.tensor_tensor(
                        P8.rearrange("p (t c) -> p t c", c=512), r8(0), k8(16),
                        OP.mult), reads=RBK + KBK, writes=("pr8",))
                    for j in range(NT):
                        p.op("V", lambda e, j=j: e.tensor_scalar(
                            self.bv(12672 + 512 * j, 512), self.bv(12672 + 512 * j, 512),
                            sb["v_u"][:, j:j + 1], None, OP.mult),
                            reads=("pr8", "v_u"), writes=("pr8",))
                else:
                    for half in range(2):
                        p.op("G", lambda e, half=half, dlt=dlt: e.tensor_tensor(
                            self.fv(FA_SCF, 2048).rearrange("p (t c) -> p t c", c=512),
                            L8(half, 15, 512), L8(half, 16 - dlt, 512), OP.subtract),
                            reads=LBK, writes=("scF0", "scF1", "scF2", "scF3"))
                        p.op("S", lambda e, half=half: e.activation(
                            E8[:, 2048 * half:2048 * (half + 1)],
                            self.fv(FA_SCF, 2048), AF.Exp),
                            reads=("scF0", "scF1", "scF2", "scF3"), writes=KHK)
                    p.op("V", lambda e, dlt=dlt: e.memset(
                        E8.rearrange("p (tc q) -> p tc q", q=16)[:, :, 0:dlt],
                        0.0), writes=KHK)
                    p.op("V", lambda e, dlt=dlt: e.tensor_tensor(
                        P8.rearrange("p (t c) -> p t c", c=512), r8(0), k8(16 - dlt),
                        OP.mult), reads=RBK + KBK, writes=("pr8",))
                    p.op("V", lambda e: e.tensor_tensor(P8, P8, E8, OP.mult),
                         reads=KHK + ("pr8",), writes=("pr8",))
                for j in range(NT):
                    vv = self.bv(8448 + 528 * j + 16 - dlt, 512)
                    bk, bkey = self.bank()
                    self.mm(p, bk[:, 0:512], bkey, sb["c_headrep"][:], "c_headrep",
                            self.bv(12672 + 512 * j, 512), "pr8")
                    tj = "bt0" if j % 2 else "yac"
                    tv = sb["bt0"][:, 0:512] if j % 2 else self.fv(FA_YAC, 512)
                    p.op("V", lambda e, vv=vv, bk=bk, tv=tv: e.tensor_tensor(
                        tv, bk[:, 0:512], vv, OP.mult),
                        reads=(bkey, f"vb{j}"), writes=(tj,))
                    p.op("V", lambda e, j=j, tv=tv: e.tensor_tensor(
                        self.obv(j), self.obv(j), tv, OP.add),
                        reads=(tj, f"ob{j}"), writes=(f"ob{j}",))
            for j in range(NT):
                Lsl = self.lbv(j, 16, 528)
                Lm1 = self.lbv(j, 15, 527)
                for q in range(CHUNK):
                    p.op("V", lambda e, j=j, q=q: e.tensor_copy(
                        self.scf(0)[:, q:512:16], self.lbv(j, 31, 528)[:, 0:497:16]),
                        reads=(f"Lb{j}",), writes=("scF0",), inc=(q == CHUNK - 1))
                    p.op("V", lambda e, j=j, q=q: e.tensor_copy(
                        self.scf(1)[:, q:512:16], self.lbv(j, 15, 512)[:, 0:497:16]),
                        reads=(f"Lb{j}",), writes=("scF1",), inc=(q == CHUNK - 1))
                p.op("G", lambda e, j=j, L=Lsl: e.tensor_tensor(
                    self.scf(2), self.scf(0), L, OP.subtract),
                    reads=("scF0", f"Lb{j}"), writes=("scF2",))
                p.op("S", lambda e: e.activation(self.bsb(0), self.scf(2), AF.Exp),
                     reads=("scF2",), writes=("sbf0",))
                p.op("V", lambda e, j=j: e.tensor_tensor(
                    self.bv(16768 + 512 * j, 512),
                    self.bv(4224 + 528 * j + 16, 512), self.bsb(0), OP.mult),
                    reads=("sbf0", f"kb{j}"), writes=(f"khb{j}",))
                p.op("G", lambda e, j=j, L=Lm1: e.tensor_tensor(
                    self.scf(2), L, self.scf(1), OP.subtract),
                    reads=("scF1", f"Lb{j}"), writes=("scF2",))
                p.op("S", lambda e: e.activation(self.bsb(1), self.scf(2), AF.Exp),
                     reads=("scF2",), writes=("sbf1",))
                p.op("V", lambda e, j=j: e.tensor_tensor(
                    self.fv(FA_AB + 512 * j, 512),
                    self.bv(528 * j, 512), self.bsb(1), OP.mult),
                    reads=("sbf1", f"rb{j}"), writes=(f"ab{j}",))
                p.op("S", lambda e, j=j, L=Lm1: e.activation(
                    self.bsb(2), L, AF.Exp), reads=(f"Lb{j}",), writes=("sbf2",))
                p.op("V", lambda e, j=j: e.tensor_tensor(
                    self.bsb(3), self.bv(528 * j, 512), self.bsb(2), OP.mult),
                    reads=("sbf2", f"rb{j}"), writes=("sbf3",))
                p.dma("G", lambda e, j=j, t0=t0: e.dma_start(
                    out=ap_["sp_rx"][128 * j:128 * (j + 1), t0:t0 + 512],
                    in_=self.bsb(3)), reads=("sbf3",), writes=("sp_rx@d",))
                p.op("V", lambda e, j=j: e.tensor_tensor(
                    sb["dec"][:, 32 * j:32 * (j + 1)],
                    self.lbv(j, 31, 528)[:, 0:497:16],
                    self.lbv(j, 15, 512)[:, 0:497:16], OP.subtract),
                    reads=(f"Lb{j}",), writes=("dec",))
            p.op("S", lambda e: e.activation(sb["dec"][:], sb["dec"][:], AF.Exp),
                 reads=("dec",), writes=("dec",))
            for j in range(NT):
                p.op("V", lambda e, j=j: e.tensor_copy(
                    sb["Ltl"][:, 16 * j:16 * (j + 1)], self.lbv(j, 512, 528)),
                    reads=(f"Lb{j}",), writes=("Ltl",))
            for c in range(NCH):
                gi = s * NCH + c
                cur, new = ("Sa", "Sb") if gi % 2 == 0 else ("Sb", "Sa")
                bT1, kT1b = self.bank()
                bT2, kT2b = self.bank()
                pt1, pt2 = bT1.bitcast(BF16), bT2.bitcast(BF16)
                for j in range(NT):
                    self.trans(p, pt1[0:16, 128 * j:128 * (j + 1)], kT1b,
                               self.bv(16768 + 512 * j + 16 * c, 16), f"khb{j}")
                    self.trans(p, pt2[0:16, 128 * j:128 * (j + 1)], kT2b,
                               self.bv(8448 + 528 * j + 16 + 16 * c, 16), f"vb{j}")
                p.op("S", lambda e, pt1=pt1: e.copy(sb["kT"][:], pt1[0:16, 0:1024]),
                     reads=(kT1b,), writes=("kT",))
                p.op("S", lambda e, pt2=pt2: e.copy(sb["vT"][:], pt2[0:16, 0:1024]),
                     reads=(kT2b,), writes=("vT",))
                bS1, kS1 = self.bank()
                bS2, kS2 = self.bank()
                bO, kO = self.bank()
                for pj in range(8):
                    bS = bS1 if pj < 4 else bS2
                    kS = kS1 if pj < 4 else kS2
                    self.mm(p, bS[:, 128 * (pj % 4):128 * (pj % 4) + 128], kS,
                            sb["kT"][0:16, 128 * pj:128 * (pj + 1)], "kT",
                            sb["vT"][0:16, 128 * pj:128 * (pj + 1)], "vT",
                            start=True, stop=True, inc=(pj % 4 == 3))
                for pj in range(8):
                    self.mm(p, bO[:, 16 * pj:16 * (pj + 1)], kO,
                            sb[cur][:, 128 * pj:128 * (pj + 1)], cur,
                            self.fv(FA_AB + 512 * pj + 16 * c, 16), f"ab{pj}",
                            start=True, stop=True, inc=(pj == 7))
                for j in range(NT):
                    p.op("V", lambda e, j=j, c=c, bO=bO: e.tensor_tensor(
                        self.obv(j, 16 * c, 16 * c + 16),
                        self.obv(j, 16 * c, 16 * c + 16),
                        bO[:, 16 * j:16 * (j + 1)], OP.add),
                        reads=(kO, f"ob{j}"), writes=(f"ob{j}",))
                for h in range(16):
                    pj, hp = h // 2, h % 2
                    bS = bS1 if pj < 4 else bS2
                    kS = kS1 if pj < 4 else kS2
                    p.op("V", lambda e, pj=pj, hp=hp, cur=cur, new=new, c=c, bS=bS:
                         e.scalar_tensor_tensor(
                             sb[new][64 * hp:64 * hp + 64,
                                     128 * pj + 64 * hp:128 * pj + 64 * hp + 64],
                             sb[cur][64 * hp:64 * hp + 64,
                                     128 * pj + 64 * hp:128 * pj + 64 * hp + 64],
                             sb["dec"][64 * hp:64 * hp + 64, 32 * pj + c:32 * pj + c + 1],
                             bS[64 * hp:64 * hp + 64,
                                128 * (pj % 4) + 64 * hp:128 * (pj % 4) + 64 * hp + 64],
                             OP.mult, OP.add),
                         reads=(cur, "dec", kS), writes=(new,))
            for j in range(NT):
                p.dma("Q", lambda e, j=j, t0=t0: e.dma_start(
                    out=ap_["sp_o"][128 * j:128 * (j + 1), t0:t0 + 512],
                    in_=self.obv(j)), reads=(f"ob{j}",), writes=("sp_o@d",))
                self.dbg1(p, "o1", self.obv(j), f"ob{j}", j, t0)
        self.S_fin = "Sa" if (self.ns * NCH) % 2 == 0 else "Sb"

    def dbg_L(self, p, t0):
        if "L" in self.dbg:
            for j in range(NT):
                p.dma("Q", lambda e, j=j: e.dma_start(
                    out=self.ap["dbg_L"][128 * j:128 * (j + 1), t0:t0 + 512],
                    in_=self.lbv(j, 16, 528)), reads=(f"Lb{j}",))

    def l0a_AGS(self, p):
        sb, ap_ = self.sb, self.ap
        if getattr(self, "nocc", False):
            p.op("V", lambda e: e.memset(sb["Scab"][:], 0.0), writes=("Scab",))
            return
        p.dma("Q", lambda e: e.dma_start(out=ap_["cc_s_in"], in_=sb[self.S_fin][:]),
              reads=(self.S_fin,), writes=("cc_s_in@d",))
        p.cc(lambda e: e.collective_compute(
            "AllGather", OP.bypass, replica_groups=[[0, 1], [2, 3], [4, 5], [6, 7]],
            ins=[ap_["cc_s_in"].opt()], outs=[ap_["cc_s_out"].opt()]),
            reads=("cc_s_in@d",), writes=("cc_s_out@d",))
        p.dma("Q", lambda e: e.dma_start(out=sb["Sb"][:], in_=ap_["cc_s_out"][0:128, :]),
              reads=("cc_s_out@d",), writes=("Sb",))
        p.op("V", lambda e: e.tensor_scalar(
            sb["Sb"][:], sb["Sb"][:], sb["c_gate"][:, 0:1], None, OP.mult),
            reads=("Sb", "c_gate"), writes=("Scar",))
        p.op("V", lambda e: e.tensor_copy(sb["Scab"][:], sb["Sb"][:]),
             reads=("Scar",), writes=("Scab",))

    def l0a_A4(self, p):
        sb, ap_ = self.sb, self.ap
        for s in range(self.ns):
            t0 = TS * s
            for j in range(NT):
                p.dma("Q", lambda e, j=j, t0=t0: e.dma_start(
                    out=self.obv(j), in_=ap_["sp_o"][128 * j:128 * (j + 1), t0:t0 + 512]),
                    reads=("sp_o@d",), writes=(f"ob{j}",))
                p.dma("Q", lambda e, j=j, t0=t0: e.dma_start(
                    out=self.bv(512 * j, 512),
                    in_=ap_["sp_rx"][128 * j:128 * (j + 1), t0:t0 + 512]),
                    reads=("sp_rx@d",), writes=(f"rb{j}",))
                p.dma("Q", lambda e, j=j, t0=t0: e.dma_start(
                    out=self.bv(4096 + 512 * j, 512),
                    in_=ap_["sp_g"][128 * j:128 * (j + 1), t0:t0 + 512]),
                    reads=("sp_g@d",), writes=(f"kb{j}",))
            for j in range(NT):
                bc, kc = self.bank()
                self.mm(p, bc[:, 0:512], kc, sb["Scab"][:, 128 * j:128 * (j + 1)],
                        "Scab", self.bv(512 * j, 512), f"rb{j}")
                p.op("V", lambda e, j=j, bc=bc: e.tensor_tensor(
                    self.obv(j), self.obv(j), bc[:, 0:512], OP.add),
                    reads=(kc, f"ob{j}"), writes=(f"ob{j}",))
                p.op("S", lambda e, j=j: e.copy(self.bsb(0), self.obv(j)),
                     reads=(f"ob{j}",), writes=("sbf0",))
                bE, kE = self.bank()
                self.mm(p, bE[:, 0:512], kE, sb["c_gnrep"][:], "c_gnrep",
                        self.bsb(0), "sbf0")
                p.op("S", lambda e: e.activation(self.bsb(1), self.bsb(0), AF.Square),
                     reads=("sbf0",), writes=("sbf1",))
                bE2, kE2 = self.bank()
                self.mm(p, bE2[:, 0:512], kE2, sb["c_gnrep"][:], "c_gnrep",
                        self.bsb(1), "sbf1")
                p.op("S", lambda e, bE=bE: e.copy(self.scf(0), bE[:, 0:512]),
                     reads=(kE,), writes=("scF0",))
                p.op("V", lambda e: e.tensor_tensor(
                    self.scf(1), self.scf(0), self.scf(0), OP.mult),
                    reads=("scF0",), writes=("scF1",))
                p.op("V", lambda e, bE2=bE2: e.scalar_tensor_tensor(
                    self.scf(1), bE2[:, 0:512], 1.0, self.scf(1), OP.mult, OP.subtract),
                    reads=(kE2, "scF1"), writes=("scF1",))
                p.op("V", lambda e: e.tensor_scalar(
                    self.scf(1), self.scf(1), EPS_GN, None, OP.add),
                    reads=("scF1",), writes=("scF1",))
                p.op("V", lambda e: e.reciprocal(self.scf(1), self.scf(1)),
                     reads=("scF1",), writes=("scF1",))
                p.op("S", lambda e: e.activation(self.scf(1), self.scf(1), AF.Sqrt),
                     reads=("scF1",), writes=("scF1",))
                p.op("V", lambda e, j=j: e.tensor_tensor(
                    self.scf(2), self.obv(j), self.scf(0), OP.subtract),
                    reads=(f"ob{j}", "scF0"), writes=("scF2",))
                p.op("V", lambda e: e.tensor_tensor(
                    self.scf(2), self.scf(2), self.scf(1), OP.mult),
                    reads=("scF1", "scF2"), writes=("scF2",))
                p.op("V", lambda e, j=j: e.tensor_scalar(
                    self.scf(2), self.scf(2), sb["v_gnw"][:, j:j + 1],
                    sb["v_gnb"][:, j:j + 1], OP.mult, OP.add),
                    reads=("scF2", "v_gnw", "v_gnb"), writes=("scF2",))
                p.op("V", lambda e, j=j: e.tensor_tensor(
                    self.bsb(2), self.scf(2), self.bv(4096 + 512 * j, 512), OP.mult),
                    reads=("scF2", f"kb{j}"), writes=("sbf2",))
                p.dma("G", lambda e, j=j, t0=t0: e.dma_start(
                    out=ap_["omd"][128 * j:128 * (j + 1), t0:t0 + 512],
                    in_=self.bsb(2)), reads=("sbf2",), writes=("omd@d",))

    def l0a_A5(self, p):
        sb, ap_ = self.sb, self.ap
        WR = sb["WRES"][:, 0:8192]
        self.wload(p, WR, "WR", ap_["w_Wo"])
        for s in range(self.ns):
            t0 = TS * s
            for j in range(NT):
                p.dma("Q", lambda e, j=j, t0=t0: e.dma_start(
                    out=self.bv(512 * j, 512),
                    in_=ap_["omd"][128 * j:128 * (j + 1), t0:t0 + 512]),
                    reads=("omd@d",), writes=(f"xm{j}",))
                p.dma("Q", lambda e, j=j, t0=t0: e.dma_start(
                    out=self.xslv(j, 0, 512),
                    in_=ap_["xdm"][128 * j:128 * (j + 1), t0 + 1:t0 + 513]),
                    reads=("xdm@d",), writes=(f"xsl{j}",))
            if getattr(self, "a5cut", 9) < 2:
                continue
            skipln = getattr(self, "a5skipln", 0)
            xk = [f"xsl{j}" for j in range(NT)]
            xvs = [self.xslv(j, 0, 512) for j in range(NT)]
            lnlvl = getattr(self, "a5ln", 9)
            if not skipln:
                bA, kA = self.bank()
                bB, kB = self.bank()
                self.colsum(p, xvs, xk, 512, bA, kA)
                if lnlvl >= 2:
                    self.colsum(p, xvs, xk, 512, bB, kB, square=(self.scf(0), "scF0"))
                if lnlvl >= 3:
                    self.rstd_from(p, bA, kA, bB, kB, 512, EPS_LN,
                                   sb["row0"], "row0", sb["row1"], "row1")
                bM, kM = self.bank()
                bR, kR = self.bank()
                if lnlvl >= 4:
                    self.repl(p, sb["row1"], "row1", 512, bM, kM)
                    self.repl(p, sb["row0"], "row0", 512, bR, kR)
                if lnlvl == 5:
                    pass
                elif lnlvl < 5:
                    continue
                else:
                    p.op("S", lambda e, bM=bM: e.copy(self.scf(0), bM[:, 0:512]),
                         reads=(kM,), writes=("scF0",))
                    p.op("S", lambda e, bR=bR: e.copy(self.scf(1), bR[:, 0:512]),
                         reads=(kR,), writes=("scF1",))
                    for j in range(NT):
                        p.op("V", lambda e, j=j: e.tensor_tensor(
                            self.obv(j), self.xslv(j, 0, 512), self.scf(0), OP.subtract),
                            reads=(xk[j], "scF0"), writes=(f"ob{j}",))
                        p.op("V", lambda e, j=j: e.tensor_tensor(
                            self.obv(j), self.obv(j), self.scf(1), OP.mult),
                            reads=(f"ob{j}", "scF1"), writes=(f"ob{j}",))
            if getattr(self, "a5cut", 9) < 3:
                continue
            for m in range(NT):
                bw, kw = self.bank()
                for k in range(NT):
                    self.mm(p, bw[:, 0:512], kw,
                            WR[:, 1024 * k + 128 * m:1024 * k + 128 * (m + 1)], "WR",
                            self.bv(512 * k, 512), f"xm{k}",
                            start=(k == 0), stop=(k == NT - 1), inc=(k == NT - 1))
                if getattr(self, "a5mm", 0) == 1:
                    continue
                p.op("V", lambda e, m=m, bw=bw: e.tensor_tensor(
                    self.obv(m), self.obv(m), bw[:, 0:512], OP.add),
                    reads=(kw, f"ob{m}"), writes=(f"ob{m}",))
                if getattr(self, "a5mm", 0) == 2:
                    continue
                p.op("V", lambda e, m=m: e.tensor_tensor(
                    self.scf(2), self.xslv(m, 0, 512), self.obv(m), OP.add),
                    reads=(f"ob{m}", f"xsl{m}"), writes=("scF2",))
                if getattr(self, "a5mm", 0) == 3:
                    continue
                p.dma("Q", lambda e, m=m, t0=t0: e.dma_start(
                    out=ap_["r1"][128 * m:128 * (m + 1), t0:t0 + 512],
                    in_=self.scf(2)), reads=("scF2",), writes=("r1@d",))
            if getattr(self, "a5cut", 9) < 4:
                continue
            obv_ = [self.obv(j) for j in range(NT)]
            obk = [f"ob{j}" for j in range(NT)]
            bC, kC = self.bank()
            self.colsum(p, obv_, obk, 512, bC, kC, square=(self.scf(0), "scF0"))
            self.rstd_from(p, None, None, bC, kC, 512, EPS_RMS,
                           sb["row0"], "row0", None, None)
            bR2, kR2 = self.bank()
            self.repl(p, sb["row0"], "row0", 512, bR2, kR2)
            p.op("S", lambda e, bR2=bR2: e.copy(self.scf(0), bR2[:, 0:512]),
                 reads=(kR2,), writes=("scF0",))
            for j in range(NT):
                p.op("V", lambda e, j=j: e.scalar_tensor_tensor(
                    self.bsb(0), self.obv(j), 1.0, self.scf(0), OP.mult, OP.mult),
                    reads=(f"ob{j}", "scF0"), writes=("sbf0",))
                p.op("V", lambda e, j=j: e.tensor_scalar(
                    self.bsb(0), self.bsb(0), sb["v_ln2w"][:, j:j + 1], None, OP.mult),
                    reads=("sbf0", "v_ln2w"), writes=("sbf0",))
                p.dma("G", lambda e, j=j, t0=t0: e.dma_start(
                    out=ap_["h2d"][128 * j:128 * (j + 1), t0:t0 + 512],
                    in_=self.bsb(0)), reads=("sbf0",), writes=("h2d@d",))

    def l0a_A6(self, p):
        sb, ap_ = self.sb, self.ap
        WR = sb["WRES"][:, 0:16384]
        f1r = ap_["w_f1"].rearrange("p (k m) -> p k m", m=4096)
        for half in range(2):
            m0 = 16 * half
            p.dma("G", lambda e, half=half: e.dma_start(
                out=WR.rearrange("p (k m) -> p k m", m=2048),
                in_=f1r[:, :, 2048 * half:2048 * (half + 1)]),
                writes=("WR",))
            for s in range(self.ns):
                t0 = TS * s
                pb_ = 4096 * (s % 2)
                pk_ = "xm" if s % 2 == 0 else "hid"
                for j in range(NT):
                    p.dma("Q", lambda e, j=j, t0=t0, pb_=pb_: e.dma_start(
                        out=self.bv(pb_ + 512 * j, 512),
                        in_=ap_["h2d"][128 * j:128 * (j + 1), t0:t0 + 512]),
                        reads=("h2d@d",), writes=(f"{pk_}{j}",))
                for m in range(m0, m0 + 16):
                    bw, kw = self.bank()
                    for k in range(NT):
                        self.mm(p, bw[:, 0:512], kw,
                                WR[:, 2048 * k + 128 * (m - m0):2048 * k + 128 * (m - m0) + 128],
                                "WR", self.bv(pb_ + 512 * k, 512), f"{pk_}{k}",
                                start=(k == 0), stop=(k == NT - 1), inc=(k == NT - 1))
                    p.op("S", lambda e, bw=bw: e.activation(self.bsb(0), bw[:, 0:512], AF.Gelu),
                         reads=(kw,), writes=("sbf0",))
                    p.dma("G", lambda e, m=m, t0=t0: e.dma_start(
                        out=ap_["hidd"][128 * m:128 * (m + 1), t0:t0 + 512],
                        in_=self.bsb(0)), reads=("sbf0",), writes=("hidd@d",))

    def l0a_A7(self, p):
        sb, ap_ = self.sb, self.ap
        WR = sb["WRES"][:, 0:16384]
        for half in range(2):
            k0 = 16 * half
            p.dma("G", lambda e, half=half: e.dma_start(
                out=WR, in_=ap_["w_f2"][:, 16384 * half:16384 * (half + 1)]),
                writes=("WR",))
            for s in range(self.ns):
                t0 = TS * s
                pb_ = 4096 + 8192 * (s % 2)
                pk_ = "hid" if s % 2 == 0 else "xssm"
                for k in range(16):
                    p.dma("Q", lambda e, k=k, k0=k0, t0=t0, pb_=pb_: e.dma_start(
                        out=self.bv(pb_ + 512 * k, 512),
                        in_=ap_["hidd"][128 * (k0 + k):128 * (k0 + k + 1), t0:t0 + 512]),
                        reads=("hidd@d",), writes=(f"{pk_}{k}",))
                for j in range(NT):
                    p.dma("Q", lambda e, j=j, t0=t0: e.dma_start(
                        out=self.accv(j), in_=ap_["r1"][128 * j:128 * (j + 1), t0:t0 + 512]),
                        reads=("r1@d",), writes=(f"acc{j}",))
                for m in range(NT):
                    bw, kw = self.bank()
                    for k in range(16):
                        self.mm(p, bw[:, 0:512], kw,
                                WR[:, 1024 * k + 128 * m:1024 * k + 128 * (m + 1)], "WR",
                                self.bv(pb_ + 512 * k, 512), f"{pk_}{k}",
                                start=(k == 0), stop=(k == 15), inc=(k == 15))
                    p.op("V", lambda e, m=m, bw=bw: e.tensor_tensor(
                        self.accv(m), self.accv(m), bw[:, 0:512], OP.add),
                        reads=(kw, f"acc{m}"), writes=(f"acc{m}",))
                    p.dma("Q", lambda e, m=m, t0=t0: e.dma_start(
                        out=ap_["r1"][128 * m:128 * (m + 1), t0:t0 + 512],
                        in_=self.accv(m)), reads=(f"acc{m}",), writes=("r1@d",))
                    if half == 1:
                        self.dbg1(p, "x1", self.accv(m), f"acc{m}", m, t0)

    def l0f(self, p):
        sb, ap_ = self.sb, self.ap
        WR = sb["WRES"]
        self.wload(p, WR[:, 0:8192], "WR", ap_["w_sup"])
        self.wload(p, WR[:, 8192:16384], "WR", ap_["w_sdn"])
        self.wload(p, WR[:, 16384:16512], "WR", ap_["w_rtr"])
        for s in range(self.ns):
            t0 = TS * s
            xk = [f"xsl{j}" for j in range(NT)]
            for j in range(NT):
                p.dma("Q", lambda e, j=j, t0=t0: e.dma_start(
                    out=self.xslv(j, 0, 512),
                    in_=ap_["r1"][128 * j:128 * (j + 1), t0:t0 + 512]),
                    reads=("r1@d",), writes=(xk[j],))
            self._ln_to(p, xk, [self.xslv(j, 0, 512) for j in range(NT)],
                        [self.bv(512 * j, 512) for j in range(NT)],
                        [f"xm{j}" for j in range(NT)])
            for j in range(NT):
                p.dma("G", lambda e, j=j, t0=t0: e.dma_start(
                    out=ap_["xtd"][128 * j:128 * (j + 1), t0:t0 + 512],
                    in_=self.bv(512 * j, 512)),
                    reads=(f"xm{j}",), writes=("xtd@d",))
            for s2 in range(2):
                for m in range(4):
                    bw, kw = self.bank()
                    for k in range(NT):
                        self.mm(p, bw[:, 0:512], kw,
                                WR[:, 4096 * s2 + 512 * k + 128 * m:4096 * s2 + 512 * k + 128 * m + 128],
                                "WR", self.bv(512 * k, 512), f"xm{k}",
                                start=(k == 0), stop=(k == NT - 1), inc=(k == NT - 1))
                    p.op("S", lambda e, bw=bw, s2=s2, m=m: e.activation(
                        self.bv(8192 + 2048 * s2 + 512 * m, 512), bw[:, 0:512], AF.Gelu),
                        reads=(kw,), writes=("hid",))
            for mo in range(NT):
                bw, kw = self.bank()
                first = True
                for s2 in range(2):
                    for ke in range(4):
                        self.mm(p, bw[:, 0:512], kw,
                                WR[:, 8192 + 4096 * s2 + 1024 * ke + 128 * mo:8192 + 4096 * s2 + 1024 * ke + 128 * mo + 128],
                                "WR", self.bv(8192 + 2048 * s2 + 512 * ke, 512), "hid",
                                start=first, stop=(s2 == 1 and ke == 3),
                                inc=(s2 == 1 and ke == 3))
                        first = False
                p.op("V", lambda e, mo=mo, bw=bw: e.tensor_tensor(
                    self.accv(mo), self.xslv(mo, 0, 512), bw[:, 0:512], OP.add),
                    reads=(kw, xk[mo]), writes=(f"acc{mo}",))
                p.dma("Q", lambda e, mo=mo, t0=t0: e.dma_start(
                    out=ap_["r2"][128 * mo:128 * (mo + 1), t0:t0 + 512],
                    in_=self.accv(mo)), reads=(f"acc{mo}",), writes=("r2@d",))
            bw, kw = self.bank()
            for k in range(NT):
                self.mm(p, bw[0:16, 0:512], kw, WR[:, 16384 + 16 * k:16384 + 16 * (k + 1)],
                        "WR", self.bv(512 * k, 512), f"xm{k}",
                        start=(k == 0), stop=(k == NT - 1), inc=(k == NT - 1))
            p.op("S", lambda e, bw=bw: e.copy(sb["lgt"][0:16, 0:512], bw[0:16, 0:512]),
                 reads=(kw,), writes=("lgt",))
            for tc in range(4):
                bt, kt = self.bank()
                self.trans(p, bt[:, 0:16], kt, sb["lgt"][0:16, 128 * tc:128 * (tc + 1)],
                           "lgt", f32=True)
                p.op("S", lambda e, bt=bt: e.copy(sb["prb"][:, 0:16], bt[:, 0:16]),
                     reads=(kt,), writes=("prb",))
                self._softmax_topk(p, 16, 4)
                p.op("V", lambda e: e.tensor_scalar(
                    sb["oh"][:, 0:16], sb["prb"][:, 0:16], sb["m8"][:, 3:4], None,
                    OP.is_ge), reads=("prb", "m8"), writes=("oh",))
                p.op("V", lambda e: e.tensor_tensor(
                    sb["pr2"][:, 0:16], sb["prb"][:, 0:16], sb["oh"][:, 0:16], OP.mult),
                    reads=("prb", "oh"), writes=("pr2",))
                bt2, kt2 = self.bank()
                self.trans(p, bt2[0:16, 0:128], kt2, sb["pr2"][:, 0:16], "pr2", f32=True)
                p.op("S", lambda e, bt2=bt2, tc=tc: e.copy(
                    sb["lgt2"][0:16, 128 * tc:128 * (tc + 1)], bt2[0:16, 0:128]),
                    reads=(kt2,), writes=("lgt2",))
            p.dma("Q", lambda e, t0=t0: e.dma_start(
                out=ap_["gted"][:, t0:t0 + 512], in_=sb["lgt2"][0:16, 0:512]),
                reads=("lgt2",), writes=("gted@d",))
        self.fence()
        for blk0 in range(0, 16, 2):
            for e2 in range(2):
                e = blk0 + e2
                self.wload(p, WR[:, 8192 * e2:8192 * e2 + 4096], "WR", ap_["w_rup"][e])
                self.wload(p, WR[:, 8192 * e2 + 4096:8192 * e2 + 8192], "WR", ap_["w_rdn"][e])
            for s in range(self.ns):
                t0 = TS * s
                pb_ = 4096 * (s % 2)
                pk_ = "xm" if s % 2 == 0 else "xssm"
                pa_ = FA_ACC if s % 2 == 0 else FA_OB + 0
                pak = "acc" if s % 2 == 0 else "ob"
                for j in range(NT):
                    p.dma("Q", lambda e, j=j, t0=t0, pb_=pb_: e.dma_start(
                        out=self.bv(pb_ + 512 * j, 512),
                        in_=ap_["xtd"][128 * j:128 * (j + 1), t0:t0 + 512]),
                        reads=("xtd@d",), writes=(f"{pk_}{j}",))
                    p.dma("Q", lambda e, j=j, t0=t0, pa_=pa_: e.dma_start(
                        out=self.fv(pa_ + 512 * j, 512),
                        in_=ap_["r2"][128 * j:128 * (j + 1), t0:t0 + 512]),
                        reads=("r2@d",), writes=(f"{pak}{j}",))
                p.dma("Q", lambda e, t0=t0: e.dma_start(
                    out=sb["lgt2"][0:16, 0:512], in_=ap_["gted"][:, t0:t0 + 512]),
                    reads=("gted@d",), writes=("lgt2",))
                for e2 in range(2):
                    e = blk0 + e2
                    p.dma("Q", lambda e_, e=e, t0=t0: e_.dma_start(
                        out=sb["grow"][0:1, 0:512], in_=ap_["gted"][e:e + 1, t0:t0 + 512]),
                        reads=("gted@d",), writes=("grow",))
                    bg, kg = self.bank()
                    self.mm(p, bg[:, 0:512], kg, sb["c_onesrowf"][0:1, 0:128],
                            "c_onesrowf", sb["grow"][0:1, 0:512], "grow",
                            start=True, stop=True)
                    p.op("S", lambda e_, bg=bg: e_.copy(self.scf(3), bg[:, 0:512]),
                         reads=(kg,), writes=("scF3",))
                    for m in range(4):
                        bw, kw = self.bank()
                        for k in range(NT):
                            self.mm(p, bw[:, 0:512], kw,
                                    WR[:, 8192 * e2 + 512 * k + 128 * m:8192 * e2 + 512 * k + 128 * m + 128],
                                    "WR", self.bv(pb_ + 512 * k, 512), f"{pk_}{k}",
                                    start=(k == 0), stop=(k == NT - 1), inc=(k == NT - 1))
                        p.op("S", lambda e_, bw=bw, m=m: e_.activation(
                            self.bsb(0), bw[:, 0:512], AF.Gelu),
                            reads=(kw,), writes=("sbf0",))
                        p.op("V", lambda e_, m=m: e_.tensor_tensor(
                            self.bv(8192 + 512 * m, 512), self.bsb(0), self.scf(3), OP.mult),
                            reads=("sbf0", "scF3"), writes=("hid",))
                    for mo in range(NT):
                        bw, kw = self.bank()
                        for ke in range(4):
                            self.mm(p, bw[:, 0:512], kw,
                                    WR[:, 8192 * e2 + 4096 + 1024 * ke + 128 * mo:8192 * e2 + 4096 + 1024 * ke + 128 * mo + 128],
                                    "WR", self.bv(8192 + 512 * ke, 512), "hid",
                                    start=(ke == 0), stop=(ke == 3), inc=(ke == 3))
                        p.op("V", lambda e_, mo=mo, bw=bw, pa_=pa_: e_.tensor_tensor(
                            self.fv(pa_ + 512 * mo, 512), self.fv(pa_ + 512 * mo, 512),
                            bw[:, 0:512], OP.add),
                            reads=(kw, f"{pak}{mo}"), writes=(f"{pak}{mo}",))
                for mo in range(NT):
                    p.dma("Q", lambda e_, mo=mo, t0=t0, pa_=pa_: e_.dma_start(
                        out=ap_["r2"][128 * mo:128 * (mo + 1), t0:t0 + 512],
                        in_=self.fv(pa_ + 512 * mo, 512)), reads=(f"{pak}{mo}",), writes=("r2@d",))
                    if blk0 == 14:
                        self.dbg1(p, "x2", self.fv(pa_ + 512 * mo, 512), f"{pak}{mo}", mo, t0)
            self.fence()

    def _ln_to(self, p, xk, xvs, outs, outks, W=512):
        sb = self.sb
        bA, kA = self.bank()
        bB, kB = self.bank()
        self.colsum(p, xvs, xk, W, bA, kA)
        self.colsum(p, xvs, xk, W, bB, kB, square=(self.fv(FA_SCF, W), "scF0"))
        self.rstd_from(p, bA, kA, bB, kB, W, EPS_LN,
                       sb["row0"], "row0", sb["row1"], "row1")
        bM, kM = self.bank()
        bR, kR = self.bank()
        self.repl(p, sb["row1"], "row1", W, bM, kM)
        self.repl(p, sb["row0"], "row0", W, bR, kR)
        p.op("S", lambda e, bM=bM: e.copy(self.scf(0, 0, W), bM[:, 0:W]),
             reads=(kM,), writes=("scF0",))
        p.op("S", lambda e, bR=bR: e.copy(self.scf(1, 0, W), bR[:, 0:W]),
             reads=(kR,), writes=("scF1",))
        for j in range(NT):
            p.op("V", lambda e, j=j: e.tensor_tensor(
                self.scf(2, 0, W), xvs[j], self.scf(0, 0, W), OP.subtract),
                reads=(xk[j], "scF0"), writes=("scF2",))
            p.op("V", lambda e, j=j: e.tensor_tensor(
                outs[j], self.scf(2, 0, W), self.scf(1, 0, W), OP.mult),
                reads=("scF2", "scF1"), writes=(outks[j],))

    def _softmax_topk(self, p, ne, kk):
        sb = self.sb
        p.op("V", lambda e: e.reduce_max(sb["ng"][:], sb["prb"][:, 0:ne], axis=AX.X),
             reads=("prb",), writes=("ng",))
        p.op("V", lambda e: e.tensor_scalar(sb["ng"][:], sb["ng"][:], -1.0, None, OP.mult),
             reads=("ng",), writes=("ng",))
        p.op("S", lambda e: e.activation(sb["pr2"][:, 0:ne], sb["prb"][:, 0:ne],
                                         AF.Exp, bias=sb["ng"][:], scale=1.0),
             reads=("prb", "ng"), writes=("pr2",))
        p.op("V", lambda e: e.tensor_reduce(sb["ng"][:], sb["pr2"][:, 0:ne], AX.X, OP.add),
             reads=("pr2",), writes=("ng",))
        p.op("V", lambda e: e.reciprocal(sb["ng"][:], sb["ng"][:]),
             reads=("ng",), writes=("ng",))
        p.op("V", lambda e: e.tensor_scalar(sb["prb"][:, 0:ne], sb["pr2"][:, 0:ne],
                                            sb["ng"][:], None, OP.mult),
             reads=("pr2", "ng"), writes=("prb",))
        p.op("V", lambda e: e.max(sb["m8"][:], sb["prb"][:, 0:ne]),
             reads=("prb",), writes=("m8",))

    def l1a(self, p):
        sb, ap_ = self.sb, self.ap
        WR = sb["WRES"]
        tl = self.tl
        p.op("S", lambda e: e.activation(sb["v_A"][:], sb["v_A"][:], AF.Exp),
             reads=("v_A",), writes=("v_A",))
        p.op("V", lambda e: e.tensor_scalar(sb["v_A"][:], sb["v_A"][:], -1.0, None, OP.mult),
             reads=("v_A",), writes=("v_A",))
        xk = [f"xsl{j}" for j in range(NT)]
        for j in range(NT):
            p.dma("Q", lambda e, j=j: e.dma_start(
                out=self.xslv(j, 0, 3), in_=ap_["r2"][128 * j:128 * (j + 1), tl - 3:tl],
                allow_slow_non_contiguous=True),
                reads=("r2@d",), writes=(xk[j],))
        self._ln_to(p, xk, [self.xslv(j, 0, 3) for j in range(NT)],
                    [self.bv(512 * j, 3) for j in range(NT)],
                    [f"xm{j}" for j in range(NT)], W=3)
        self.wload(p, WR[:, 0:16384], "WR",
                   ap_["w_Win"].rearrange("p (k m) -> p k m", m=4096)[:, :, 0:2048])
        for m in range(16):
            bw, kw = self.bank()
            for k in range(NT):
                self.mm(p, bw[:, 0:3], kw,
                        WR[:, 2048 * k + 128 * m:2048 * k + 128 * m + 128], "WR",
                        self.bv(512 * k, 3), f"xm{k}",
                        start=(k == 0), stop=(k == NT - 1), inc=(k == NT - 1))
            p.op("S", lambda e, bw=bw, m=m: e.copy(sb["xtl"][:, 3 * m:3 * m + 3],
                                                   bw[:, 0:3]),
                 reads=(kw,), writes=("xtl",))
        p.dma("Q", lambda e: e.dma_start(out=ap_["cc_t_in"], in_=sb["xtl"][:]),
              reads=("xtl",), writes=("cc_t_in@d",))
        p.cc(lambda e: e.collective_compute(
            "AllGather", OP.bypass, replica_groups=[[0, 1], [2, 3], [4, 5], [6, 7]],
            ins=[ap_["cc_t_in"].opt()], outs=[ap_["cc_t_out"].opt()]),
            reads=("cc_t_in@d",), writes=("cc_t_out@d",))
        p.dma("Q", lambda e: e.dma_start(out=sb["xtl"][:], in_=ap_["cc_t_out"][0:128, :]),
              reads=("cc_t_out@d",), writes=("xtl",))
        p.op("V", lambda e: e.tensor_scalar(
            sb["xtl"][:], sb["xtl"][:], sb["c_gate"][:, 0:1], None, OP.mult),
            reads=("xtl", "c_gate"), writes=("xtl",))
        p.op("G", lambda e: e.tensor_copy(sb["xpt"][:], sb["xtl"][:]),
             reads=("xtl",), writes=("xpt",))
        self.fence()
        for s in range(self.ns):
            t0 = TS * s
            for j in range(NT):
                p.dma("Q", lambda e, j=j, t0=t0: e.dma_start(
                    out=self.xslv(j, 0, 512),
                    in_=ap_["r2"][128 * j:128 * (j + 1), t0:t0 + 512]),
                    reads=("r2@d",), writes=(xk[j],))
            self._ln_to(p, xk, [self.xslv(j, 0, 512) for j in range(NT)],
                        [self.bv(512 * j, 512) for j in range(NT)],
                        [f"xm{j}" for j in range(NT)])
            for j in range(NT):
                p.dma("G", lambda e, j=j, t0=t0: e.dma_start(
                    out=ap_["xtd"][128 * j:128 * (j + 1), t0:t0 + 512],
                    in_=self.bv(512 * j, 512)),
                    reads=(f"xm{j}",), writes=("xtd@d",))
            for m in range(16):
                bw, kw = self.bank()
                for k in range(NT):
                    self.mm(p, bw[:, 0:512], kw,
                            WR[:, 2048 * k + 128 * m:2048 * k + 128 * m + 128], "WR",
                            self.bv(512 * k, 512), f"xm{k}",
                            start=(k == 0), stop=(k == NT - 1), inc=(k == NT - 1))
                p.op("S", lambda e, bw=bw, m=m: e.copy(
                    self.bv(8192 + 515 * m + 3, 512), bw[:, 0:512]),
                    reads=(kw,), writes=("res",))
                p.op("V", lambda e, m=m: e.tensor_copy(
                    self.bv(8192 + 515 * m, 3), sb["xpt"][:, 3 * m:3 * m + 3]),
                    reads=("xpt",), writes=("res",))
                p.op("V", lambda e, m=m: e.tensor_scalar(
                    self.scf(0), self.bv(8192 + 515 * m, 512),
                    sb["v_wconv"][:, 4 * m:4 * m + 1], None, OP.mult),
                    reads=("res", "v_wconv"), writes=("scF0",))
                for q in (1, 2, 3):
                    p.op("V", lambda e, m=m, q=q: e.scalar_tensor_tensor(
                        self.scf(0), self.bv(8192 + 515 * m + q, 512),
                        sb["v_wconv"][:, 4 * m + q:4 * m + q + 1],
                        self.scf(0), OP.mult, OP.add),
                        reads=("res", "v_wconv", "scF0"), writes=("scF0",))
                p.op("S", lambda e, m=m: e.activation(
                    self.bsb(0), self.scf(0), AF.Silu,
                    bias=sb["v_bconv"][:, m:m + 1], scale=1.0),
                    reads=("scF0", "v_bconv"), writes=("sbf0",))
                p.dma("G", lambda e, m=m, t0=t0: e.dma_start(
                    out=ap_["xsmd"][128 * m:128 * (m + 1), t0:t0 + 512],
                    in_=self.bsb(0)), reads=("sbf0",), writes=("xsmd@d",))
                p.op("V", lambda e, m=m: e.tensor_copy(
                    sb["xpt"][:, 3 * m:3 * m + 3], self.bv(8192 + 515 * m + 512, 3)),
                    reads=("res",), writes=("xpt",))
        self.fence()
        self.wload(p, WR[:, 0:16384], "WR",
                   ap_["w_Win"].rearrange("p (k m) -> p k m", m=4096)[:, :, 2048:4096])
        for s in range(self.ns):
            t0 = TS * s
            for j in range(NT):
                p.dma("Q", lambda e, j=j, t0=t0: e.dma_start(
                    out=self.bv(512 * j, 512),
                    in_=ap_["xtd"][128 * j:128 * (j + 1), t0:t0 + 512]),
                    reads=("xtd@d",), writes=(f"xm{j}",))
            for m in range(16):
                bw, kw = self.bank()
                for k in range(NT):
                    self.mm(p, bw[:, 0:512], kw,
                            WR[:, 2048 * k + 128 * m:2048 * k + 128 * m + 128], "WR",
                            self.bv(512 * k, 512), f"xm{k}",
                            start=(k == 0), stop=(k == NT - 1), inc=(k == NT - 1))
                p.op("S", lambda e, bw=bw: e.activation(self.bsb(0), bw[:, 0:512], AF.Silu),
                     reads=(kw,), writes=("sbf0",))
                p.dma("G", lambda e, m=m, t0=t0: e.dma_start(
                    out=ap_["resd"][128 * m:128 * (m + 1), t0:t0 + 512],
                    in_=self.bsb(0)), reads=("sbf0",), writes=("resd@d",))
        self.fence()
        self.wload(p, WR[:, 16384:16896], "WR", ap_["w_Wx"])
        for half in range(2):
            m0 = 8 * half
            p.dma("G", lambda e, half=half: e.dma_start(
                out=WR[:, 0:16384].rearrange("p (k m) -> p k m", m=1024),
                in_=ap_["w_Wdt"].rearrange("p (k m) -> p k m", m=2048)[:, :, 1024 * half:1024 * (half + 1)]),
                writes=("WR",))
            for s in range(self.ns):
                t0 = TS * s
                for k in range(16):
                    p.dma("Q", lambda e, k=k, t0=t0: e.dma_start(
                        out=self.bv(512 * k, 512),
                        in_=ap_["xsmd"][128 * k:128 * (k + 1), t0:t0 + 512]),
                        reads=("xsmd@d",), writes=(f"xssm{k}",))
                for m in range(m0, m0 + 8):
                    bw, kw = self.bank()
                    for k in range(16):
                        self.mm(p, bw[:, 0:512], kw,
                                WR[:, 1024 * k + 128 * (m - m0):1024 * k + 128 * (m - m0) + 128],
                                "WR", self.bv(512 * k, 512), f"xssm{k}",
                                start=(k == 0), stop=(k == 15), inc=(k == 15))
                    p.op("V", lambda e, bw=bw, m=m: e.tensor_scalar(
                        self.scf(0), bw[:, 0:512], sb["v_bdt"][:, m:m + 1], None, OP.add),
                        reads=(kw, "v_bdt"), writes=("scF0",))
                    p.op("S", lambda e: e.activation(self.scf(1), self.scf(0), AF.Abs),
                         reads=("scF0",), writes=("scF1",))
                    p.op("S", lambda e: e.activation(self.scf(1), self.scf(1), AF.Exp,
                                                     scale=-1.0),
                         reads=("scF1",), writes=("scF1",))
                    p.op("S", lambda e: e.activation(self.scf(1), self.scf(1), AF.Ln,
                                                     bias=1.0),
                         reads=("scF1",), writes=("scF1",))
                    p.op("V", lambda e: e.tensor_scalar(
                        self.scf(2), self.scf(0), 0.0, None, OP.max),
                        reads=("scF0",), writes=("scF2",))
                    p.op("V", lambda e, m=m: e.tensor_tensor(
                        self.bsb(0), self.scf(1), self.scf(2), OP.add),
                        reads=("scF1", "scF2"), writes=("sbf0",))
                    p.dma("G", lambda e, m=m, t0=t0: e.dma_start(
                        out=ap_["dtd"][128 * m:128 * (m + 1), t0:t0 + 512],
                        in_=self.bsb(0)), reads=("sbf0",), writes=("dtd@d",))
                if half == 0:
                    bw, kw = self.bank()
                    for k in range(16):
                        self.mm(p, bw[0:32, 0:512], kw,
                                WR[:, 16384 + 32 * k:16384 + 32 * (k + 1)], "WR",
                                self.bv(512 * k, 512), f"xssm{k}",
                                start=(k == 0), stop=(k == 15), inc=(k == 15))
                    p.op("S", lambda e, bw=bw: e.copy(sb["bcs"][:, 0:512], bw[0:32, 0:512]),
                         reads=(kw,), writes=("bcs",))
                    p.dma("Q", lambda e, t0=t0: e.dma_start(
                        out=ap_["bcd"][:, t0:t0 + 512], in_=sb["bcs"][0:32, 0:512]),
                        reads=("bcs",), writes=("bcd@d",))
        self.fence()
        p.op("V", lambda e: e.memset(sb["hst"][:], 0.0), writes=("hst",))
        for s in range(self.ns):
            t0 = TS * s
            for k in range(16):
                p.dma("Q", lambda e, k=k, t0=t0: e.dma_start(
                    out=self.bv(512 * k, 512),
                    in_=ap_["xsmd"][128 * k:128 * (k + 1), t0:t0 + 512]),
                    reads=("xsmd@d",), writes=(f"xssm{k}",))
                p.dma("Q", lambda e, k=k, t0=t0: e.dma_start(
                    out=self.bv(8192 + 512 * k, 512),
                    in_=ap_["dtd"][128 * k:128 * (k + 1), t0:t0 + 512]),
                    reads=("dtd@d",), writes=(f"dt{k}",))
            for n in range(16):
                p.dma("Q", lambda e, n=n, t0=t0: e.dma_start(
                    out=sb["grow"][0:1, 0:512],
                    in_=ap_["bcd"][16 + n:17 + n, t0:t0 + 512]),
                    reads=("bcd@d",), writes=("grow",))
                bg, kg = self.bank()
                self.mm(p, bg[:, 0:512], kg, sb["c_onesrowf"][0:1, 0:128],
                        "c_onesrowf", sb["grow"][0:1, 0:512], "grow",
                        start=True, stop=True)
                p.op("S", lambda e, bg=bg, n=n: e.copy(
                    self.bv(16384 + 512 * n, 512), bg[:, 0:512]),
                    reads=(kg,), writes=("beta",))
            for j in range(16):
                p.op("V", lambda e, j=j: e.tensor_tensor(
                    sb["tmaj"][:, 0:512], self.bv(8192 + 512 * j, 512),
                    self.bv(512 * j, 512),
                    OP.mult), reads=(f"dt{j}", f"xssm{j}"), writes=("tmaj",))
                yfirst = True
                for n in range(16):
                    q = n % 4
                    q2 = n % 4
                    p.op("S", lambda e, j=j, n=n, q=q: e.activation(
                        self.fv(FA_SCF + 512 * q, 512),
                        self.bv(8192 + 512 * j, 512), AF.Exp,
                        scale=sb["v_A"][:, 16 * j + n:16 * j + n + 1]),
                        reads=(f"dt{j}", "v_A"), writes=(f"scF{q}",))
                    p.op("G", lambda e, n=n, q2=q2: e.tensor_tensor(
                        self.fv(FA_XSL + 512 * q2, 512), sb["tmaj"][:, 0:512],
                        self.bv(16384 + 512 * n, 512), OP.mult),
                        reads=("tmaj", "beta"), writes=(f"xsl{q2}",))
                    p.op("V", lambda e, j=j, n=n, q=q, q2=q2: e.tensor_tensor_scan(
                        self.fv(FA_OB + 512 * q2, 512),
                        self.fv(FA_SCF + 512 * q, 512),
                        self.fv(FA_XSL + 512 * q2, 512),
                        sb["hst"][:, 16 * j + n:16 * j + n + 1],
                        OP.mult, OP.add),
                        reads=(f"scF{q}", f"xsl{q2}", "hst"), writes=(f"ob{q2}",))
                    p.op("V", lambda e, j=j, n=n, q2=q2: e.tensor_copy(
                        sb["hst"][:, 16 * j + n:16 * j + n + 1],
                        self.fv(FA_OB + 512 * q2 + 511, 1)),
                        reads=(f"ob{q2}",), writes=("hst",))
                    if yfirst:
                        p.op("V", lambda e, q2=q2: e.tensor_copy(
                            self.fv(FA_YAC, 512), self.fv(FA_OB + 512 * q2, 512)),
                            reads=(f"ob{q2}",), writes=("yac",))
                        yfirst = False
                    else:
                        p.op("V", lambda e, q2=q2: e.tensor_tensor(
                            self.fv(FA_YAC, 512), self.fv(FA_YAC, 512),
                            self.fv(FA_OB + 512 * q2, 512), OP.add),
                            reads=(f"ob{q2}", "yac"), writes=("yac",))
                p.op("V", lambda e, j=j: e.scalar_tensor_tensor(
                    sb["tmaj"][:, 512:1024], self.bv(512 * j, 512), sb["v_D"][:, j:j + 1],
                    self.fv(FA_YAC, 512), OP.mult, OP.add),
                    reads=(f"xssm{j}", "v_D", "yac"), writes=("tmaj",))
                p.dma("G", lambda e, j=j, t0=t0: e.dma_start(
                    out=ap_["yd"][128 * j:128 * (j + 1), t0:t0 + 512],
                    in_=sb["tmaj"][:, 512:1024]), reads=("tmaj",), writes=("yd@d",))
        p.dma("Q", lambda e: e.dma_start(out=ap_["cc_h_in"], in_=sb["hst"][:]),
              reads=("hst",), writes=("cc_h_in@d",))
        p.cc(lambda e: e.collective_compute(
            "AllGather", OP.bypass, replica_groups=[[0, 1], [2, 3], [4, 5], [6, 7]],
            ins=[ap_["cc_h_in"].opt()], outs=[ap_["cc_h_out"].opt()]),
            reads=("cc_h_in@d",), writes=("cc_h_out@d",))
        p.dma("Q", lambda e: e.dma_start(out=sb["hcar"][:], in_=ap_["cc_h_out"][0:128, :]),
              reads=("cc_h_out@d",), writes=("hcar",))
        p.op("V", lambda e: e.tensor_scalar(
            sb["hcar"][:], sb["hcar"][:], sb["c_gate"][:, 0:1], None, OP.mult),
            reads=("hcar", "c_gate"), writes=("hcar",))
        self.fence()
        p.op("V", lambda e: e.memset(sb["hst"][:, 0:16], 0.0), writes=("hst",))
        for s in range(self.ns):
            t0 = TS * s
            for k in range(16):
                p.dma("Q", lambda e, k=k, t0=t0: e.dma_start(
                    out=self.bv(512 * k, 512),
                    in_=ap_["yd"][128 * k:128 * (k + 1), t0:t0 + 512]),
                    reads=("yd@d",), writes=(f"ysl{k}",))
                p.dma("Q", lambda e, k=k, t0=t0: e.dma_start(
                    out=self.bv(8192 + 512 * k, 512),
                    in_=ap_["dtd"][128 * k:128 * (k + 1), t0:t0 + 512]),
                    reads=("dtd@d",), writes=(f"dt{k}",))
                p.dma("Q", lambda e, k=k, t0=t0: e.dma_start(
                    out=self.bv(16384 + 512 * k, 512),
                    in_=ap_["resd"][128 * k:128 * (k + 1), t0:t0 + 512]),
                    reads=("resd@d",), writes=(f"res{k}",))
            for j in range(16):
                p.op("V", lambda e, j=j: e.tensor_tensor_scan(
                    self.scf(0), sb["onesF"][:], self.bv(8192 + 512 * j, 512),
                    sb["hst"][:, j:j + 1], OP.mult, OP.add),
                    reads=("onesF", f"dt{j}", "hst"), writes=("scF0",))
                p.op("V", lambda e, j=j: e.tensor_copy(
                    sb["hst"][:, j:j + 1], self.scf(0)[:, 511:512]),
                    reads=("scF0",), writes=("hst",))
                p.op("V", lambda e: e.memset(self.fv(FA_YAC, 512), 0.0),
                     writes=("yac",))
                for n in range(16):
                    p.op("S", lambda e, j=j, n=n: e.activation(
                        self.scf(1), self.scf(0), AF.Exp,
                        scale=sb["v_A"][:, 16 * j + n:16 * j + n + 1]),
                        reads=("scF0", "v_A"), writes=("scF1",))
                    p.op("V", lambda e, j=j, n=n: e.scalar_tensor_tensor(
                        self.fv(FA_YAC, 512), self.scf(1),
                        sb["hcar"][:, 16 * j + n:16 * j + n + 1],
                        self.fv(FA_YAC, 512), OP.mult, OP.add),
                        reads=("scF1", "hcar", "yac"), writes=("yac",))
                p.op("V", lambda e, j=j: e.tensor_tensor(
                    self.scf(2), self.bv(512 * j, 512), self.fv(FA_YAC, 512), OP.add),
                    reads=(f"ysl{j}", "yac"), writes=("scF2",))
                p.op("V", lambda e, j=j: e.tensor_tensor(
                    self.scf(3), self.scf(2), self.bv(16384 + 512 * j, 512), OP.mult),
                    reads=("scF2", f"res{j}"), writes=("scF3",))
                p.dma("G", lambda e, j=j, t0=t0: e.dma_start(
                    out=ap_["yfd"][128 * j:128 * (j + 1), t0:t0 + 512],
                    in_=self.scf(3)), reads=("scF3",), writes=("yfd@d",))
        self.fence()
        self.wload(p, WR[:, 0:16384], "WR", ap_["w_Wout"])
        for s in range(self.ns):
            t0 = TS * s
            pb_ = 8192 * (s % 2)
            pk_ = "ysl" if s % 2 == 0 else "dt"
            for k in range(16):
                p.dma("Q", lambda e, k=k, t0=t0, pb_=pb_: e.dma_start(
                    out=self.bv(pb_ + 512 * k, 512),
                    in_=ap_["yfd"][128 * k:128 * (k + 1), t0:t0 + 512]),
                    reads=("yfd@d",), writes=(f"{pk_}{k}",))
            for j in range(NT):
                p.dma("Q", lambda e, j=j, t0=t0: e.dma_start(
                    out=self.xslv(j, 0, 512),
                    in_=ap_["r2"][128 * j:128 * (j + 1), t0:t0 + 512]),
                    reads=("r2@d",), writes=(xk[j],))
            for mo in range(NT):
                bw, kw = self.bank()
                for k in range(16):
                    self.mm(p, bw[:, 0:512], kw,
                            WR[:, 1024 * k + 128 * mo:1024 * k + 128 * mo + 128], "WR",
                            self.bv(pb_ + 512 * k, 512), f"{pk_}{k}",
                            start=(k == 0), stop=(k == 15), inc=(k == 15))
                p.op("V", lambda e, mo=mo, bw=bw: e.tensor_tensor(
                    self.scf(2), self.xslv(mo, 0, 512), bw[:, 0:512], OP.add),
                    reads=(kw, xk[mo]), writes=("scF2",))
                p.dma("Q", lambda e, mo=mo, t0=t0: e.dma_start(
                    out=ap_["r3"][128 * mo:128 * (mo + 1), t0:t0 + 512],
                    in_=self.scf(2)), reads=("scF2",), writes=("r3@d",))
                self.dbg1(p, "x3", self.scf(2), "scF2", mo, t0)

    def l1f(self, p):
        sb, ap_ = self.sb, self.ap
        WR = sb["WRES"]
        tl = self.tl
        ntt = tl // 128
        xk = [f"xsl{j}" for j in range(NT)]
        for s in range(self.ns):
            t0 = TS * s
            for j in range(NT):
                p.dma("Q", lambda e, j=j, t0=t0: e.dma_start(
                    out=self.xslv(j, 0, 512),
                    in_=ap_["r3"][128 * j:128 * (j + 1), t0:t0 + 512]),
                    reads=("r3@d",), writes=(xk[j],))
            self._ln_to(p, xk, [self.xslv(j, 0, 512) for j in range(NT)],
                        [self.bv(512 * j, 512) for j in range(NT)],
                        [f"xm{j}" for j in range(NT)])
            for j in range(NT):
                p.dma("G", lambda e, j=j, t0=t0: e.dma_start(
                    out=ap_["ln4d"][128 * j:128 * (j + 1), t0:t0 + 512],
                    in_=self.bv(512 * j, 512)),
                    reads=(f"xm{j}",), writes=("ln4d@d",))
        self.fence()
        self.wload(p, WR[:, 0:256], "WR", ap_["w_rt0"])
        self.wload(p, WR[:, 256:512], "WR", ap_["w_rt1"])
        p.op("V", lambda e: e.memset(sb["crow"][:], 0.0), writes=("crow",))
        p.op("V", lambda e: e.memset(sb["crow2"][:], 0.0), writes=("crow2",))
        p.op("V", lambda e: e.memset(sb["zz"][:, 0:1], 0.0), writes=("zz",))
        p.op("V", lambda e: e.memset(sb["wgt"][:], 0.0), writes=("wgt",))
        for s in range(self.ns):
            t0 = TS * s
            for j in range(NT):
                p.dma("Q", lambda e, j=j, t0=t0: e.dma_start(
                    out=self.bv(512 * j, 512),
                    in_=ap_["ln4d"][128 * j:128 * (j + 1), t0:t0 + 512]),
                    reads=("ln4d@d",), writes=(f"xm{j}",))
            for g in range(2):
                crk = "crow" if g == 0 else "crow2"
                crb = sb["crow"] if g == 0 else sb["crow2"]
                bw, kw = self.bank()
                for k in range(NT):
                    self.mm(p, bw[0:32, 0:512], kw,
                            WR[:, 256 * g + 32 * k:256 * g + 32 * (k + 1)], "WR",
                            self.bv(512 * k, 512), f"xm{k}",
                            start=(k == 0), stop=(k == NT - 1), inc=(k == NT - 1))
                p.op("S", lambda e, bw=bw: e.copy(sb["lgt"][:], bw[0:32, 0:512]),
                     reads=(kw,), writes=("lgt",))
                for tc in range(4):
                    tt = 4 * s + tc
                    bt, kt = self.bank()
                    self.trans(p, bt[:, 0:32], kt, sb["lgt"][0:32, 128 * tc:128 * (tc + 1)],
                               "lgt", f32=True)
                    p.op("S", lambda e, bt=bt: e.copy(sb["prb"][:, 0:32], bt[:, 0:32]),
                         reads=(kt,), writes=("prb",))
                    self._softmax_topk(p, 32, 2)
                    for jj in range(2):
                        wv = sb["m8"][:, jj:jj + 1]
                        p.op("V", lambda e, wv=wv: e.tensor_scalar(
                            sb["ohb"][:, 0:32], sb["prb"][:, 0:32], wv, None, OP.is_equal),
                            reads=("prb", "m8"), writes=("ohb",))
                        p.op("V", lambda e: e.tensor_copy(sb["oh"][:, 0:32], sb["ohb"][:, 0:32]),
                             reads=("ohb",), writes=("oh",))
                        p.op("V", lambda e: e.tensor_tensor(
                            sb["pr2"][:, 0:32], sb["oh"][:, 0:32], sb["c_erow32"][:, 0:32],
                            OP.mult), reads=("oh", "c_erow32"), writes=("pr2",))
                        p.op("V", lambda e: e.tensor_reduce(
                            sb["idxc"][:, 0:1], sb["pr2"][:, 0:32], AX.X, OP.add),
                            reads=("pr2",), writes=("idxc",))
                        bp, kp = self.bank()
                        self.mm(p, bp[:, 0:32], kp, sb["c_tril"][:], "c_tril",
                                sb["ohb"][:, 0:32], "ohb", start=True, stop=True)
                        p.op("S", lambda e, bp=bp: e.copy(sb["pr2"][:, 0:32], bp[:, 0:32]),
                             reads=(kp,), writes=("pr2",))
                        p.op("V", lambda e: e.tensor_tensor(
                            sb["pr2"][:, 0:32], sb["pr2"][:, 0:32], sb["oh"][:, 0:32],
                            OP.mult), reads=("pr2", "oh"), writes=("pr2",))
                        p.op("V", lambda e: e.tensor_reduce(
                            sb["rkc"][:, 0:1], sb["pr2"][:, 0:32], AX.X, OP.add),
                            reads=("pr2",), writes=("rkc",))
                        bc2, kc2 = self.bank()
                        self.mm(p, bc2[:, 0:32], kc2, sb["c_onesrowf"][0:1, 0:128],
                                "c_onesrowf", crb[0:1, 0:32], crk, start=True, stop=True)
                        p.op("S", lambda e, bc2=bc2: e.copy(sb["pr2"][:, 0:32], bc2[:, 0:32]),
                             reads=(kc2,), writes=("pr2",))
                        p.op("V", lambda e: e.tensor_tensor(
                            sb["pr2"][:, 0:32], sb["pr2"][:, 0:32], sb["oh"][:, 0:32],
                            OP.mult), reads=("pr2", "oh"), writes=("pr2",))
                        p.op("V", lambda e: e.tensor_reduce(
                            sb["bsc"][:, 0:1], sb["pr2"][:, 0:32], AX.X, OP.add),
                            reads=("pr2",), writes=("bsc",))
                        p.op("V", lambda e: e.tensor_tensor(
                            sb["rkc"][:, 0:1], sb["rkc"][:, 0:1], sb["bsc"][:, 0:1],
                            OP.add), reads=("rkc", "bsc"), writes=("rkc",))
                        bc3, kc3 = self.bank()
                        self.mm(p, bc3[0:1, 0:32], kc3, sb["c_onescolb"][:, 0:1],
                                "c_onescolb", sb["ohb"][:, 0:32], "ohb",
                                start=True, stop=True)
                        p.op("V", lambda e, crb=crb, bc3=bc3: e.tensor_tensor(
                            crb[0:1, 0:32], crb[0:1, 0:32], bc3[0:1, 0:32], OP.add),
                            reads=(crk, kc3), writes=(crk,))
                        col = 128 * g + 64 * jj + tt
                        p.op("V", lambda e: e.tensor_scalar(
                            sb["slotF"][:, 0:1], sb["idxc"][:, 0:1], float(CAP),
                            sb["rkc"][:, 0:1], OP.mult, OP.add),
                            reads=("idxc", "rkc"), writes=("slotF",))
                        p.op("V", lambda e: e.tensor_scalar(
                            sb["ovmu"][:, 0:1], sb["rkc"][:, 0:1], float(CAP), None,
                            OP.is_ge), reads=("rkc",), writes=("ovm",))
                        p.op("V", lambda e: e.tensor_scalar(
                            sb["slotF"][:, 1:2], sb["rkc"][:, 0:1], float(SLOTS), None,
                            OP.add), reads=("rkc",), writes=("slotF",))
                        p.op("V", lambda e: e.select(
                            sb["slotF"][:, 2:3], sb["ovmu"][:, 0:1],
                            sb["slotF"][:, 1:2], sb["slotF"][:, 0:1], add_drain=True),
                            reads=("ovm", "slotF"), writes=("slotF",))
                        p.op("V", lambda e: e.select(
                            sb["slotF"][:, 3:4], sb["ovmu"][:, 0:1], sb["zz"][:, 0:1],
                            sb["slotF"][:, 0:1], add_drain=True),
                            reads=("ovm", "slotF", "zz"), writes=("slotF",))
                        p.op("V", lambda e, col=col: e.tensor_copy(
                            sb["slot"][:, col:col + 1], sb["slotF"][:, 2:3]),
                            reads=("slotF",), writes=("slot",))
                        p.op("V", lambda e, col=col: e.tensor_copy(
                            sb["slotc"][:, col:col + 1], sb["slotF"][:, 3:4]),
                            reads=("slotF",), writes=("slotc",))
                        p.op("V", lambda e, col=col, wv=wv: e.select(
                            sb["wgt"][:, col:col + 1], sb["ovmu"][:, 0:1], sb["zz"][:, 0:1],
                            wv, add_drain=True), reads=("ovm", "m8", "zz"), writes=("wgt",))
        p.op("V", lambda e: e.tensor_scalar(
            sb["wgt"][:], sb["wgt"][:], 0.5, None, OP.mult),
            reads=("wgt",), writes=("wgt",))
        self.fence()
        p.op("V", lambda e: e.memset(sb["gath"][:], 0.0), writes=("gath",))
        for g in range(2):
            for ch in range(SLOTS // 128):
                p.dma("Q", lambda e, g=g, ch=ch: e.dma_start(
                    out=ap_[f"xs{g}"][128 * ch:128 * (ch + 1), :], in_=sb["gath"][:]),
                    reads=("gath",), writes=(f"xs{g}@d",))
        for tt in range(ntt):
            bt, ktb = self.bank()
            ptb = bt.bitcast(BF16)
            for j in range(NT):
                p.dma("Q", lambda e, j=j, tt=tt: e.dma_start(
                    out=self.bv(512 * j, 128),
                    in_=ap_["ln4d"][128 * j:128 * (j + 1), 128 * tt:128 * (tt + 1)]),
                    reads=("ln4d@d",), writes=(f"xm{j}",))
                self.trans(p, ptb[0:128, 128 * j:128 * (j + 1)], ktb,
                           self.bv(512 * j, 128), f"xm{j}")
            p.op("S", lambda e, ptb=ptb: e.copy(sb["tmaj"][:], ptb[0:128, 0:1024]),
                 reads=(ktb,), writes=("tmaj",))
            for g in range(2):
                for jj in range(2):
                    col = 128 * g + 64 * jj + tt
                    p.dma("G", lambda e, g=g, col=col: e.indirect_dma_start(
                        out=ap_[f"xs{g}"],
                        out_offset=IndirectOffsetOnAxis(ap=sb["slot"][:, col:col + 1], axis=0),
                        in_=sb["tmaj"][:], in_offset=None),
                        reads=("tmaj", "slot"), writes=(f"xs{g}@d",))
        self.fence()
        for g in range(2):
            for e in range(NE1):
                wb = 12288 * (e % 2)
                wkey = "WR" if e % 2 == 0 else "WR2"
                self.wload(p, WR[:, wb:wb + 4096], wkey, ap_[f"w_wg{g}"][e])
                self.wload(p, WR[:, wb + 4096:wb + 8192], wkey, ap_[f"w_wu{g}"][e])
                self.wload(p, WR[:, wb + 8192:wb + 12288], wkey, ap_[f"w_wd{g}"][e])
                for dc in range(NT):
                    p.dma("Q", lambda e_, g=g, e=e, dc=dc: e_.dma_start(
                        out=self.bv(640 * dc, 640),
                        in_=ap_[f"xs{g}"][CAP * e:CAP * (e + 1), 128 * dc:128 * (dc + 1)],
                        transpose=True),
                        reads=(f"xs{g}@d",), writes=(f"xm{dc}",))
                for m in range(4):
                    for fo, fn in ((0, 512), (512, 128)):
                        bg_, kg_ = self.bank()
                        bu_, ku_ = self.bank()
                        for k in range(NT):
                            self.mm(p, bg_[:, 0:fn], kg_,
                                    WR[:, wb + 512 * k + 128 * m:wb + 512 * k + 128 * m + 128],
                                    wkey, self.bv(640 * k + fo, fn), f"xm{k}",
                                    start=(k == 0), stop=(k == NT - 1), inc=(k == NT - 1))
                        for k in range(NT):
                            self.mm(p, bu_[:, 0:fn], ku_,
                                    WR[:, wb + 4096 + 512 * k + 128 * m:wb + 4096 + 512 * k + 128 * m + 128],
                                    wkey, self.bv(640 * k + fo, fn), f"xm{k}",
                                    start=(k == 0), stop=(k == NT - 1), inc=(k == NT - 1))
                        p.op("S", lambda e_, bg_=bg_, fn=fn: e_.activation(
                            self.bsb(0, fn), bg_[:, 0:fn], AF.Silu),
                            reads=(kg_,), writes=("sbf0",))
                        p.op("V", lambda e_, m=m, fo=fo, fn=fn, bu_=bu_: e_.tensor_tensor(
                            self.bv(8192 + 640 * m + fo, fn), self.bsb(0, fn),
                            bu_[:, 0:fn], OP.mult),
                            reads=("sbf0", ku_), writes=("hid",))
                for sc in range(5):
                    for dc2 in range(2):
                        bw, kw = self.bank()
                        for ke in range(4):
                            self.mm(p, bw[:, 0:512], kw,
                                    self.bv(8192 + 640 * ke + 128 * sc, 128), "hid",
                                    WR[:, wb + 8192 + 1024 * ke + 512 * dc2:wb + 8192 + 1024 * ke + 512 * (dc2 + 1)],
                                    wkey, start=(ke == 0), stop=(ke == 3), inc=(ke == 3))
                        p.op("S", lambda e_, bw=bw, dc2=dc2: e_.copy(
                            sb["tmaj"][:, 512 * dc2:512 * (dc2 + 1)], bw[:, 0:512]),
                            reads=(kw,), writes=("tmaj",))
                    p.dma("Q", lambda e_, g=g, e=e, sc=sc: e_.dma_start(
                        out=ap_[f"ys{g}"][CAP * e + 128 * sc:CAP * e + 128 * (sc + 1), :],
                        in_=sb["tmaj"][:]),
                        reads=("tmaj",), writes=(f"ys{g}@d",))
        self.fence()
        for tt in range(ntt):
            bt1, kt1 = self.bank()
            bt2, kt2 = self.bank()
            for j in range(NT):
                p.dma("Q", lambda e_, j=j, tt=tt: e_.dma_start(
                    out=self.xslv(j, 0, 128),
                    in_=ap_["r3"][128 * j:128 * (j + 1), 128 * tt:128 * (tt + 1)]),
                    reads=("r3@d",), writes=(xk[j],))
                bsel, ksel = (bt1, kt1) if j < 4 else (bt2, kt2)
                self.trans(p, bsel[:, 128 * (j % 4):128 * (j % 4) + 128], ksel,
                           self.xslv(j, 0, 128), xk[j], f32=True)
            p.op("S", lambda e_, bt1=bt1: e_.copy(sb["ytok"][:, 0:512], bt1[:, 0:512]),
                 reads=(kt1,), writes=("ytok",))
            p.op("S", lambda e_, bt2=bt2: e_.copy(sb["ytok"][:, 512:1024], bt2[:, 0:512]),
                 reads=(kt2,), writes=("ytok",))
            for g in range(2):
                for jj in range(2):
                    col = 128 * g + 64 * jj + tt
                    p.dma("G", lambda e_, g=g, col=col: e_.indirect_dma_start(
                        out=sb["gath"][:], out_offset=None,
                        in_=ap_[f"ys{g}"],
                        in_offset=IndirectOffsetOnAxis(ap=sb["slotc"][:, col:col + 1], axis=0)),
                        reads=(f"ys{g}@d", "slotc"), writes=("gath",))
                    p.op("V", lambda e_, col=col: e_.scalar_tensor_tensor(
                        sb["ytok"][:], sb["gath"][:], sb["wgt"][:, col:col + 1],
                        sb["ytok"][:], OP.mult, OP.add),
                        reads=("gath", "wgt", "ytok"), writes=("ytok",))
            p.dma("Q", lambda e_, tt=tt: e_.dma_start(
                out=ap_["out"][128 * tt:128 * (tt + 1), :], in_=sb["ytok"][:]),
                reads=("ytok",), writes=("out@d",))

    def build(self):
        self.declare()
        p = Prog(self.nc)
        self.p = p
        with ExitStack() as ctx:
            self.alloc(ctx)
            self.ldvec(p)
            pl = getattr(self, "passes", None)
            if "l0a" in self.stages:
                for nm, fn in [("A1", self.l0a_A1), ("A2", self.l0a_A2),
                               ("A3", self.l0a_A3), ("AGS", self.l0a_AGS),
                               ("A4", self.l0a_A4), ("A5", self.l0a_A5),
                               ("A6", self.l0a_A6), ("A7", self.l0a_A7)]:
                    if pl is None or nm in pl:
                        fn(p)
                        if nm != "AGS":
                            self.fence()
            if "l0f" in self.stages:
                self.l0f(p)
                self.fence()
            if "l1a" in self.stages:
                self.l1a(p)
                self.fence()
            if "l1f" in self.stages:
                self.l1f(p)
            p.emit()
        return self.nc


def _kt(w):
    din, dout = w.shape
    nk = din // 128
    return np.ascontiguousarray(
        w.reshape(nk, 128, dout).transpose(1, 0, 2).reshape(128, nk * dout))


def _vec(v, nk=8):
    return np.ascontiguousarray(np.asarray(v).reshape(nk, 128).T)


def _consts(core):
    import ml_dtypes
    BFD = ml_dtypes.bfloat16
    ident = np.eye(128)
    headrep = (np.arange(128)[:, None] // 64 == np.arange(128)[None] // 64)
    tril = (np.arange(128)[:, None] < np.arange(128)[None]).astype(np.float32)
    import ml_dtypes
    BF = ml_dtypes.bfloat16
    return {
        "c_ident": ident.astype(BF),
        "c_identf": ident.astype(np.float32),
        "c_headrep": headrep.astype(BF),
        "c_gnrep": (headrep / 64.0).astype(BF),
        "c_tril": tril.T.astype(BF),
        "c_onesrowf": np.ones((1, 128), np.float32),
        "c_onescolf": np.ones((128, 1), np.float32),
        "c_erow32": np.tile(np.arange(32).astype(np.float32)[None], (128, 1)),
        "c_gate": np.full((128, 1), 1.0 if core % 2 == 1 else 0.0, np.float32),
        "c_onescolb": np.ones((128, 1), BFD),
    }


def stage_inputs(inputs, tl=4096):
    x = np.asarray(inputs["L0_attn_x"], np.float32)
    B = x.shape[0]
    maps = []
    wcommon = {
        "w_Wr": _kt(inputs["L0_attn_Wr"]), "w_Wk": _kt(inputs["L0_attn_Wk"]),
        "w_Wv": _kt(inputs["L0_attn_Wv"]), "w_Wg": _kt(inputs["L0_attn_Wg"]),
        "w_Wo": _kt(inputs["L0_attn_Wo"]), "w_wA": _kt(inputs["L0_attn_wA"]),
        "w_wB": np.ascontiguousarray(inputs["L0_attn_wB"]),
        "w_f1": _kt(inputs["L0_attn_ffn_w1"]), "w_f2": _kt(inputs["L0_attn_ffn_w2"]),
        "w_sup": np.concatenate([_kt(inputs["L0_ffn_shared_up"][s].T) for s in range(2)], 1),
        "w_sdn": np.concatenate([_kt(inputs["L0_ffn_shared_down"][s].T) for s in range(2)], 1),
        "w_rup": np.stack([_kt(inputs["L0_ffn_routed_up"][r].T) for r in range(16)]),
        "w_rdn": np.stack([_kt(inputs["L0_ffn_routed_down"][r].T) for r in range(16)]),
        "w_rtr": _kt(np.ascontiguousarray(inputs["L0_ffn_router_w"].T)),
        "w_Win": _kt(np.ascontiguousarray(inputs["L1_attn_W_in"].T)),
        "c_wconv": _wconv(inputs["L1_attn_W_conv"]),
        "c_bconv": _vec(inputs["L1_attn_b_conv"], 16),
        "w_Wx": _kt(np.ascontiguousarray(inputs["L1_attn_W_x"].T)),
        "w_Wdt": _kt(np.ascontiguousarray(inputs["L1_attn_W_dt"].T)),
        "c_bdt": _vec(inputs["L1_attn_b_dt"], 16),
        "w_Wout": _kt(np.ascontiguousarray(inputs["L1_attn_W_out"].T)),
        "c_A": _amat(inputs["L1_attn_A_log"]),
        "c_D": _vec(inputs["L1_attn_D"], 16),
        "w_rt0": _kt(inputs["L1_ffn_router_w0"]),
        "w_rt1": _kt(inputs["L1_ffn_router_w1"]),
        "c_ln1w": _vec(inputs["L0_attn_ln1_w"]),
        "c_mu_r": _vec(inputs["L0_attn_mu_r"]), "c_mu_k": _vec(inputs["L0_attn_mu_k"]),
        "c_mu_v": _vec(inputs["L0_attn_mu_v"]), "c_mu_w": _vec(inputs["L0_attn_mu_w"]),
        "c_mu_g": _vec(inputs["L0_attn_mu_g"]), "c_w0": _vec(inputs["L0_attn_w0"]),
        "c_u": _vec(inputs["L0_attn_u"].reshape(-1)),
        "c_gnw": _vec(inputs["L0_attn_gn_w"].reshape(-1)),
        "c_gnb": _vec(inputs["L0_attn_gn_b"].reshape(-1)),
        "c_ln2w": _vec(inputs["L0_attn_ln2_w"]),
    }
    for g in (0, 1):
        wcommon[f"w_wg{g}"] = np.stack([_kt(inputs[f"L1_ffn_wg{g}"][e]) for e in range(NE1)])
        wcommon[f"w_wu{g}"] = np.stack([_kt(inputs[f"L1_ffn_wu{g}"][e]) for e in range(NE1)])
        wcommon[f"w_wd{g}"] = np.stack([_kt(inputs[f"L1_ffn_wd{g}"][e]) for e in range(NE1)])
    for c in range(8):
        b, h = c // 2, c % 2
        s0 = h * tl
        xc = np.zeros((D, tl + 1), np.float32)
        xc[:, 1:] = x[b, s0:s0 + tl].T
        if h == 1:
            xc[:, 0] = x[b, s0 - 1]
        m = {"xdm": np.ascontiguousarray(xc)}
        m.update(wcommon)
        m.update(_consts(c))
        maps.append(m)
    return maps


def _wconv(wc):
    w = np.asarray(wc)[:, 0, :]
    return np.ascontiguousarray(
        w.reshape(16, 128, 4).transpose(1, 0, 2).reshape(128, 64))


def _amat(alog):
    a = np.asarray(alog, np.float32)
    return np.ascontiguousarray(
        a.reshape(16, 128, 16).transpose(1, 0, 2).reshape(128, 256))


_BUILT = {}


def kernel(**inputs):
    key = ("full", 4096)
    if key not in _BUILT:
        kk = K(tl=4096, stages=STAGES)
        _BUILT[key] = kk.build()
    nc = _BUILT[key]
    maps = stage_inputs(inputs, 4096)
    res = run_bass_kernel_spmd(nc, maps, core_ids=list(range(8)))
    out = np.zeros((4, 8192, 1024), np.float32)
    for c in range(8):
        b, h = c // 2, c % 2
        out[b, h * 4096:(h + 1) * 4096] = res.results[c]["out"]
    return out
